# version 01
# baseline (speedup 1.0000x reference)
import contextlib
import numpy as np
import ml_dtypes
import concourse.bass as bass
import concourse.mybir as mybir
from concourse.bass_utils import run_bass_kernel_spmd

F32 = mybir.dt.float32
BF16 = mybir.dt.bfloat16
U8 = mybir.dt.uint8
AF = mybir.ActivationFunctionType
ALU = mybir.AluOpType

P = 128
S = 16
D = 1024
NDT = 8
NT = 2048
NPR = 6144
C = 128
NCH_OWN = 16
NCH_ALL = 64
BT_PAD = 0
BT_COLS = NPR + NT
OWN0 = NPR
EPS = 1e-5
KF = 16384

DBG = set()
BUILD_PHASES = 99
ENTRY_SUB = 99
SKIP_TABLES = False
N_GROUPS = 4
TAPG = "own"
INTER_G = True
DBG_VTS = 0
DBG_VTD = 0
N1DT = 0
SKIP_INTRA = False
SKIP_DSKIP = False
SKIP_INTER = False


class Ctx:
    def __init__(self, nc, stack):
        self.nc = nc
        self.eng = {"T": nc.tensor, "V": nc.vector, "Sc": nc.scalar,
                    "G": nc.gpsimd, "Sy": nc.sync}
        self.sems = {}
        self.cnt = {}
        self.waited = {}
        for name in ("dS", "dG", "T", "V", "Sc", "G"):
            self.sems[name] = stack.enter_context(nc.semaphore(name))
            self.cnt[name] = 0

    def inc(self, ins, sem, amt=1):
        self.cnt[sem] += amt
        ins.then_inc(self.sems[sem], amt)
        return (sem, self.cnt[sem])

    def ev(self, sem):
        return (sem, self.cnt[sem])

    def incd(self, engname, sem=None):
        return self.inc(self.eng[engname].drain(), sem or engname)

    def wait(self, engname, *evs):
        for ev in evs:
            if ev is None:
                continue
            sem, val = ev
            key = (engname, sem)
            if self.waited.get(key, 0) >= val:
                continue
            self.eng[engname].wait_ge(self.sems[sem], val)
            self.waited[key] = val

    def fence(self):
        for en in ("V", "Sc", "G"):
            self.inc(self.eng[en].drain(), en)
        evs = [self.ev(s) for s in ("dS", "dG", "T", "V", "Sc", "G")]
        for en in ("T", "V", "Sc", "G", "Sy"):
            self.wait(en, *evs)


class Region:
    def __init__(self, nc, name, nbytes):
        self.t = nc.alloc_sbuf_tensor(name, [P, nbytes], U8)
        self.nbytes = nbytes
        self.name = name

    def v(self, off, dtype, rows, cols):
        esz = 2 if dtype == BF16 else 4
        assert off % 4 == 0 and off + cols * esz <= self.nbytes, \
            f"{self.name} off={off} cols={cols}"
        return self.t.ap()[0:rows, off:off + cols * esz].bitcast(dtype)


class NS:
    def __init__(self, **kw):
        self.__dict__.update(kw)


def build_program():
    nc = bass.Bass("TRN2", target_bir_lowering=False, debug=False)

    def din(name, shape, dt=F32):
        return nc.dram_tensor(name, list(shape), dt, kind="ExternalInput").ap()

    ins_d = dict(
        xo=din("xo", [NT, D]),
        xp=din("xp", [NPR, D]),
        logA=din("logA", [D, S]),
        CwIn=din("CwIn", [D, S]),
        CwIn2=din("CwIn2", [D, S]),
        BwEIn=din("BwEIn", [NDT, P, 18]),
        DvIn=din("DvIn", [NDT, P]),
        ymIn=din("ymIn", [32, P, P]),
        zlIn=din("zlIn", [P, 32 * P]),
        wc=din("wc", [D, KF]),
        cI128f=din("cI128f", [P, P]),
        cI128b=din("cI128b", [P, P], BF16),
        cIotaNeg=din("cIotaNeg", [P, P]),
        cM0=din("cM0", [P, 8], BF16),
        cSelRep=din("cSelRep", [S, P]),
        cSelW=din("cSelW", [S, 8 * P], BF16),
        cOnes1=din("cOnes1", [1, P]),
        cOnes1b=din("cOnes1b", [1, P], BF16),
        cNegD8=din("cNegD8", [P, 1]),
        cRevJ=din("cRevJ", [P, 1]),
        cOnesColF=din("cOnesColF", [P, 1]),
        cOnesColB=din("cOnesColB", [P, 1], BF16),
        cEps=din("cEps", [P, 1]),
    )
    out = nc.dram_tensor("out", [D, NT], F32, kind="ExternalOutput").ap()
    wt_hbm = nc.dram_tensor("wt_hbm", [KF, D], BF16, kind="Internal").ap()

    dbg_out = {}
    for name, shape, dt in [
        ("bt", [S, BT_COLS], BF16),
        ("x0f", [D, NT], F32),
        ("rows", [2, NT], BF16),
        ("est", [P, P * S], F32),
        ("aout", [D, NT], F32),
        ("x1", [D, NT], F32),
        ("hdbg", [P, 512], BF16),
        ("stat", [P, 64], F32),
        ("est2", [P, P * S], F32),
        ("sqa", [P, 8], F32),
        ("statp", [18, 512], F32),
        ("ddbg", [P, NDT * P + 2 * NDT], BF16),
        ("xpf", [D, 512], BF16),
    ]:
        if name in DBG:
            dbg_out[name] = nc.dram_tensor(
                "dbg_" + name, list(shape), dt, kind="ExternalOutput").ap()

    stack = contextlib.ExitStack()
    with stack:
        RX = Region(nc, "RX", 65536)
        RA = Region(nc, "RA", 32768)
        RB = Region(nc, "RB", 32768)
        RC = Region(nc, "RC", 32768)
        RD = Region(nc, "RD", 32768)

        sb = dict(
            e1T=nc.alloc_sbuf_tensor("e1T", [P, NDT * S], F32),
            a1c=nc.alloc_sbuf_tensor("a1c", [P, NDT * S], F32),
            
            c16=nc.alloc_sbuf_tensor("c16", [S, 1], F32),
            dcols=nc.alloc_sbuf_tensor("dcols", [P, NDT], F32),
            ddiag=nc.alloc_sbuf_tensor("ddiag", [P, NDT * P], BF16),
            consts=nc.alloc_sbuf_tensor("consts", [P, 520], F32),
            constsb=nc.alloc_sbuf_tensor("constsb", [P, 2600], U8),
            rows=nc.alloc_sbuf_tensor("rows", [1, 2 * NT], BF16),
            c16rep=nc.alloc_sbuf_tensor("c16rep", [P, S], F32),
            sqacc=nc.alloc_sbuf_tensor("sqacc", [P, 8], F32),
        )

        p512 = [stack.enter_context(nc.psum_tensor(f"p512_{i}", [P, 512], F32))
                for i in range(3)]
        pzz = stack.enter_context(nc.psum_tensor("pzz", [P, 1024], F32))
        pTb = stack.enter_context(nc.psum_tensor("pTb", [P, 2048], BF16))

        ctx = Ctx(nc, stack)
        block = stack.enter_context(nc.Block())

        n = NS(**ins_d, **sb, out=out, wt_hbm=wt_hbm, dbg=dbg_out,
               RX=RX, RA=RA, RB=RB, RC=RC, RD=RD,
               p512=p512, pzz=pzz, pTb=pTb)
        cs = n.consts.ap()
        cb = n.constsb.ap()
        n.vI128f = cs[:, 0:128]
        n.vIotaNeg = cs[:, 128:256]
        n.vSelRep = cs[0:S, 256:384]
        n.vOnes1 = cs[0:1, 384:512]
        n.vNegD8 = cs[:, 512:513]
        n.vRevJ = cs[:, 513:514]
        n.vOnesColF = cs[:, 514:515]
        n.vEps = cs[:, 515:516]
        n.vI128b = cb[:, 0:256].bitcast(BF16)
        n.vM0 = cb[:, 256:272].bitcast(BF16)
        n.vSelW = cb[0:S, 272:2320].bitcast(BF16)
        n.vOnes1b = cb[0:1, 2320:2576].bitcast(BF16)
        n.vOnesColB = cb[:, 2576:2578].bitcast(BF16)

        @block.sync
        def _(sy):
            emit_all(nc, ctx, n)

    return nc, dbg_out


def emit_all(nc, ctx, n):
    T, V, Sc, G, Sy = (ctx.eng[k] for k in ("T", "V", "Sc", "G", "Sy"))
    p512, pzz, pTb = n.p512, n.pzz, n.pTb
    RX, RA, RB, RC, RD = n.RX, n.RA, n.RB, n.RC, n.RD

    x0f = [RX.v(dt * 8192, F32, P, NT) for dt in range(NDT)]
    n.x0f = x0f
    n.cwT = RC.v(30720, F32, P, NDT * S)
    n.a128 = n.RA.t
    n.a128 = n.RA.v(27648, F32, P, NDT * S)
    n.bwe = n.RA.v(28160 + 32 if False else 28672, BF16, P, NDT * 18)

    RCdv = RC.v(28672, F32, 8, P)
    for ap_dst, src in [
        (n.vI128f, n.cI128f[:]), (n.vIotaNeg, n.cIotaNeg[:]),
        (n.vSelRep, n.cSelRep[:]), (n.vOnes1, n.cOnes1[:]),
        (n.vNegD8, n.cNegD8[:]), (n.vRevJ, n.cRevJ[:]),
        (n.vOnesColF, n.cOnesColF[:]), (n.vI128b, n.cI128b[:]),
        (n.vM0, n.cM0[:]), (n.vSelW, n.cSelW[:]),
        (n.vOnes1b, n.cOnes1b[:]), (n.vOnesColB, n.cOnesColB[:]),
        (n.vEps, n.cEps[:]),
        (n.e1T[:].rearrange("p (n s) -> p n s", n=NDT),
         n.logA.rearrange("(n p) s -> p n s", p=P)),
        (n.a1c[:].rearrange("p (n s) -> p n s", n=NDT),
         n.CwIn.rearrange("(n p) s -> p n s", p=P)),
        (n.cwT.rearrange("p (n s) -> p n s", n=NDT),
         n.CwIn2.rearrange("(n p) s -> p n s", p=P)),
        (RCdv, n.DvIn[:]),
    ]:
        ctx.inc(Sy.dma_start(out=ap_dst, in_=src), "dS", 16)
    ctx.inc(G.dma_start(out=n.bwe.rearrange("p (n c) -> p n c", n=NDT),
                        in_=n.BwEIn.rearrange("n p c -> p n c")), "dG", 16)
    ctx.fence()

    if not SKIP_TABLES:
        ctx.inc(Sc.activation(n.e1T[:], n.e1T[:], AF.Exp), "Sc")
        ctx.inc(Sc.activation(n.a128, n.e1T[:], AF.Exp, scale=-128.0), "Sc")
        sc_a1 = RC.v(24576, F32, P, NDT * S)
        ev = ctx.inc(Sc.activation(sc_a1, n.e1T[:], AF.Exp, scale=-1.0), "Sc")
        ctx.wait("V", ev)
        ctx.inc(V.tensor_mul(n.a1c[:], n.a1c[:], sc_a1), "V")

        ev = ctx.inc(T.transpose(pzz[0:P, 0:8], RCdv[0:8, 0:P],
                                 n.vI128f[0:8, 0:8]), "T")
        ctx.wait("V", ev)
        V.tensor_copy(n.dcols[:], pzz[:, 0:8])
        ctx.incd("V")
        for dt in range(NDT):
            ctx.inc(V.tensor_scalar_mul(n.ddiag[:, dt * P:(dt + 1) * P],
                                        n.vI128b, n.dcols[:, dt:dt + 1]), "V")

        for dt in range(NDT):
            ins = T.matmul(p512[2][0:18, 0:1], n.bwe[:, dt * 18:(dt + 1) * 18],
                           n.vOnesColB, start=(dt == 0), stop=(dt == NDT - 1))
        ev = ctx.inc(ins, "T")
        ctx.wait("V", ev)
        ev = ctx.inc(V.tensor_copy(n.c16[:], p512[2][0:S, 0:1]), "V")
        ctx.wait("T", ev)
        ev = ctx.inc(T.transpose(pzz[0:1, 8:24], n.c16[:], n.vI128f[0:S, 0:S]),
                     "T")
        ctx.wait("V", ev)
        c16row = RC.v(29696, F32, 1, S)
        ev = ctx.inc(V.tensor_copy(c16row, pzz[0:1, 8:24]), "V")
        ctx.wait("T", ev)
        ev = ctx.inc(T.matmul(pzz[0:P, 32:48], n.vOnes1, c16row,
                              start=True, stop=True), "T")
        ctx.wait("V", ev)
        ctx.inc(V.tensor_copy(n.c16rep[:], pzz[:, 32:48]), "V")
        ctx.fence()

        e1sm = RC.v(16384, F32, S, D)
        cwsm = RC.v(20480, F32, S, D)
        for src, dst in [(n.e1T[:], e1sm), (n.cwT, cwsm)]:
            for dt in range(NDT):
                ins = T.transpose(pzz[0:S, dt * P:(dt + 1) * P],
                                  src[:, dt * S:(dt + 1) * S], n.vI128f)
            ev = ctx.inc(ins, "T")
            ctx.wait("V", ev)
            ev = ctx.inc(V.tensor_copy(dst, pzz[0:S, :]), "V")
            ctx.wait("T", ev)

        e1rep = RC.v(0, F32, P, D)
        crep = RC.v(4096, F32, P, D)
        for src, dst in [(e1sm, e1rep), (cwsm, crep)]:
            for half in range(2):
                ins = T.matmul(pzz[:, half * 512:(half + 1) * 512], n.vSelRep,
                               src[0:S, half * 512:(half + 1) * 512],
                               start=True, stop=True)
            ev = ctx.inc(ins, "T")
            ctx.wait("V", ev)
            ev = ctx.inc(V.tensor_copy(dst, pzz[:, :]), "V")
            ctx.wait("T", ev)

        a8 = RC.v(8192, F32, P, D)
        tg = RC.v(12288, F32, P, D)
        ctx.wait("Sc", ctx.ev("V"))
        ev = ctx.inc(Sc.activation(a8, e1rep, AF.Exp, scale=-8.0), "Sc")
        ev2 = ctx.inc(V.tensor_scalar_mul(tg, e1rep, n.vNegD8), "V")
        ctx.wait("Sc", ev2)
        ev3 = ctx.inc(Sc.activation(tg, tg, AF.Exp), "Sc")
        ctx.wait("V", ev, ev3)
        W2 = [RB.v(g * 2048, BF16, P, D) for g in range(16)]
        n.W2 = W2
        for g in range(16):
            ctx.inc(V.tensor_mul(W2[g], tg, crep), "V")
            if g < 15:
                ctx.inc(V.tensor_mul(tg, tg, a8), "V")
        ctx.fence()

    if BUILD_PHASES <= 1:
        finish(nc, ctx, n, partial=True)
        return
    Btbf = RA.v(0, BF16, S, BT_COLS)
    n.Btbf = Btbf
    xbf = [RC.v(dt * 4096, BF16, P, NT) for dt in range(NDT)]
    n.xbf = xbf
    stg = [[RD.v((b * 4 + j) * 4096, F32, P, D) for j in range(4)]
           for b in range(2)]
    xorr = n.xo.rearrange("(n p) d -> n p d", p=P)
    for grp in range(N_GROUPS):
        b = grp % 2
        for j in range(4):
            ev_ld = ctx.inc(Sy.dma_start(out=stg[b][j],
                                         in_=xorr[grp * 4 + j]), "dS", 16)
        ctx.wait("T", ev_ld)
        for dt in range(NDT):
            if ENTRY_SUB == -4:
                break
            ph = (dt % 2) * 512
            for j in range(4):
                ins = T.transpose(pzz[:, ph + j * P:ph + (j + 1) * P],
                                  stg[b][j][:, dt * P:(dt + 1) * P], n.vI128f)
            ev_tr = ctx.inc(ins, "T")
            if ENTRY_SUB == -3:
                continue
            ctx.wait("Sc", ev_tr)
            ev_a = ctx.inc(Sc.copy(x0f[dt][:, grp * 512:(grp + 1) * 512],
                                   pzz[:, ph:ph + 512]), "Sc")
            ctx.wait("T", ev_a)
            if ENTRY_SUB == -2:
                continue
            ctx.wait("V", ev_tr, ev_a)
            ev_b = ctx.inc(V.tensor_copy(xbf[dt][:, grp * 512:(grp + 1) * 512],
                                         pzz[:, ph:ph + 512]), "V")
            ctx.wait("T", ev_a, ev_b)
        if ENTRY_SUB >= 2 and ENTRY_SUB > 0:
            for j in range(4):
                ctx.inc(Sc.activation(stg[b][j], stg[b][j], AF.Square,
                                      accum_out=n.sqacc[:, j:j + 1]), "Sc")
        if ENTRY_SUB >= 3 and ENTRY_SUB > 0:
            for dt in range(NDT):
                ins = T.matmul(p512[2][0:18, :],
                               n.bwe[:, dt * 18:(dt + 1) * 18],
                               xbf[dt][:, grp * 512:(grp + 1) * 512],
                               start=(dt == 0), stop=(dt == NDT - 1))
            ev_bx = ctx.inc(ins, "T")
        if ENTRY_SUB >= 4 and ENTRY_SUB > 0:
            emit_rows_and_bt(
                nc, ctx, n, ev_bx, stg[b][0],
                bt_dst=Btbf[0:S, OWN0 + grp * 512:OWN0 + (grp + 1) * 512],
                rows_off=grp * 512)
        ctx.fence()

    if BUILD_PHASES <= 2:
        finish(nc, ctx, n, partial=True)
        return
    pstg = [[RD.v((b * 4 + j) * 2048, BF16, P, D) for j in range(4)]
            for b in range(2)]
    xpfT = [RD.v(16384 + dt * 1024, BF16, P, 512) for dt in range(NDT)]
    pscr = RD.v(24576, F32, P, 1024)
    xprr = n.xp.rearrange("(n p) d -> n p d", p=P)
    for grp in range(12):
        b = grp % 2
        for j in range(4):
            ev_ld = ctx.inc(G.dma_start(out=pstg[b][j],
                                        in_=xprr[grp * 4 + j]), "dG", 16)
        ctx.wait("T", ev_ld)
        for dt in range(NDT):
            ph = (dt % 2) * 1024
            for j in range(4):
                ins = T.transpose(pTb[:, ph + j * P:ph + (j + 1) * P],
                                  pstg[b][j][:, dt * P:(dt + 1) * P],
                                  n.vI128b)
            ev_tr = ctx.inc(ins, "T")
            ctx.wait("V", ev_tr)
            ev_b = ctx.inc(V.tensor_copy(xpfT[dt], pTb[:, ph:ph + 512]), "V")
            ctx.wait("T", ev_b)
        ctx.wait("Sc", ctx.ev("T"))
        for j in range(4):
            ctx.inc(Sc.activation(pstg[b][j], pstg[b][j], AF.Square,
                                  accum_out=n.sqacc[:, j:j + 1]), "Sc")
        for dt in range(NDT):
            ins = T.matmul(p512[2][0:18, :], n.bwe[:, dt * 18:(dt + 1) * 18],
                           xpfT[dt], start=(dt == 0), stop=(dt == NDT - 1))
        ev_bx = ctx.inc(ins, "T")
        if "xpf" in n.dbg and grp == 11:
            ctx.wait("G", ctx.ev("V"))
            for dt in range(NDT):
                ctx.inc(G.dma_start(out=n.dbg["xpf"][dt * P:(dt + 1) * P, :],
                                    in_=xpfT[dt]), "dG", 16)
        emit_rows_and_bt(
            nc, ctx, n, ev_bx, pscr,
            bt_dst=Btbf[0:S, BT_PAD + grp * 512:BT_PAD + (grp + 1) * 512],
            rows_off=None, tap=(grp == 11 and TAPG == "pre"))
        ctx.fence()
    if BT_PAD:
        ctx.inc(V.memset(Btbf[0:S, 0:BT_PAD], 0.0), "V")
    ctx.fence()

    if "bt" in n.dbg:
        ctx.inc(G.dma_start(out=n.dbg["bt"][:], in_=Btbf), "dG", 16)
    if "x0f" in n.dbg:
        for dt in range(NDT):
            ctx.inc(G.dma_start(out=n.dbg["x0f"][dt * P:(dt + 1) * P, :],
                                in_=x0f[dt]), "dG", 16)
    if "rows" in n.dbg:
        ctx.inc(G.dma_start(out=n.dbg["rows"][:], in_=n.rows[:]), "dG", 16)
    if BUILD_PHASES <= 3:
        finish(nc, ctx, n, partial=True)
        return

    emit_scan_phase(nc, ctx, n)
    if BUILD_PHASES <= 4:
        finish(nc, ctx, n, partial=True)
        return
    emit_wprep_phase(nc, ctx, n)
    if BUILD_PHASES <= 5:
        finish(nc, ctx, n, partial=True)
        return
    emit_ssm_main(nc, ctx, n)
    if BUILD_PHASES <= 6:
        finish(nc, ctx, n, partial=True)
        return
    emit_ffn(nc, ctx, n)
    finish(nc, ctx, n, partial=False)


def finish(nc, ctx, n, partial):
    G = ctx.eng["G"]
    ctx.fence()
    if partial:
        ev = ctx.inc(G.dma_start(out=n.out[0:P, :], in_=n.x0f[0]), "dG", 16)
        ctx.wait("G", ev)
    ctx.wait("G", ctx.ev("dG"), ctx.ev("dS"))


def emit_rows_and_bt(nc, ctx, n, ev_bx, scr, bt_dst, rows_off, tap=False):
    V, Sc, T = ctx.eng["V"], ctx.eng["Sc"], ctx.eng["T"]
    p512, pzz = n.p512, n.pzz
    scr18 = scr[0:18, 0:512]
    ctx.wait("V", ev_bx, ctx.ev("Sc"))
    ev = ctx.inc(V.tensor_copy(scr18, p512[2][0:18, :]), "V")
    ctx.wait("T", ev)
    for j in range(4):
        ins = T.transpose(pzz[0:P, j * 32:j * 32 + 18],
                          scr18[0:18, j * 128:(j + 1) * 128],
                          n.vI128f[0:18, 0:18])
    ev_cols = ctx.inc(ins, "T")
    ctx.wait("V", ev_cols)
    ev_tr = None
    for j in range(4):
        Bxc = pzz[:, j * 32:j * 32 + 16]
        sumc = pzz[:, j * 32 + 16:j * 32 + 17]
        cb = 512 + j * 8
        mc = scr[:, cb:cb + 1]
        vc = scr[:, cb + 1:cb + 2]
        rc = scr[:, cb + 2:cb + 3]
        mrc = scr[:, cb + 3:cb + 4]
        mm2 = scr[:, cb + 4:cb + 5]
        V.tensor_scalar_mul(mc, sumc, 1.0 / D)
        V.tensor_scalar_mul(vc, n.sqacc[:, j:j + 1], 1.0 / D)
        V.drain()
        V.tensor_scalar_mul(mm2, mc, mc)
        V.drain()
        V.tensor_sub(vc, vc, mm2)
        ev = ctx.incd("V")
        ctx.wait("Sc", ev)
        Sc.activation(rc, vc, AF.Ln, bias=n.vEps)
        Sc.drain()
        Sc.activation(rc, rc, AF.Exp, scale=-0.5)
        ev = ctx.incd("Sc")
        ctx.wait("V", ev)
        V.tensor_scalar_mul(mrc, mc, rc)
        V.drain()
        t1 = scr[:, 576 + j * 32:576 + j * 32 + 16]
        t2 = scr[:, 576 + j * 32 + 16:576 + j * 32 + 32]
        V.tensor_scalar_mul(t1, Bxc, rc)
        V.tensor_scalar_mul(t2, n.c16rep[:], mrc)
        V.drain()
        V.tensor_sub(t1, t1, t2)
        ev = ctx.incd("V")
        ctx.wait("T", ev)
        ev_tr = ctx.inc(T.transpose(
            pzz[0:S, 256 + j * 128:256 + (j + 1) * 128], t1, n.vI128f), "T")
        if rows_off is not None:
            T.transpose(pzz[0:1, 768:896], mrc, n.vI128f)
            ev2 = ctx.inc(T.transpose(pzz[0:1, 896:1024], rc, n.vI128f), "T")
            ctx.wait("V", ev2)
            rb = n.rows.ap()
            V.tensor_copy(rb[0:1, rows_off + j * 128:rows_off + (j + 1) * 128],
                          pzz[0:1, 768:896])
            ev3 = ctx.incd("V")
            V.tensor_copy(
                rb[0:1, NT + rows_off + j * 128:NT + rows_off + (j + 1) * 128],
                pzz[0:1, 896:1024])
            ev3 = ctx.incd("V")
            ctx.wait("T", ev3)
    if "stat" in n.dbg and tap:
        ctx.wait("G", ctx.ev("V"), ctx.ev("Sc"), ctx.ev("T"))
        ctx.inc(ctx.eng["G"].dma_start(out=n.dbg["stat"][:],
                                       in_=scr[:, 512:576]), "dG", 16)
        ctx.inc(ctx.eng["G"].dma_start(out=n.dbg["sqa"][:],
                                       in_=n.sqacc[:]), "dG", 16)
        ctx.inc(ctx.eng["G"].dma_start(out=n.dbg["statp"][:],
                                       in_=scr[0:18, 0:512]), "dG", 16)
    ctx.wait("V", ev_tr)
    ctx.inc(V.tensor_copy(bt_dst, pzz[0:S, 256:768]), "V")



def emit_ln_rows(nc, ctx, n, sum_ap, sq_ap, scr, scrcols, rrep_ps, qrep_ps,
                 ntok=512):
    V, Sc, T = ctx.eng["V"], ctx.eng["Sc"], ctx.eng["T"]
    pzz = n.pzz
    nj = ntok // 128
    V.tensor_copy(scr[0:1, 0:ntok], sum_ap)
    V.tensor_copy(scr[0:1, ntok:2 * ntok], sq_ap)
    ev = ctx.incd("V")
    ctx.wait("T", ev)
    evs = []
    for j in range(nj):
        T.transpose(pzz[0:P, j * 4:j * 4 + 1],
                    scr[0:1, j * 128:(j + 1) * 128], n.vI128f[0:1, 0:1])
        ins = T.transpose(pzz[0:P, j * 4 + 1:j * 4 + 2],
                          scr[0:1, ntok + j * 128:ntok + (j + 1) * 128],
                          n.vI128f[0:1, 0:1])
    ev = ctx.inc(ins, "T")
    ctx.wait("V", ev)
    for j in range(nj):
        mc = scrcols[:, j * 4:j * 4 + 1]
        vc = scrcols[:, j * 4 + 1:j * 4 + 2]
        rc = scrcols[:, j * 4 + 2:j * 4 + 3]
        qc = scrcols[:, j * 4 + 3:j * 4 + 4]
        V.tensor_scalar_mul(mc, pzz[:, j * 4:j * 4 + 1], 1.0 / D)
        V.tensor_scalar_mul(vc, pzz[:, j * 4 + 1:j * 4 + 2], 1.0 / D)
        V.drain()
        V.tensor_scalar_mul(qc, mc, mc)
        V.drain()
        V.tensor_sub(vc, vc, qc)
        ev = ctx.incd("V")
        ctx.wait("Sc", ev)
        Sc.activation(rc, vc, AF.Ln, bias=n.vEps)
        Sc.drain()
        Sc.activation(rc, rc, AF.Exp, scale=-0.5)
        ev = ctx.incd("Sc")
        ctx.wait("V", ev)
        V.tensor_scalar_mul(qc, mc, rc)
        V.drain()
        V.tensor_scalar(qc, qc, -1.0, None, ALU.mult)
        ev = ctx.incd("V")
        ctx.wait("T", ev)
        T.transpose(pzz[0:1, 64:192], rc, n.vI128f)
        ev = ctx.inc(T.transpose(pzz[0:1, 192:320], qc, n.vI128f), "T")
        ctx.wait("V", ev)
        V.tensor_copy(scr[0:1, j * 128:(j + 1) * 128], pzz[0:1, 64:192])
        V.tensor_copy(scr[0:1, ntok + j * 128:ntok + (j + 1) * 128],
                      pzz[0:1, 192:320])
        ev = ctx.incd("V")
        ctx.wait("T", ev)
    for j in range(nj):
        T.matmul(rrep_ps[:, j * 128:(j + 1) * 128], n.vOnes1,
                 scr[0:1, j * 128:(j + 1) * 128], start=True, stop=True)
        ins = T.matmul(qrep_ps[:, j * 128:(j + 1) * 128], n.vOnes1,
                       scr[0:1, ntok + j * 128:ntok + (j + 1) * 128],
                       start=True, stop=True)
    return ctx.inc(ins, "T")


def emit_scan_phase(nc, ctx, n):
    T, V, Sc, G, Sy = (ctx.eng[k] for k in ("T", "V", "Sc", "G", "Sy"))
    p512, pzz, pTb = n.p512, n.pzz, n.pTb
    RA, RC, RD = n.RA, n.RC, n.RD

    Rtab = [RC.v(i * 256, BF16, P, P) for i in range(128)]
    n.Rtab = Rtab
    tmpf = [RD.v(i * 512, F32, P, P) for i in range(2)]
    evp = [None, None]
    for i in range(128):
        b = i % 2
        ctx.wait("V", evp[b])
        V.tensor_scalar_mul(tmpf[b], n.vIotaNeg, n.e1T[:, i:i + 1])
        ev = ctx.incd("V")
        ctx.wait("Sc", ev)
        evp[b] = ctx.inc(Sc.activation(Rtab[i], tmpf[b], AF.Exp), "Sc")
    ctx.fence()

    Vtab = [RD.v(s * 2048, BF16, P, D) for s in range(16)]
    e1piece = RA.v(24576, F32, 1, P)
    tmpV = RA.v(16384, F32, P, D)
    for s in range(16):
        for dt in range(NDT):
            ev = ctx.inc(T.transpose(
                pzz[0:1, 0:128],
                n.e1T[:, dt * S + s:dt * S + s + 1], n.vI128f), "T")
            ctx.wait("V", ev)
            V.tensor_copy(e1piece, pzz[0:1, 0:128])
            ev = ctx.incd("V")
            ctx.wait("T", ev)
            ins = T.matmul(p512[dt // 4][:, (dt % 4) * 128:(dt % 4 + 1) * 128],
                           n.vOnes1, e1piece, start=True, stop=True)
            ev = ctx.inc(ins, "T")
        ctx.wait("V", ev)
        V.tensor_scalar_mul(tmpV[:, 0:512], p512[0][:, :], n.vRevJ)
        ev = ctx.inc(V.tensor_scalar_mul(tmpV[:, 512:1024], p512[1][:, :],
                                         n.vRevJ), "V")
        ctx.wait("Sc", ev)
        ev = ctx.inc(Sc.activation(Vtab[s], tmpV, AF.Exp), "Sc")
        ctx.wait("T", ev)
        ctx.wait("V", ev)
    ctx.fence()

    BtT = RA.v(24576, BF16, P, NCH_ALL * S)
    n.BtT = BtT
    evb = {}
    for batch in range(8):
        ph = (batch % 2) * 1024
        if batch >= 2:
            ctx.wait("T", evb[batch % 2])
        for cc in range(8):
            c = batch * 8 + cc
            ins = T.transpose(
                pTb[:, ph + cc * S:ph + (cc + 1) * S],
                n.Btbf[0:S, BT_PAD + c * 128:BT_PAD + (c + 1) * 128],
                n.vI128b[0:S, 0:S])
        ev = ctx.inc(ins, "T")
        ctx.wait("V", ev)
        evb[batch % 2] = ctx.inc(
            V.tensor_copy(BtT[:, batch * 128:(batch + 1) * 128],
                          pTb[:, ph:ph + 128]), "V")
    ctx.fence()

    Estore = RA.v(16384, F32, P, 128 * S)
    n.Estore = Estore
    vseg = RA.v(26624, F32, P, 64)
    a128rep = RA.v(26880, F32, P, 64)
    hscan = RA.v(27136, F32, P, 64)
    btt3 = BtT.rearrange("p (c u) -> p c u", u=S)
    evseg = {0: None, 1: None}
    for dt in range(NDT):
        for s in range(S):
            seg = dt * S + s
            b = seg % 2
            pb = p512[b]
            ctx.wait("T", evseg[b])
            rhs = btt3[:, :, s:s + 1].squeeze(2)
            ev = ctx.inc(T.matmul(pb[:, 0:64],
                                  Vtab[s][:, dt * 128:(dt + 1) * 128],
                                  rhs, start=True, stop=True), "T")
            ctx.wait("V", ev)
            V.tensor_copy(vseg, pb[:, 0:64])
            V.memset(a128rep, 1.0)
            V.drain()
            V.tensor_scalar_mul(a128rep, a128rep, n.a128[:, seg:seg + 1])
            V.drain()
            V.tensor_tensor_scan(hscan, a128rep, vseg, 0.0, ALU.mult, ALU.add)
            V.drain()
            V.tensor_scalar_mul(Estore[:, seg * S:(seg + 1) * S],
                                hscan[:, 47:63], n.a1c[:, seg:seg + 1])
            evseg[b] = ctx.incd("V")
    ctx.fence()
    if "ddbg" in n.dbg:
        ctx.inc(ctx.eng["G"].dma_start(out=n.dbg["ddbg"][:, 0:NDT * P],
                                       in_=n.ddiag[:]), "dG", 16)
        ctx.inc(ctx.eng["G"].dma_start(
            out=n.dbg["ddbg"][:, NDT * P:NDT * P + NDT].bitcast(F32)
            if False else n.dbg["ddbg"][:, NDT * P:NDT * P + 2 * NDT],
            in_=n.dcols[:].bitcast(BF16)), "dG", 16)
        ctx.fence()
    if "hdbg" in n.dbg:
        for i in range(4):
            ctx.inc(ctx.eng["G"].dma_start(
                out=n.dbg["hdbg"][:, i * 128:(i + 1) * 128],
                in_=Vtab[DBG_VTS + i][:, DBG_VTD * 128:(DBG_VTD + 1) * 128]),
                "dG", 16)
        ctx.fence()
    if "est" in n.dbg:
        ctx.inc(G.dma_start(out=n.dbg["est"][:], in_=Estore), "dG", 16)
        ctx.fence()


def emit_wprep_phase(nc, ctx, n):
    T, V, Sc, G, Sy = (ctx.eng[k] for k in ("T", "V", "Sc", "G", "Sy"))
    pTb = n.pTb
    RD = n.RD
    wn = [RD.v(i * 2048, BF16, P, 1024) for i in range(2)]
    wto = [RD.v(4096 + i * 2048, BF16, P, 1024) for i in range(2)]
    wcr = n.wc.rearrange("(ot p) k -> p ot k", p=P)
    ev_tr = {0: None, 1: None}
    ev_st = {0: None, 1: None}
    for kc in range(128):
        b = kc % 2
        ctx.wait("G", ev_tr[b])
        ev_ld = ctx.inc(G.dma_start(
            out=wn[b].rearrange("p (ot k) -> p ot k", ot=NDT),
            in_=wcr[:, :, kc * 128:(kc + 1) * 128]), "dG", 16)
        ctx.wait("T", ev_ld)
        ph = b * 1024
        for ot in range(NDT):
            ins = T.transpose(pTb[:, ph + ot * 128:ph + (ot + 1) * 128],
                              wn[b][:, ot * 128:(ot + 1) * 128], n.vI128b)
        ev_tr[b] = ctx.inc(ins, "T")
        ctx.wait("V", ev_tr[b], ev_st[b])
        ev = ctx.inc(V.tensor_copy(wto[b], pTb[:, ph:ph + 1024]), "V")
        ctx.wait("Sy", ev)
        ev_st[b] = ctx.inc(Sy.dma_start(
            out=n.wt_hbm[kc * 128:(kc + 1) * 128, :], in_=wto[b]), "dS", 16)
    ctx.fence()


def emit_ssm_main(nc, ctx, n):
    T, V, Sc, G, Sy = (ctx.eng[k] for k in ("T", "V", "Sc", "G", "Sy"))
    p512, pzz, pTb = n.p512, n.pzz, n.pTb
    RA, RD = n.RA, n.RD

    Bwide = RA.v(27648, BF16, P, 2176)
    for blk5 in range(5):
        width = 512 if blk5 < 4 else 128
        for d8 in range(8):
            src = n.Btbf[0:S,
                         OWN0 - 128 + blk5 * 512 - d8:
                         OWN0 - 128 + blk5 * 512 - d8 + width]
            ins = T.matmul(pzz[:, 0:width], n.vSelW[:, d8 * 128:(d8 + 1) * 128],
                           src, start=(d8 == 0), stop=(d8 == 7))
        ev = ctx.inc(ins, "T")
        ctx.wait("V", ev)
        ev = ctx.inc(V.tensor_copy(Bwide[:, blk5 * 512:blk5 * 512 + width],
                                   pzz[:, 0:width]), "V")
        ctx.wait("T", ev)

    bwp = [RA.v(24576 + i * 1024, BF16, P, 512) for i in range(2)]
    ablock = [RD.v(dt * 2048, F32, P, 512) for dt in range(NDT)]
    abf = [RD.v(24576 + dt * 1024, BF16, P, 512) for dt in range(NDT)]
    n1bf = [RD.v(16384 + dt * 1024, BF16, P, 512) for dt in range(NDT)]
    sqscr = RA.v(26624, BF16, P, 512)
    rb = n.rows.ap()

    if "est2" in n.dbg:
        ctx.inc(G.dma_start(out=n.dbg["est2"][:], in_=n.Estore), "dG", 16)
        ctx.fence()
    for blk in range(4):
        ctx.fence()
        T.matmul(p512[0][:, :], n.vOnes1b,
                 rb[0:1, NT + blk * 512:NT + (blk + 1) * 512],
                 start=True, stop=True)
        ev = ctx.inc(T.matmul(p512[1][:, :], n.vOnes1b,
                              rb[0:1, blk * 512:(blk + 1) * 512],
                              start=True, stop=True), "T")
        ctx.wait("V", ev)
        for dt in range(NDT):
            xsl = n.x0f[dt][:, blk * 512:(blk + 1) * 512]
            V.tensor_mul(ablock[dt], xsl, p512[0][:, :])
            V.drain()
            V.tensor_sub(ablock[dt], ablock[dt], p512[1][:, :])
            V.drain()
            ins = V.tensor_copy(n1bf[dt], ablock[dt])
        ev_n1 = ctx.incd("V")
        if "stat" in n.dbg and blk == 0:
            ctx.wait("G", ev_n1)
            ctx.inc(G.dma_start(out=n.dbg["est2"][:, 0:512],
                                in_=ablock[0]), "dG", 16)
            V.tensor_copy(ablock[7], p512[1][:, :])
            ev_t = ctx.incd("V")
            ctx.wait("G", ev_t)
            ctx.inc(G.dma_start(out=n.dbg["est2"][:, 512:1024],
                                in_=ablock[7]), "dG", 16)
            ctx.wait("T", ctx.ev("dG"))
            ctx.wait("V", ctx.ev("dG"))
        ctx.wait("T", ev_n1)
        if "hdbg" in n.dbg and blk == 0:
            ctx.wait("G", ev_n1)
            ctx.inc(G.dma_start(out=n.dbg["hdbg"][:], in_=n1bf[N1DT]),
                    "dG", 16)
            ctx.wait("T", ctx.ev("dG"))
            ctx.wait("V", ctx.ev("dG"))

        for ps in range(2):
            dts = list(range(ps * 4, ps * 4 + 4))
            banks = [p512[0][:, :], p512[1][:, :], p512[2][:, :],
                     pzz[:, 0:512]]
            started = [False] * 4
            evbw = [None, None]
            for g in range(16 if not SKIP_INTRA else 0):
                gb = g % 2
                ctx.wait("V", evbw[gb])
                V.tensor_copy(bwp[gb], Bwide[:, 128 + blk * 512 - 8 * g:
                                             128 + blk * 512 - 8 * g + 512])
                V.drain()
                for cc in range(4):
                    if g > 0:
                        V.memset(bwp[gb][:, cc * 128:cc * 128 + 8 * g], 0.0)
                    mslice = bwp[gb][:, cc * 128 + 8 * g:cc * 128 + 8 * g + 8]
                    V.tensor_mul(mslice, mslice, n.vM0)
                ev = ctx.incd("V")
                ctx.wait("T", ev)
                for i, dt in enumerate(dts):
                    ins = T.matmul(banks[i], n.W2[g][:, dt * 128:(dt + 1) * 128],
                                   bwp[gb], start=(not started[i]), stop=False)
                    started[i] = True
                evbw[gb] = ctx.inc(ins, "T")
            ev_dskip = None
            if not SKIP_DSKIP:
                for i, dt in enumerate(dts):
                    ins = T.matmul(banks[i], n.ddiag[:, dt * P:(dt + 1) * P],
                                   n1bf[dt], start=(not started[i]), stop=False)
                    started[i] = True
                ev_dskip = ctx.inc(ins, "T")
            ring = RD.v(16384, BF16, P, 512)
            evring = [ev_dskip] * 4
            if SKIP_INTER:
                ctx.wait("V", ev_dskip)
                V.memset(ring[:, 0:128], 0.0)
                ev = ctx.incd("V")
                ctx.wait("T", ev)
            for i, dt in enumerate(dts):
                if SKIP_INTER:
                    ins = T.matmul(banks[i][:, 0:128], n.vI128b,
                                   ring[:, 0:128],
                                   start=(not started[i]), stop=True)
                    ctx.inc(ins, "T")
                    started[i] = True
                    continue
                for cc in range(4):
                    cown = blk * 4 + cc
                    for s in range(S):
                        seg = dt * S + s
                        slot = (cc * S + s) % 4
                        eng = V if (s % 2 == 0 or not INTER_G) else G
                        engname = "V" if (s % 2 == 0 or not INTER_G) else "G"
                        ctx.wait(engname, evring[slot])
                        eng.tensor_scalar_mul(
                            ring[:, slot * 128:(slot + 1) * 128],
                            n.Rtab[seg],
                            n.Estore[:, seg * S + cown:seg * S + cown + 1])
                        ev = ctx.inc(eng.drain(), engname)
                        ctx.wait("T", ev)
                        last = (cc == 3 and s == S - 1)
                        ins = T.matmul(banks[i][:, cc * 128:(cc + 1) * 128],
                                       n.vI128b,
                                       ring[:, slot * 128:(slot + 1) * 128],
                                       start=(not started[i]), stop=last)
                        started[i] = True
                        evring[slot] = ctx.inc(ins, "T")
            ev_mm = ctx.ev("T")
            for i, dt in enumerate(dts):
                ctx.wait("Sc", ev_mm)
                ev_a = ctx.inc(Sc.copy(ablock[dt], banks[i]), "Sc")
                ctx.wait("V", ev_a)
                ctx.inc(V.tensor_copy(abf[dt], banks[i]), "V")
            ctx.fence()

        if "aout" in n.dbg and blk == 0:
            ctx.wait("G", ctx.ev("Sc"), ctx.ev("V"))
            for dt in range(NDT):
                ctx.inc(G.dma_start(
                    out=n.dbg["aout"][dt * P:(dt + 1) * P, 0:512],
                    in_=ablock[dt]), "dG", 16)
        for dt in range(NDT):
            V.tensor_mul(sqscr, abf[dt], abf[dt])
            ev = ctx.incd("V")
            ctx.wait("T", ev)
            T.matmul(pzz[0:1, 0:512], n.vOnesColB, abf[dt],
                     start=(dt == 0), stop=(dt == NDT - 1))
            ev = ctx.inc(T.matmul(pzz[0:1, 512:1024], n.vOnesColB, sqscr,
                                  start=(dt == 0), stop=(dt == NDT - 1)), "T")
            ctx.wait("V", ev)
        lnscr = RD.v(24576, F32, 1, 1024)
        lncols = RD.v(28672, F32, P, 16)
        ev = emit_ln_rows(nc, ctx, n, pzz[0:1, 0:512], pzz[0:1, 512:1024],
                          lnscr, lncols, p512[0][:, :], p512[1][:, :])
        ctx.wait("V", ev)
        for dt in range(NDT):
            xsl = n.x0f[dt][:, blk * 512:(blk + 1) * 512]
            V.tensor_mul(ablock[dt], ablock[dt], p512[0][:, :])
            V.drain()
            V.tensor_add(ablock[dt], ablock[dt], p512[1][:, :])
            V.drain()
            ins = V.tensor_add(xsl, xsl, ablock[dt])
        ctx.inc(ins, "V")
        ctx.fence()
    if "x1" in n.dbg:
        for dt in range(NDT):
            ctx.inc(G.dma_start(out=n.dbg["x1"][dt * P:(dt + 1) * P, :],
                                in_=n.x0f[dt]), "dG", 16)
        ctx.fence()


def emit_ffn(nc, ctx, n):
    T, V, Sc, G, Sy = (ctx.eng[k] for k in ("T", "V", "Sc", "G", "Sy"))
    p512, pzz = n.p512, n.pzz
    RA, RB, RC, RD = n.RA, n.RB, n.RC, n.RD

    zl = RA.v(0, BF16, P, 4096)
    ymr = RA.v(8192, BF16, P, 32 * P)
    n2 = [RA.v(16384 + dt * 1024, BF16, P, 512) for dt in range(NDT)]
    yring = [RA.v(24576 + i * 1024, BF16, P, 512) for i in range(6)]
    facc = [RD.v(dt * 2048, F32, P, 512) for dt in range(NDT)]
    hsup = RD.v(16384, BF16, P, 16 * 512)
    x1bfs = RD.v(0, BF16, P, 512)
    sqs = RD.v(1024, BF16, P, 512)

    ctx.inc(G.dma_start(out=zl, in_=n.zlIn[:]), "dG", 16)
    ctx.inc(G.dma_start(out=ymr.rearrange("p (t c) -> p t c", t=32),
                        in_=n.ymIn.rearrange("t p c -> p t c")), "dG", 16)
    ctx.fence()

    wtbuf = [[RB.v(i * 2048, BF16, P, 1024) for i in range(16)],
             [RC.v(i * 2048, BF16, P, 1024) for i in range(16)]]
    ev_wt = [None, None]
    ev_wt_free = [None, None]

    for q4 in range(4):
        ctx.fence()
        for dt in range(NDT):
            xsl = n.x0f[dt][:, q4 * 512:(q4 + 1) * 512]
            V.tensor_copy(x1bfs, xsl)
            V.drain()
            V.tensor_mul(sqs, x1bfs, x1bfs)
            ev = ctx.incd("V")
            ctx.wait("T", ev)
            T.matmul(pzz[0:1, 0:512], n.vOnesColB, x1bfs,
                     start=(dt == 0), stop=(dt == NDT - 1))
            ev = ctx.inc(T.matmul(pzz[0:1, 512:1024], n.vOnesColB, sqs,
                                  start=(dt == 0), stop=(dt == NDT - 1)), "T")
            ctx.wait("V", ev)
        lnscr = RD.v(16384, F32, 1, 1024)
        lncols = RD.v(20480, F32, P, 16)
        ev = emit_ln_rows(nc, ctx, n, pzz[0:1, 0:512], pzz[0:1, 512:1024],
                          lnscr, lncols, p512[0][:, :], p512[1][:, :])
        ctx.wait("V", ev)
        for dt in range(NDT):
            xsl = n.x0f[dt][:, q4 * 512:(q4 + 1) * 512]
            V.tensor_mul(facc[dt], xsl, p512[0][:, :])
            V.drain()
            ins = V.tensor_add(n2[dt], facc[dt], p512[1][:, :])
        ev_n2 = ctx.inc(ins, "V")
        ctx.fence()

        for s8 in range(8):
            b = s8 % 2
            ctx.wait("Sy", ev_wt_free[b])
            for kcr in range(16):
                kc = s8 * 16 + kcr
                ins = Sy.dma_start(out=wtbuf[b][kcr],
                                   in_=n.wt_hbm[kc * 128:(kc + 1) * 128, :])
                ev_wt[b] = ctx.inc(ins, "dS", 16)
            hh = s8 // 2
            ev_y = []
            for gi in range(4):
                gg = (s8 % 2) * 4 + gi
                ev = ctx.inc(T.matmul(pzz[:, 0:512],
                                      ymr[:, (hh * 8 + gg) * P:
                                          (hh * 8 + gg + 1) * P],
                                      n2[gg], start=True, stop=True), "T")
                ctx.wait("V", ev)
                evy = ctx.inc(V.tensor_copy(yring[gi], pzz[:, 0:512]), "V")
                ctx.wait("T", evy)
                ev_y.append(evy)
            ctx.wait("T", ev_wt[b], *ev_y)
            ev_gelu = None
            for kcr in range(16):
                kc = s8 * 16 + kcr
                a4 = kc % 4
                tp = (96, 0) if a4 == 3 else None
                ctx.wait("T", ev_gelu)
                ins = T.matmul(pzz[:, (kcr % 2) * 512:(kcr % 2) * 512 + 512],
                               zl[32 * a4:32 * a4 + 32,
                                  (kc // 4) * 128:(kc // 4 + 1) * 128],
                               yring[kcr // 4][32 * a4:32 * a4 + 32, :],
                               start=True, stop=True, tile_position=tp)
                ev_z = ctx.inc(ins, "T")
                if kcr % 2 == 1:
                    ctx.wait("Sc", ev_z)
                    ev_gelu = ctx.inc(Sc.activation(
                        hsup[:, (kcr - 1) * 512:(kcr + 1) * 512],
                        pzz[:, 0:1024], AF.Gelu), "Sc")
                    ctx.wait("T", ev_gelu)
                    for kk in (kcr - 1, kcr):
                        for ot in range(3):
                            ins = T.matmul(
                                p512[ot][:, :],
                                wtbuf[b][kk][:, ot * 128:(ot + 1) * 128],
                                hsup[:, kk * 512:(kk + 1) * 512],
                                start=(kk == 0), stop=(kk == 15))
                    ctx.inc(ins, "T")
            ev_mm = ctx.ev("T")
            for ot in range(3):
                ctx.wait("V", ev_mm)
                if s8 == 0:
                    ins = V.tensor_copy(facc[ot], p512[ot][:, :])
                else:
                    ins = V.tensor_add(facc[ot], facc[ot], p512[ot][:, :])
            ev_fa = ctx.inc(ins, "V")
            ctx.wait("T", ev_fa)
            for pg, ots in ((1, (3, 4, 5)), (2, (6, 7))):
                for kcr in range(16):
                    for oi, ot in enumerate(ots):
                        ins = T.matmul(
                            p512[oi][:, :],
                            wtbuf[b][kcr][:, ot * 128:(ot + 1) * 128],
                            hsup[:, kcr * 512:(kcr + 1) * 512],
                            start=(kcr == 0), stop=(kcr == 15))
                ev_mm = ctx.inc(ins, "T")
                for oi, ot in enumerate(ots):
                    ctx.wait("V", ev_mm)
                    if s8 == 0:
                        ins = V.tensor_copy(facc[ot], p512[oi][:, :])
                    else:
                        ins = V.tensor_add(facc[ot], facc[ot], p512[oi][:, :])
                ev_fa = ctx.inc(ins, "V")
                ctx.wait("T", ev_fa)
            ev_wt_free[b] = ctx.ev("T")
        ctx.fence()

        if "aout" in n.dbg and q4 == 0:
            ctx.wait("G", ctx.ev("V"))
            for dt in range(NDT):
                ctx.inc(G.dma_start(
                    out=n.dbg["aout"][dt * P:(dt + 1) * P, 0:512],
                    in_=facc[dt]), "dG", 16)
            ctx.wait("T", ctx.ev("dG"))
            ctx.wait("V", ctx.ev("dG"))
            ctx.wait("Sc", ctx.ev("dG"))
        fbf = [RD.v(16384 + dt * 1024, BF16, P, 512) for dt in range(NDT)]
        sqf = RD.v(24576, BF16, P, 512)
        for dt in range(NDT):
            V.tensor_copy(fbf[dt], facc[dt])
            V.drain()
            V.tensor_mul(sqf, fbf[dt], fbf[dt])
            ev = ctx.incd("V")
            ctx.wait("T", ev)
            T.matmul(pzz[0:1, 0:512], n.vOnesColB, fbf[dt],
                     start=(dt == 0), stop=(dt == NDT - 1))
            ev = ctx.inc(T.matmul(pzz[0:1, 512:1024], n.vOnesColB, sqf,
                                  start=(dt == 0), stop=(dt == NDT - 1)), "T")
            ctx.wait("V", ev)
        lnscr2 = RD.v(25600, F32, 1, 1024)
        lncols2 = RD.v(29696, F32, P, 16)
        ev = emit_ln_rows(nc, ctx, n, pzz[0:1, 0:512], pzz[0:1, 512:1024],
                          lnscr2, lncols2, p512[0][:, :], p512[1][:, :])
        ctx.wait("V", ev)
        for dt in range(NDT):
            xsl = n.x0f[dt][:, q4 * 512:(q4 + 1) * 512]
            V.tensor_mul(facc[dt], facc[dt], p512[0][:, :])
            V.drain()
            V.tensor_add(facc[dt], facc[dt], p512[1][:, :])
            V.drain()
            V.tensor_add(xsl, xsl, facc[dt])
            ev = ctx.incd("V")
            ctx.wait("G", ev)
            ctx.inc(G.dma_start(out=n.out[dt * P:(dt + 1) * P,
                                         q4 * 512:(q4 + 1) * 512],
                                in_=xsl), "dG", 16)


def _mk_consts():
    c = {}
    c["cI128f"] = np.eye(P, dtype=np.float32)
    c["cI128b"] = np.eye(P, dtype=np.float32).astype(ml_dtypes.bfloat16)
    iota = np.arange(P, dtype=np.float32)
    c["cIotaNeg"] = np.tile(-iota[None, :], (P, 1))
    m0 = np.zeros((P, 8), dtype=np.float32)
    for p in range(P):
        m0[p, p // 16:] = 1.0
    c["cM0"] = m0.astype(ml_dtypes.bfloat16)
    sel = np.zeros((S, P), dtype=np.float32)
    for s in range(S):
        for d8 in range(8):
            sel[s, d8 * 16 + s] = 1.0
    c["cSelRep"] = sel
    selw = np.zeros((S, 8 * P), dtype=np.float32)
    for d8 in range(8):
        for s in range(S):
            selw[s, d8 * P + d8 * 16 + s] = 1.0
    c["cSelW"] = selw.astype(ml_dtypes.bfloat16)
    c["cOnes1"] = np.ones((1, P), dtype=np.float32)
    c["cOnes1b"] = np.ones((1, P), dtype=np.float32).astype(ml_dtypes.bfloat16)
    c["cNegD8"] = (-(np.arange(P) // 16).astype(np.float32))[:, None]
    c["cRevJ"] = (-(127.0 - np.arange(P, dtype=np.float32)))[:, None]
    c["cOnesColF"] = np.ones((P, 1), dtype=np.float32)
    c["cOnesColB"] = np.ones((P, 1), dtype=np.float32).astype(
        ml_dtypes.bfloat16)
    c["cEps"] = np.full((P, 1), EPS, dtype=np.float32)
    return c


def _host_inputs(x, logA, Bw, Cw, Dv, rm, ru, wcp):
    x = np.asarray(x)
    consts = _mk_consts()
    bwe = np.zeros((NDT, P, 18), dtype=np.float32)
    for dt in range(NDT):
        bwe[dt, :, 0:S] = np.asarray(Bw)[:, dt * P:(dt + 1) * P].T
        bwe[dt, :, 16] = 1.0
    ym = np.zeros((32, P, P), dtype=np.float32)
    rm = np.asarray(rm)
    for h in range(4):
        for g in range(8):
            for rr in range(8):
                ym[h * 8 + g, rr * 16:(rr + 1) * 16,
                   rr * 16:(rr + 1) * 16] = rm[h, g * 8 + rr]
    zl = np.zeros((P, 32 * P), dtype=np.float32)
    ru = np.asarray(ru)
    for t in range(128):
        h, r2 = t // 32, t % 32
        blk, a = t // 4, t % 4
        for rr2 in range(2):
            r_ = r2 * 2 + rr2
            zl[a * 32 + rr2 * 16: a * 32 + (rr2 + 1) * 16,
               blk * P + rr2 * 64: blk * P + (rr2 + 1) * 64] = ru[h, r_].T
    dv = np.asarray(Dv).reshape(NDT, P)

    maps = []
    for core in range(8):
        b, q = core // 4, core % 4
        t0 = q * NT
        xo = np.ascontiguousarray(x[b, t0:t0 + NT])
        pr = np.zeros((NPR, D), dtype=np.float32)
        lo = max(0, t0 - NPR)
        if t0 > 0:
            pr[NPR - (t0 - lo):] = x[b, lo:t0]
        maps.append(dict(
            xo=xo, xp=pr, logA=np.asarray(logA), CwIn=np.asarray(Cw),
            CwIn2=np.asarray(Cw), BwEIn=bwe, DvIn=dv, ymIn=ym, zlIn=zl,
            wc=np.asarray(wcp), **consts))
    return maps


def kernel(L0_attn_x, L0_attn_log_A_diag, L0_attn_B_w, L0_attn_C_w,
           L0_attn_D_vec, L0_ffn_rail_maker, L0_ffn_rail_up_proj,
           L0_ffn_w_cproj):
    maps = _host_inputs(L0_attn_x, L0_attn_log_A_diag, L0_attn_B_w,
                        L0_attn_C_w, L0_attn_D_vec, L0_ffn_rail_maker,
                        L0_ffn_rail_up_proj, L0_ffn_w_cproj)
    nc, _ = build_program()
    res = run_bass_kernel_spmd(nc, maps, core_ids=list(range(8)))
    full = np.zeros((2, 8192, D), dtype=np.float32)
    for core in range(8):
        b, q = core // 4, core % 4
        full[b, q * NT:(q + 1) * NT] = np.asarray(res.results[core]["out"]).T
    return full


# version 04
# speedup vs baseline: 1.3105x; 1.3105x over previous
import contextlib
import numpy as np
import ml_dtypes
import concourse.bass as bass
import concourse.mybir as mybir
from concourse.bass_utils import run_bass_kernel_spmd

F32 = mybir.dt.float32
BF16 = mybir.dt.bfloat16
U8 = mybir.dt.uint8
AF = mybir.ActivationFunctionType
ALU = mybir.AluOpType

P = 128
S = 16
D = 1024
NDT = 8
NT = 2048
NPR = 6144
C = 128
NCH_OWN = 16
NCH_ALL = 64
BT_PAD = 0
BT_COLS = NPR + NT
OWN0 = NPR
EPS = 1e-5
KF = 16384

DBG = set()
BUILD_PHASES = 99
ENTRY_SUB = 99
SKIP_TABLES = False
N_GROUPS = 4
TAPG = "own"
INTER_G = True
DBG_VTS = 0
DBG_VTD = 0
N1DT = 0
SKIP_INTRA = False
SKIP_DSKIP = False
SKIP_INTER = False


class Ctx:
    def __init__(self, nc, stack):
        self.nc = nc
        self.eng = {"T": nc.tensor, "V": nc.vector, "Sc": nc.scalar,
                    "G": nc.gpsimd, "Sy": nc.sync}
        self.sems = {}
        self.cnt = {}
        self.waited = {}
        for name in ("dS", "dG", "T", "V", "Sc", "G"):
            self.sems[name] = stack.enter_context(nc.semaphore(name))
            self.cnt[name] = 0

    def inc(self, ins, sem, amt=1):
        self.cnt[sem] += amt
        ins.then_inc(self.sems[sem], amt)
        return (sem, self.cnt[sem])

    def ev(self, sem):
        return (sem, self.cnt[sem])

    def incd(self, engname, sem=None):
        return self.inc(self.eng[engname].drain(), sem or engname)

    def wait(self, engname, *evs):
        for ev in evs:
            if ev is None:
                continue
            sem, val = ev
            key = (engname, sem)
            if self.waited.get(key, 0) >= val:
                continue
            self.eng[engname].wait_ge(self.sems[sem], val)
            self.waited[key] = val

    def fence(self):
        for en in ("V", "Sc", "G"):
            self.inc(self.eng[en].drain(), en)
        evs = [self.ev(s) for s in ("dS", "dG", "T", "V", "Sc", "G")]
        for en in ("T", "V", "Sc", "G", "Sy"):
            self.wait(en, *evs)


class Region:
    def __init__(self, nc, name, nbytes):
        self.t = nc.alloc_sbuf_tensor(name, [P, nbytes], U8)
        self.nbytes = nbytes
        self.name = name

    def v(self, off, dtype, rows, cols):
        esz = 2 if dtype == BF16 else 4
        assert off % 4 == 0 and off + cols * esz <= self.nbytes, \
            f"{self.name} off={off} cols={cols}"
        return self.t.ap()[0:rows, off:off + cols * esz].bitcast(dtype)


class NS:
    def __init__(self, **kw):
        self.__dict__.update(kw)


def build_program():
    nc = bass.Bass("TRN2", target_bir_lowering=False, debug=False)

    def din(name, shape, dt=F32):
        return nc.dram_tensor(name, list(shape), dt, kind="ExternalInput").ap()

    ins_d = dict(
        xo=din("xo", [NT, D]),
        xp=din("xp", [NPR, D]),
        logA=din("logA", [D, S]),
        CwIn=din("CwIn", [D, S]),
        CwIn2=din("CwIn2", [D, S]),
        BwEIn=din("BwEIn", [NDT, P, 18]),
        DvIn=din("DvIn", [NDT, P]),
        ymIn=din("ymIn", [32, P, P]),
        zlIn=din("zlIn", [P, 32 * P]),
        wc=din("wc", [D, KF]),
        cI128f=din("cI128f", [P, P]),
        cI128b=din("cI128b", [P, P], BF16),
        cIotaNeg=din("cIotaNeg", [P, P]),
        cM0=din("cM0", [P, 8], BF16),
        cSelRep=din("cSelRep", [S, P]),
        cSelW=din("cSelW", [S, 8 * P], BF16),
        cOnes1=din("cOnes1", [1, P]),
        cOnes1b=din("cOnes1b", [1, P], BF16),
        cNegD8=din("cNegD8", [P, 1]),
        cRevJ=din("cRevJ", [P, 1]),
        cOnesColF=din("cOnesColF", [P, 1]),
        cOnesColB=din("cOnesColB", [P, 1], BF16),
        cEps=din("cEps", [P, 1]),
    )
    out = nc.dram_tensor("out", [D, NT], F32, kind="ExternalOutput").ap()
    wt_hbm = nc.dram_tensor("wt_hbm", [KF, D], BF16, kind="Internal").ap()

    dbg_out = {}
    for name, shape, dt in [
        ("bt", [S, BT_COLS], BF16),
        ("x0f", [D, NT], F32),
        ("rows", [2, NT], BF16),
        ("est", [P, P * S], F32),
        ("aout", [D, NT], F32),
        ("x1", [D, NT], F32),
        ("hdbg", [P, 512], BF16),
        ("stat", [P, 64], F32),
        ("est2", [P, P * S], F32),
        ("sqa", [P, 8], F32),
        ("statp", [18, 512], F32),
        ("ddbg", [P, NDT * P + 2 * NDT], BF16),
        ("xpf", [D, 512], BF16),
    ]:
        if name in DBG:
            dbg_out[name] = nc.dram_tensor(
                "dbg_" + name, list(shape), dt, kind="ExternalOutput").ap()

    stack = contextlib.ExitStack()
    with stack:
        RX = Region(nc, "RX", 65536)
        RA = Region(nc, "RA", 32768)
        RB = Region(nc, "RB", 32768)
        RC = Region(nc, "RC", 32768)
        RD = Region(nc, "RD", 32768)

        sb = dict(
            e1T=nc.alloc_sbuf_tensor("e1T", [P, NDT * S], F32),
            a1c=nc.alloc_sbuf_tensor("a1c", [P, NDT * S], F32),
            
            c16=nc.alloc_sbuf_tensor("c16", [S, 1], F32),
            dcols=nc.alloc_sbuf_tensor("dcols", [P, NDT], F32),
            ddiag=nc.alloc_sbuf_tensor("ddiag", [P, NDT * P], BF16),
            consts=nc.alloc_sbuf_tensor("consts", [P, 520], F32),
            constsb=nc.alloc_sbuf_tensor("constsb", [P, 2600], U8),
            rows=nc.alloc_sbuf_tensor("rows", [1, 2 * NT], BF16),
            c16rep=nc.alloc_sbuf_tensor("c16rep", [P, S], F32),
            sqacc=nc.alloc_sbuf_tensor("sqacc", [P, 8], F32),
        )

        p512 = [stack.enter_context(nc.psum_tensor(f"p512_{i}", [P, 512], F32))
                for i in range(3)]
        pzz = stack.enter_context(nc.psum_tensor("pzz", [P, 1024], F32))
        pTb = stack.enter_context(nc.psum_tensor("pTb", [P, 2048], BF16))

        ctx = Ctx(nc, stack)
        block = stack.enter_context(nc.Block())

        n = NS(**ins_d, **sb, out=out, wt_hbm=wt_hbm, dbg=dbg_out,
               RX=RX, RA=RA, RB=RB, RC=RC, RD=RD,
               p512=p512, pzz=pzz, pTb=pTb)
        cs = n.consts.ap()
        cb = n.constsb.ap()
        n.vI128f = cs[:, 0:128]
        n.vIotaNeg = cs[:, 128:256]
        n.vSelRep = cs[0:S, 256:384]
        n.vOnes1 = cs[0:1, 384:512]
        n.vNegD8 = cs[:, 512:513]
        n.vRevJ = cs[:, 513:514]
        n.vOnesColF = cs[:, 514:515]
        n.vEps = cs[:, 515:516]
        n.vI128b = cb[:, 0:256].bitcast(BF16)
        n.vM0 = cb[:, 256:272].bitcast(BF16)
        n.vSelW = cb[0:S, 272:2320].bitcast(BF16)
        n.vOnes1b = cb[0:1, 2320:2576].bitcast(BF16)
        n.vOnesColB = cb[:, 2576:2578].bitcast(BF16)

        @block.sync
        def _(sy):
            emit_all(nc, ctx, n)

    return nc, dbg_out


def emit_all(nc, ctx, n):
    T, V, Sc, G, Sy = (ctx.eng[k] for k in ("T", "V", "Sc", "G", "Sy"))
    p512, pzz, pTb = n.p512, n.pzz, n.pTb
    RX, RA, RB, RC, RD = n.RX, n.RA, n.RB, n.RC, n.RD

    x0f = [RX.v(dt * 8192, F32, P, NT) for dt in range(NDT)]
    n.x0f = x0f
    n.cwT = RC.v(30720, F32, P, NDT * S)
    n.a128 = n.RA.t
    n.a128 = n.RA.v(27648, F32, P, NDT * S)
    n.bwe = n.RA.v(28160 + 32 if False else 28672, BF16, P, NDT * 18)

    RCdv = RC.v(28672, F32, 8, P)
    for ap_dst, src in [
        (n.vI128f, n.cI128f[:]), (n.vIotaNeg, n.cIotaNeg[:]),
        (n.vSelRep, n.cSelRep[:]), (n.vOnes1, n.cOnes1[:]),
        (n.vNegD8, n.cNegD8[:]), (n.vRevJ, n.cRevJ[:]),
        (n.vOnesColF, n.cOnesColF[:]), (n.vI128b, n.cI128b[:]),
        (n.vM0, n.cM0[:]), (n.vSelW, n.cSelW[:]),
        (n.vOnes1b, n.cOnes1b[:]), (n.vOnesColB, n.cOnesColB[:]),
        (n.vEps, n.cEps[:]),
        (n.e1T[:].rearrange("p (n s) -> p n s", n=NDT),
         n.logA.rearrange("(n p) s -> p n s", p=P)),
        (n.a1c[:].rearrange("p (n s) -> p n s", n=NDT),
         n.CwIn.rearrange("(n p) s -> p n s", p=P)),
        (n.cwT.rearrange("p (n s) -> p n s", n=NDT),
         n.CwIn2.rearrange("(n p) s -> p n s", p=P)),
        (RCdv, n.DvIn[:]),
    ]:
        ctx.inc(Sy.dma_start(out=ap_dst, in_=src), "dS", 16)
    ctx.inc(G.dma_start(out=n.bwe.rearrange("p (n c) -> p n c", n=NDT),
                        in_=n.BwEIn.rearrange("n p c -> p n c")), "dG", 16)
    ctx.fence()

    if not SKIP_TABLES:
        ctx.inc(Sc.activation(n.e1T[:], n.e1T[:], AF.Exp), "Sc")
        ctx.inc(Sc.activation(n.a128, n.e1T[:], AF.Exp, scale=-128.0), "Sc")
        sc_a1 = RC.v(24576, F32, P, NDT * S)
        ev = ctx.inc(Sc.activation(sc_a1, n.e1T[:], AF.Exp, scale=-1.0), "Sc")
        ctx.wait("V", ev)
        ctx.inc(V.tensor_mul(n.a1c[:], n.a1c[:], sc_a1), "V")

        ev = ctx.inc(T.transpose(pzz[0:P, 0:8], RCdv[0:8, 0:P],
                                 n.vI128f[0:8, 0:8]), "T")
        ctx.wait("V", ev)
        V.tensor_copy(n.dcols[:], pzz[:, 0:8])
        ctx.incd("V")
        for dt in range(NDT):
            ctx.inc(V.tensor_scalar_mul(n.ddiag[:, dt * P:(dt + 1) * P],
                                        n.vI128b, n.dcols[:, dt:dt + 1]), "V")

        for dt in range(NDT):
            ins = T.matmul(p512[2][0:18, 0:1], n.bwe[:, dt * 18:(dt + 1) * 18],
                           n.vOnesColB, start=(dt == 0), stop=(dt == NDT - 1))
        ev = ctx.inc(ins, "T")
        ctx.wait("V", ev)
        ev = ctx.inc(V.tensor_copy(n.c16[:], p512[2][0:S, 0:1]), "V")
        ctx.wait("T", ev)
        ev = ctx.inc(T.transpose(pzz[0:1, 8:24], n.c16[:], n.vI128f[0:S, 0:S]),
                     "T")
        ctx.wait("V", ev)
        c16row = RC.v(29696, F32, 1, S)
        ev = ctx.inc(V.tensor_copy(c16row, pzz[0:1, 8:24]), "V")
        ctx.wait("T", ev)
        ev = ctx.inc(T.matmul(pzz[0:P, 32:48], n.vOnes1, c16row,
                              start=True, stop=True), "T")
        ctx.wait("V", ev)
        ctx.inc(V.tensor_copy(n.c16rep[:], pzz[:, 32:48]), "V")
        ctx.fence()

        e1sm = RC.v(16384, F32, S, D)
        cwsm = RC.v(20480, F32, S, D)
        for src, dst in [(n.e1T[:], e1sm), (n.cwT, cwsm)]:
            for dt in range(NDT):
                ins = T.transpose(pzz[0:S, dt * P:(dt + 1) * P],
                                  src[:, dt * S:(dt + 1) * S], n.vI128f)
            ev = ctx.inc(ins, "T")
            ctx.wait("V", ev)
            ev = ctx.inc(V.tensor_copy(dst, pzz[0:S, :]), "V")
            ctx.wait("T", ev)

        e1rep = RC.v(0, F32, P, D)
        crep = RC.v(4096, F32, P, D)
        for src, dst in [(e1sm, e1rep), (cwsm, crep)]:
            for half in range(2):
                ins = T.matmul(pzz[:, half * 512:(half + 1) * 512], n.vSelRep,
                               src[0:S, half * 512:(half + 1) * 512],
                               start=True, stop=True)
            ev = ctx.inc(ins, "T")
            ctx.wait("V", ev)
            ev = ctx.inc(V.tensor_copy(dst, pzz[:, :]), "V")
            ctx.wait("T", ev)

        a8 = RC.v(8192, F32, P, D)
        tg = RC.v(12288, F32, P, D)
        ctx.wait("Sc", ctx.ev("V"))
        ev = ctx.inc(Sc.activation(a8, e1rep, AF.Exp, scale=-8.0), "Sc")
        ev2 = ctx.inc(V.tensor_scalar_mul(tg, e1rep, n.vNegD8), "V")
        ctx.wait("Sc", ev2)
        ev3 = ctx.inc(Sc.activation(tg, tg, AF.Exp), "Sc")
        ctx.wait("V", ev, ev3)
        W2 = [RB.v(g * 2048, BF16, P, D) for g in range(16)]
        n.W2 = W2
        for g in range(16):
            ctx.inc(V.tensor_mul(W2[g], tg, crep), "V")
            if g < 15:
                ctx.inc(V.tensor_mul(tg, tg, a8), "V")
        ctx.fence()

    if BUILD_PHASES <= 1:
        finish(nc, ctx, n, partial=True)
        return
    Btbf = RA.v(0, BF16, S, BT_COLS)
    n.Btbf = Btbf
    xbf = [RC.v(dt * 4096, BF16, P, NT) for dt in range(NDT)]
    n.xbf = xbf
    stg = [[RD.v((b * 4 + j) * 4096, F32, P, D) for j in range(4)]
           for b in range(2)]
    xorr = n.xo.rearrange("(n p) d -> n p d", p=P)
    for grp in range(N_GROUPS):
        b = grp % 2
        for j in range(4):
            ev_ld = ctx.inc(Sy.dma_start(out=stg[b][j],
                                         in_=xorr[grp * 4 + j]), "dS", 16)
        ctx.wait("T", ev_ld)
        for dt in range(NDT):
            if ENTRY_SUB == -4:
                break
            ph = (dt % 2) * 512
            for j in range(4):
                ins = T.transpose(pzz[:, ph + j * P:ph + (j + 1) * P],
                                  stg[b][j][:, dt * P:(dt + 1) * P], n.vI128f)
            ev_tr = ctx.inc(ins, "T")
            if ENTRY_SUB == -3:
                continue
            ctx.wait("Sc", ev_tr)
            ev_a = ctx.inc(Sc.copy(x0f[dt][:, grp * 512:(grp + 1) * 512],
                                   pzz[:, ph:ph + 512]), "Sc")
            ctx.wait("T", ev_a)
            if ENTRY_SUB == -2:
                continue
            ctx.wait("V", ev_tr, ev_a)
            ev_b = ctx.inc(V.tensor_copy(xbf[dt][:, grp * 512:(grp + 1) * 512],
                                         pzz[:, ph:ph + 512]), "V")
            ctx.wait("T", ev_a, ev_b)
        if ENTRY_SUB >= 2 and ENTRY_SUB > 0:
            for j in range(4):
                ctx.inc(Sc.activation(stg[b][j], stg[b][j], AF.Square,
                                      accum_out=n.sqacc[:, j:j + 1]), "Sc")
        if ENTRY_SUB >= 3 and ENTRY_SUB > 0:
            for dt in range(NDT):
                ins = T.matmul(p512[2][0:18, :],
                               n.bwe[:, dt * 18:(dt + 1) * 18],
                               xbf[dt][:, grp * 512:(grp + 1) * 512],
                               start=(dt == 0), stop=(dt == NDT - 1))
            ev_bx = ctx.inc(ins, "T")
        if ENTRY_SUB >= 4 and ENTRY_SUB > 0:
            emit_rows_and_bt(
                nc, ctx, n, ev_bx, stg[b][0],
                bt_dst=Btbf[0:S, OWN0 + grp * 512:OWN0 + (grp + 1) * 512],
                rows_off=grp * 512)
        ctx.fence()

    if BUILD_PHASES <= 2:
        finish(nc, ctx, n, partial=True)
        return
    pstg = [[RD.v((b * 4 + j) * 2048, BF16, P, D) for j in range(4)]
            for b in range(2)]
    xpfT = [RD.v(16384 + dt * 1024, BF16, P, 512) for dt in range(NDT)]
    pscr = RD.v(24576, F32, P, 1024)
    xprr = n.xp.rearrange("(n p) d -> n p d", p=P)
    for grp in range(12):
        b = grp % 2
        for j in range(4):
            ev_ld = ctx.inc(G.dma_start(out=pstg[b][j],
                                        in_=xprr[grp * 4 + j]), "dG", 16)
        ctx.wait("T", ev_ld)
        for dt in range(NDT):
            ph = (dt % 2) * 1024
            for j in range(4):
                ins = T.transpose(pTb[:, ph + j * P:ph + (j + 1) * P],
                                  pstg[b][j][:, dt * P:(dt + 1) * P],
                                  n.vI128b)
            ev_tr = ctx.inc(ins, "T")
            ctx.wait("V", ev_tr)
            ev_b = ctx.inc(V.tensor_copy(xpfT[dt], pTb[:, ph:ph + 512]), "V")
            ctx.wait("T", ev_b)
        ctx.wait("Sc", ctx.ev("T"))
        for j in range(4):
            ctx.inc(Sc.activation(pstg[b][j], pstg[b][j], AF.Square,
                                  accum_out=n.sqacc[:, j:j + 1]), "Sc")
        for dt in range(NDT):
            ins = T.matmul(p512[2][0:18, :], n.bwe[:, dt * 18:(dt + 1) * 18],
                           xpfT[dt], start=(dt == 0), stop=(dt == NDT - 1))
        ev_bx = ctx.inc(ins, "T")
        if "xpf" in n.dbg and grp == 11:
            ctx.wait("G", ctx.ev("V"))
            for dt in range(NDT):
                ctx.inc(G.dma_start(out=n.dbg["xpf"][dt * P:(dt + 1) * P, :],
                                    in_=xpfT[dt]), "dG", 16)
        emit_rows_and_bt(
            nc, ctx, n, ev_bx, pscr,
            bt_dst=Btbf[0:S, BT_PAD + grp * 512:BT_PAD + (grp + 1) * 512],
            rows_off=None, tap=(grp == 11 and TAPG == "pre"))
        ctx.fence()
    if BT_PAD:
        ctx.inc(V.memset(Btbf[0:S, 0:BT_PAD], 0.0), "V")
    ctx.fence()

    if "bt" in n.dbg:
        ctx.inc(G.dma_start(out=n.dbg["bt"][:], in_=Btbf), "dG", 16)
    if "x0f" in n.dbg:
        for dt in range(NDT):
            ctx.inc(G.dma_start(out=n.dbg["x0f"][dt * P:(dt + 1) * P, :],
                                in_=x0f[dt]), "dG", 16)
    if "rows" in n.dbg:
        ctx.inc(G.dma_start(out=n.dbg["rows"][:], in_=n.rows[:]), "dG", 16)
    if BUILD_PHASES <= 3:
        finish(nc, ctx, n, partial=True)
        return

    emit_scan_phase(nc, ctx, n)
    if BUILD_PHASES <= 4:
        finish(nc, ctx, n, partial=True)
        return
    emit_wprep_phase(nc, ctx, n)
    if BUILD_PHASES <= 5:
        finish(nc, ctx, n, partial=True)
        return
    emit_ssm_main(nc, ctx, n)
    if BUILD_PHASES <= 6:
        finish(nc, ctx, n, partial=True)
        return
    emit_ffn(nc, ctx, n)
    finish(nc, ctx, n, partial=False)


def finish(nc, ctx, n, partial):
    G = ctx.eng["G"]
    ctx.fence()
    if partial:
        ev = ctx.inc(G.dma_start(out=n.out[0:P, :], in_=n.x0f[0]), "dG", 16)
        ctx.wait("G", ev)
    ctx.wait("G", ctx.ev("dG"), ctx.ev("dS"))


def emit_rows_and_bt(nc, ctx, n, ev_bx, scr, bt_dst, rows_off, tap=False):
    V, Sc, T = ctx.eng["V"], ctx.eng["Sc"], ctx.eng["T"]
    p512, pzz = n.p512, n.pzz
    scr18 = scr[0:18, 0:512]
    ctx.wait("V", ev_bx, ctx.ev("Sc"))
    ev = ctx.inc(V.tensor_copy(scr18, p512[2][0:18, :]), "V")
    ctx.wait("T", ev)
    for j in range(4):
        ins = T.transpose(pzz[0:P, j * 32:j * 32 + 18],
                          scr18[0:18, j * 128:(j + 1) * 128],
                          n.vI128f[0:18, 0:18])
    ev_cols = ctx.inc(ins, "T")
    ctx.wait("V", ev_cols)
    ev_tr = None
    for j in range(4):
        Bxc = pzz[:, j * 32:j * 32 + 16]
        sumc = pzz[:, j * 32 + 16:j * 32 + 17]
        cb = 512 + j * 8
        mc = scr[:, cb:cb + 1]
        vc = scr[:, cb + 1:cb + 2]
        rc = scr[:, cb + 2:cb + 3]
        mrc = scr[:, cb + 3:cb + 4]
        mm2 = scr[:, cb + 4:cb + 5]
        V.tensor_scalar_mul(mc, sumc, 1.0 / D)
        V.tensor_scalar_mul(vc, n.sqacc[:, j:j + 1], 1.0 / D)
        V.drain()
        V.tensor_scalar_mul(mm2, mc, mc)
        V.drain()
        V.tensor_sub(vc, vc, mm2)
        ev = ctx.incd("V")
        ctx.wait("Sc", ev)
        Sc.activation(rc, vc, AF.Ln, bias=n.vEps)
        Sc.drain()
        Sc.activation(rc, rc, AF.Exp, scale=-0.5)
        ev = ctx.incd("Sc")
        ctx.wait("V", ev)
        V.tensor_scalar_mul(mrc, mc, rc)
        V.drain()
        t1 = scr[:, 576 + j * 32:576 + j * 32 + 16]
        t2 = scr[:, 576 + j * 32 + 16:576 + j * 32 + 32]
        V.tensor_scalar_mul(t1, Bxc, rc)
        V.tensor_scalar_mul(t2, n.c16rep[:], mrc)
        V.drain()
        V.tensor_sub(t1, t1, t2)
        ev = ctx.incd("V")
        ctx.wait("T", ev)
        ev_tr = ctx.inc(T.transpose(
            pzz[0:S, 256 + j * 128:256 + (j + 1) * 128], t1, n.vI128f), "T")
        if rows_off is not None:
            T.transpose(pzz[0:1, 768:896], mrc, n.vI128f)
            ev2 = ctx.inc(T.transpose(pzz[0:1, 896:1024], rc, n.vI128f), "T")
            ctx.wait("V", ev2)
            rb = n.rows.ap()
            V.tensor_copy(rb[0:1, rows_off + j * 128:rows_off + (j + 1) * 128],
                          pzz[0:1, 768:896])
            ev3 = ctx.incd("V")
            V.tensor_copy(
                rb[0:1, NT + rows_off + j * 128:NT + rows_off + (j + 1) * 128],
                pzz[0:1, 896:1024])
            ev3 = ctx.incd("V")
            ctx.wait("T", ev3)
    if "stat" in n.dbg and tap:
        ctx.wait("G", ctx.ev("V"), ctx.ev("Sc"), ctx.ev("T"))
        ctx.inc(ctx.eng["G"].dma_start(out=n.dbg["stat"][:],
                                       in_=scr[:, 512:576]), "dG", 16)
        ctx.inc(ctx.eng["G"].dma_start(out=n.dbg["sqa"][:],
                                       in_=n.sqacc[:]), "dG", 16)
        ctx.inc(ctx.eng["G"].dma_start(out=n.dbg["statp"][:],
                                       in_=scr[0:18, 0:512]), "dG", 16)
    ctx.wait("V", ev_tr)
    ctx.inc(V.tensor_copy(bt_dst, pzz[0:S, 256:768]), "V")



def emit_ln_rows(nc, ctx, n, sum_ap, sq_ap, scr, scrcols, rrep_ps, qrep_ps,
                 ntok=512):
    V, Sc, T = ctx.eng["V"], ctx.eng["Sc"], ctx.eng["T"]
    pzz = n.pzz
    nj = ntok // 128
    V.tensor_copy(scr[0:1, 0:ntok], sum_ap)
    V.tensor_copy(scr[0:1, ntok:2 * ntok], sq_ap)
    ev = ctx.incd("V")
    ctx.wait("T", ev)
    evs = []
    for j in range(nj):
        T.transpose(pzz[0:P, j * 4:j * 4 + 1],
                    scr[0:1, j * 128:(j + 1) * 128], n.vI128f[0:1, 0:1])
        ins = T.transpose(pzz[0:P, j * 4 + 1:j * 4 + 2],
                          scr[0:1, ntok + j * 128:ntok + (j + 1) * 128],
                          n.vI128f[0:1, 0:1])
    ev = ctx.inc(ins, "T")
    ctx.wait("V", ev)
    for j in range(nj):
        mc = scrcols[:, j * 4:j * 4 + 1]
        vc = scrcols[:, j * 4 + 1:j * 4 + 2]
        rc = scrcols[:, j * 4 + 2:j * 4 + 3]
        qc = scrcols[:, j * 4 + 3:j * 4 + 4]
        V.tensor_scalar_mul(mc, pzz[:, j * 4:j * 4 + 1], 1.0 / D)
        V.tensor_scalar_mul(vc, pzz[:, j * 4 + 1:j * 4 + 2], 1.0 / D)
        V.drain()
        V.tensor_scalar_mul(qc, mc, mc)
        V.drain()
        V.tensor_sub(vc, vc, qc)
        ev = ctx.incd("V")
        ctx.wait("Sc", ev)
        Sc.activation(rc, vc, AF.Ln, bias=n.vEps)
        Sc.drain()
        Sc.activation(rc, rc, AF.Exp, scale=-0.5)
        ev = ctx.incd("Sc")
        ctx.wait("V", ev)
        V.tensor_scalar_mul(qc, mc, rc)
        V.drain()
        V.tensor_scalar(qc, qc, -1.0, None, ALU.mult)
        ev = ctx.incd("V")
        ctx.wait("T", ev)
        T.transpose(pzz[0:1, 64:192], rc, n.vI128f)
        ev = ctx.inc(T.transpose(pzz[0:1, 192:320], qc, n.vI128f), "T")
        ctx.wait("V", ev)
        V.tensor_copy(scr[0:1, j * 128:(j + 1) * 128], pzz[0:1, 64:192])
        V.tensor_copy(scr[0:1, ntok + j * 128:ntok + (j + 1) * 128],
                      pzz[0:1, 192:320])
        ev = ctx.incd("V")
        ctx.wait("T", ev)
    for j in range(nj):
        T.matmul(rrep_ps[:, j * 128:(j + 1) * 128], n.vOnes1,
                 scr[0:1, j * 128:(j + 1) * 128], start=True, stop=True)
        ins = T.matmul(qrep_ps[:, j * 128:(j + 1) * 128], n.vOnes1,
                       scr[0:1, ntok + j * 128:ntok + (j + 1) * 128],
                       start=True, stop=True)
    return ctx.inc(ins, "T")


def emit_scan_phase(nc, ctx, n):
    T, V, Sc, G, Sy = (ctx.eng[k] for k in ("T", "V", "Sc", "G", "Sy"))
    p512, pzz, pTb = n.p512, n.pzz, n.pTb
    RA, RC, RD = n.RA, n.RC, n.RD

    Rtab = [RC.v(i * 256, BF16, P, P) for i in range(128)]
    n.Rtab = Rtab
    tmpf = [RD.v(i * 512, F32, P, P) for i in range(8)]
    evp = [None, None]
    for q in range(32):
        b = q % 2
        ctx.wait("V", evp[b])
        for u in range(4):
            V.tensor_scalar_mul(tmpf[b * 4 + u], n.vIotaNeg,
                                n.e1T[:, q * 4 + u:q * 4 + u + 1])
        ev = ctx.incd("V")
        ctx.wait("Sc", ev)
        for u in range(4):
            ins = Sc.activation(Rtab[q * 4 + u], tmpf[b * 4 + u], AF.Exp)
        evp[b] = ctx.inc(ins, "Sc")
    ctx.fence()

    Vtab = [RD.v(s * 2048, BF16, P, D) for s in range(16)]
    e1piece = RA.v(24576, F32, 1, P)
    tmpV = RA.v(16384, F32, P, D)
    for s in range(16):
        for dt in range(NDT):
            ev = ctx.inc(T.transpose(
                pzz[0:1, 0:128],
                n.e1T[:, dt * S + s:dt * S + s + 1], n.vI128f), "T")
            ctx.wait("V", ev)
            V.tensor_copy(e1piece, pzz[0:1, 0:128])
            ev = ctx.incd("V")
            ctx.wait("T", ev)
            ins = T.matmul(p512[dt // 4][:, (dt % 4) * 128:(dt % 4 + 1) * 128],
                           n.vOnes1, e1piece, start=True, stop=True)
            ev = ctx.inc(ins, "T")
        ctx.wait("V", ev)
        V.tensor_scalar_mul(tmpV[:, 0:512], p512[0][:, :], n.vRevJ)
        ev = ctx.inc(V.tensor_scalar_mul(tmpV[:, 512:1024], p512[1][:, :],
                                         n.vRevJ), "V")
        ctx.wait("Sc", ev)
        ev = ctx.inc(Sc.activation(Vtab[s], tmpV, AF.Exp), "Sc")
        ctx.wait("T", ev)
        ctx.wait("V", ev)
    ctx.fence()

    BtT = RA.v(24576, BF16, P, NCH_ALL * S)
    n.BtT = BtT
    evb = {}
    for batch in range(8):
        ph = (batch % 2) * 1024
        if batch >= 2:
            ctx.wait("T", evb[batch % 2])
        for cc in range(8):
            c = batch * 8 + cc
            ins = T.transpose(
                pTb[:, ph + cc * S:ph + (cc + 1) * S],
                n.Btbf[0:S, BT_PAD + c * 128:BT_PAD + (c + 1) * 128],
                n.vI128b[0:S, 0:S])
        ev = ctx.inc(ins, "T")
        ctx.wait("V", ev)
        evb[batch % 2] = ctx.inc(
            V.tensor_copy(BtT[:, batch * 128:(batch + 1) * 128],
                          pTb[:, ph:ph + 128]), "V")
    ctx.fence()

    Estore = RA.v(16384, F32, P, 128 * S)
    n.Estore = Estore
    vseg = RA.v(26624, F32, P, 64)
    a128rep = RA.v(26880, F32, P, 64)
    hscan = RA.v(27136, F32, P, 64)
    btt3 = BtT.rearrange("p (c u) -> p c u", u=S)
    evseg = {0: None, 1: None, 2: None}
    for dt in range(NDT):
        for s in range(S):
            seg = dt * S + s
            b = seg % 3
            pb = p512[b]
            ctx.wait("T", evseg[b])
            rhs = btt3[:, :, s:s + 1].squeeze(2)
            ev = ctx.inc(T.matmul(pb[:, 0:64],
                                  Vtab[s][:, dt * 128:(dt + 1) * 128],
                                  rhs, start=True, stop=True), "T")
            ctx.wait("V", ev)
            V.tensor_tensor_scan(
                hscan, n.a128[:, seg:seg + 1].to_broadcast((P, 64)),
                pb[:, 0:64], 0.0, ALU.mult, ALU.add)
            V.tensor_scalar_mul(Estore[:, seg * S:(seg + 1) * S],
                                hscan[:, 47:63], n.a1c[:, seg:seg + 1])
            evseg[b] = ctx.incd("V")
    ctx.fence()
    if "ddbg" in n.dbg:
        ctx.inc(ctx.eng["G"].dma_start(out=n.dbg["ddbg"][:, 0:NDT * P],
                                       in_=n.ddiag[:]), "dG", 16)
        ctx.inc(ctx.eng["G"].dma_start(
            out=n.dbg["ddbg"][:, NDT * P:NDT * P + NDT].bitcast(F32)
            if False else n.dbg["ddbg"][:, NDT * P:NDT * P + 2 * NDT],
            in_=n.dcols[:].bitcast(BF16)), "dG", 16)
        ctx.fence()
    if "hdbg" in n.dbg:
        for i in range(4):
            ctx.inc(ctx.eng["G"].dma_start(
                out=n.dbg["hdbg"][:, i * 128:(i + 1) * 128],
                in_=Vtab[DBG_VTS + i][:, DBG_VTD * 128:(DBG_VTD + 1) * 128]),
                "dG", 16)
        ctx.fence()
    if "est" in n.dbg:
        ctx.inc(G.dma_start(out=n.dbg["est"][:], in_=Estore), "dG", 16)
        ctx.fence()


def emit_wprep_phase(nc, ctx, n):
    T, V, Sc, G, Sy = (ctx.eng[k] for k in ("T", "V", "Sc", "G", "Sy"))
    pTb = n.pTb
    RD = n.RD
    wn = [RD.v(i * 2048, BF16, P, 1024) for i in range(2)]
    wto = [RD.v(4096 + i * 2048, BF16, P, 1024) for i in range(2)]
    wcr = n.wc.rearrange("(ot p) k -> p ot k", p=P)
    ev_tr = {0: None, 1: None}
    ev_st = {0: None, 1: None}
    for kc in range(128):
        b = kc % 2
        ctx.wait("G", ev_tr[b])
        ev_ld = ctx.inc(G.dma_start(
            out=wn[b].rearrange("p (ot k) -> p ot k", ot=NDT),
            in_=wcr[:, :, kc * 128:(kc + 1) * 128]), "dG", 16)
        ctx.wait("T", ev_ld)
        ph = b * 1024
        for ot in range(NDT):
            ins = T.transpose(pTb[:, ph + ot * 128:ph + (ot + 1) * 128],
                              wn[b][:, ot * 128:(ot + 1) * 128], n.vI128b)
        ev_tr[b] = ctx.inc(ins, "T")
        ctx.wait("V", ev_tr[b], ev_st[b])
        ev = ctx.inc(V.tensor_copy(wto[b], pTb[:, ph:ph + 1024]), "V")
        ctx.wait("Sy", ev)
        ev_st[b] = ctx.inc(Sy.dma_start(
            out=n.wt_hbm[kc * 128:(kc + 1) * 128, :], in_=wto[b]), "dS", 16)
    ctx.fence()


def emit_ssm_main(nc, ctx, n):
    T, V, Sc, G, Sy = (ctx.eng[k] for k in ("T", "V", "Sc", "G", "Sy"))
    p512, pzz, pTb = n.p512, n.pzz, n.pTb
    RA, RD = n.RA, n.RD

    Bwide = RA.v(27648, BF16, P, 2176)
    for blk5 in range(5):
        width = 512 if blk5 < 4 else 128
        for d8 in range(8):
            src = n.Btbf[0:S,
                         OWN0 - 128 + blk5 * 512 - d8:
                         OWN0 - 128 + blk5 * 512 - d8 + width]
            ins = T.matmul(pzz[:, 0:width], n.vSelW[:, d8 * 128:(d8 + 1) * 128],
                           src, start=(d8 == 0), stop=(d8 == 7))
        ev = ctx.inc(ins, "T")
        ctx.wait("V", ev)
        ev = ctx.inc(V.tensor_copy(Bwide[:, blk5 * 512:blk5 * 512 + width],
                                   pzz[:, 0:width]), "V")
        ctx.wait("T", ev)

    bwp = [RA.v(24576 + i * 1024, BF16, P, 512) for i in range(2)]
    ablock = [RD.v(dt * 2048, F32, P, 512) for dt in range(NDT)]
    abf = [RD.v(24576 + dt * 1024, BF16, P, 512) for dt in range(NDT)]
    n1bf = [RD.v(16384 + dt * 1024, BF16, P, 512) for dt in range(NDT)]
    sqscr = RA.v(26624, BF16, P, 512)
    rb = n.rows.ap()

    if "est2" in n.dbg:
        ctx.inc(G.dma_start(out=n.dbg["est2"][:], in_=n.Estore), "dG", 16)
        ctx.fence()
    for blk in range(4):
        ctx.fence()
        T.matmul(p512[0][:, :], n.vOnes1b,
                 rb[0:1, NT + blk * 512:NT + (blk + 1) * 512],
                 start=True, stop=True)
        ev = ctx.inc(T.matmul(p512[1][:, :], n.vOnes1b,
                              rb[0:1, blk * 512:(blk + 1) * 512],
                              start=True, stop=True), "T")
        ctx.wait("V", ev)
        for dt in range(NDT):
            xsl = n.x0f[dt][:, blk * 512:(blk + 1) * 512]
            V.tensor_mul(ablock[dt], xsl, p512[0][:, :])
            V.drain()
            V.tensor_sub(ablock[dt], ablock[dt], p512[1][:, :])
            V.drain()
            ins = V.tensor_copy(n1bf[dt], ablock[dt])
        ev_n1 = ctx.incd("V")
        if "stat" in n.dbg and blk == 0:
            ctx.wait("G", ev_n1)
            ctx.inc(G.dma_start(out=n.dbg["est2"][:, 0:512],
                                in_=ablock[0]), "dG", 16)
            V.tensor_copy(ablock[7], p512[1][:, :])
            ev_t = ctx.incd("V")
            ctx.wait("G", ev_t)
            ctx.inc(G.dma_start(out=n.dbg["est2"][:, 512:1024],
                                in_=ablock[7]), "dG", 16)
            ctx.wait("T", ctx.ev("dG"))
            ctx.wait("V", ctx.ev("dG"))
        ctx.wait("T", ev_n1)
        if "hdbg" in n.dbg and blk == 0:
            ctx.wait("G", ev_n1)
            ctx.inc(G.dma_start(out=n.dbg["hdbg"][:], in_=n1bf[N1DT]),
                    "dG", 16)
            ctx.wait("T", ctx.ev("dG"))
            ctx.wait("V", ctx.ev("dG"))

        for ps in range(2):
            dts = list(range(ps * 4, ps * 4 + 4))
            banks = [p512[0][:, :], p512[1][:, :], p512[2][:, :],
                     pzz[:, 0:512]]
            started = [False] * 4
            evbw = [None, None]
            for g in range(16 if not SKIP_INTRA else 0):
                gb = g % 2
                ctx.wait("V", evbw[gb])
                V.tensor_copy(bwp[gb], Bwide[:, 128 + blk * 512 - 8 * g:
                                             128 + blk * 512 - 8 * g + 512])
                V.drain()
                for cc in range(4):
                    if g > 0:
                        V.memset(bwp[gb][:, cc * 128:cc * 128 + 8 * g], 0.0)
                    mslice = bwp[gb][:, cc * 128 + 8 * g:cc * 128 + 8 * g + 8]
                    V.tensor_mul(mslice, mslice, n.vM0)
                ev = ctx.incd("V")
                ctx.wait("T", ev)
                for i, dt in enumerate(dts):
                    ins = T.matmul(banks[i], n.W2[g][:, dt * 128:(dt + 1) * 128],
                                   bwp[gb], start=(not started[i]), stop=False)
                    started[i] = True
                evbw[gb] = ctx.inc(ins, "T")
            ev_dskip = None
            if not SKIP_DSKIP:
                for i, dt in enumerate(dts):
                    ins = T.matmul(banks[i], n.ddiag[:, dt * P:(dt + 1) * P],
                                   n1bf[dt], start=(not started[i]), stop=False)
                    started[i] = True
                ev_dskip = ctx.inc(ins, "T")
            ring = RD.v(16384, BF16, P, 512)
            if SKIP_INTER:
                ctx.wait("V", ev_dskip)
                V.memset(ring[:, 0:128], 0.0)
                ev = ctx.incd("V")
                ctx.wait("T", ev)
            ev_q = ev_dskip
            for i, dt in enumerate(dts):
                if SKIP_INTER:
                    ins = T.matmul(banks[i][:, 0:128], n.vI128b,
                                   ring[:, 0:128],
                                   start=(not started[i]), stop=True)
                    ctx.inc(ins, "T")
                    started[i] = True
                    continue
                for cc in range(4):
                    cown = blk * 4 + cc
                    for sq in range(4):
                        ctx.wait("V", ev_q)
                        for u in range(4):
                            s = sq * 4 + u
                            seg = dt * S + s
                            V.tensor_scalar_mul(
                                ring[:, u * 128:(u + 1) * 128],
                                n.Rtab[seg],
                                n.Estore[:, seg * S + cown:
                                          seg * S + cown + 1])
                        ev = ctx.incd("V")
                        ctx.wait("T", ev)
                        for u in range(4):
                            last = (cc == 3 and sq == 3 and u == 3)
                            ins = T.matmul(
                                banks[i][:, cc * 128:(cc + 1) * 128],
                                n.vI128b, ring[:, u * 128:(u + 1) * 128],
                                start=(not started[i]), stop=last)
                            started[i] = True
                        ev_q = ctx.inc(ins, "T")
            ev_mm = ctx.ev("T")
            for i, dt in enumerate(dts):
                ctx.wait("Sc", ev_mm)
                ev_a = ctx.inc(Sc.copy(ablock[dt], banks[i]), "Sc")
                ctx.wait("V", ev_a)
                ctx.inc(V.tensor_copy(abf[dt], banks[i]), "V")
            ctx.fence()

        if "aout" in n.dbg and blk == 0:
            ctx.wait("G", ctx.ev("Sc"), ctx.ev("V"))
            for dt in range(NDT):
                ctx.inc(G.dma_start(
                    out=n.dbg["aout"][dt * P:(dt + 1) * P, 0:512],
                    in_=ablock[dt]), "dG", 16)
        for dt in range(NDT):
            V.tensor_mul(sqscr, abf[dt], abf[dt])
            ev = ctx.incd("V")
            ctx.wait("T", ev)
            T.matmul(pzz[0:1, 0:512], n.vOnesColB, abf[dt],
                     start=(dt == 0), stop=(dt == NDT - 1))
            ev = ctx.inc(T.matmul(pzz[0:1, 512:1024], n.vOnesColB, sqscr,
                                  start=(dt == 0), stop=(dt == NDT - 1)), "T")
            ctx.wait("V", ev)
        lnscr = RD.v(24576, F32, 1, 1024)
        lncols = RD.v(28672, F32, P, 16)
        ev = emit_ln_rows(nc, ctx, n, pzz[0:1, 0:512], pzz[0:1, 512:1024],
                          lnscr, lncols, p512[0][:, :], p512[1][:, :])
        ctx.wait("V", ev)
        for dt in range(NDT):
            xsl = n.x0f[dt][:, blk * 512:(blk + 1) * 512]
            V.tensor_mul(ablock[dt], ablock[dt], p512[0][:, :])
            V.drain()
            V.tensor_add(ablock[dt], ablock[dt], p512[1][:, :])
            V.drain()
            ins = V.tensor_add(xsl, xsl, ablock[dt])
        ctx.inc(ins, "V")
        ctx.fence()
    if "x1" in n.dbg:
        for dt in range(NDT):
            ctx.inc(G.dma_start(out=n.dbg["x1"][dt * P:(dt + 1) * P, :],
                                in_=n.x0f[dt]), "dG", 16)
        ctx.fence()


def emit_ffn(nc, ctx, n):
    T, V, Sc, G, Sy = (ctx.eng[k] for k in ("T", "V", "Sc", "G", "Sy"))
    p512, pzz = n.p512, n.pzz
    RA, RB, RC, RD = n.RA, n.RB, n.RC, n.RD

    zl = RA.v(0, BF16, P, 4096)
    ymr = RA.v(8192, BF16, P, 32 * P)
    n2 = [RA.v(16384 + dt * 1024, BF16, P, 512) for dt in range(NDT)]
    yring = [RA.v(24576 + i * 1024, BF16, P, 512) for i in range(6)]
    facc = [RD.v(dt * 2048, F32, P, 512) for dt in range(NDT)]
    hsup = RD.v(16384, BF16, P, 16 * 512)
    x1bfs = RD.v(0, BF16, P, 512)
    sqs = RD.v(1024, BF16, P, 512)

    ctx.inc(G.dma_start(out=zl, in_=n.zlIn[:]), "dG", 16)
    ctx.inc(G.dma_start(out=ymr.rearrange("p (t c) -> p t c", t=32),
                        in_=n.ymIn.rearrange("t p c -> p t c")), "dG", 16)
    ctx.fence()

    wtbuf = [[RB.v(i * 2048, BF16, P, 1024) for i in range(16)],
             [RC.v(i * 2048, BF16, P, 1024) for i in range(16)]]
    ev_wt = [None, None]
    ev_wt_free = [None, None]

    for q4 in range(4):
        ctx.fence()
        for dt in range(NDT):
            xsl = n.x0f[dt][:, q4 * 512:(q4 + 1) * 512]
            V.tensor_copy(x1bfs, xsl)
            V.drain()
            V.tensor_mul(sqs, x1bfs, x1bfs)
            ev = ctx.incd("V")
            ctx.wait("T", ev)
            T.matmul(pzz[0:1, 0:512], n.vOnesColB, x1bfs,
                     start=(dt == 0), stop=(dt == NDT - 1))
            ev = ctx.inc(T.matmul(pzz[0:1, 512:1024], n.vOnesColB, sqs,
                                  start=(dt == 0), stop=(dt == NDT - 1)), "T")
            ctx.wait("V", ev)
        lnscr = RD.v(16384, F32, 1, 1024)
        lncols = RD.v(20480, F32, P, 16)
        ev = emit_ln_rows(nc, ctx, n, pzz[0:1, 0:512], pzz[0:1, 512:1024],
                          lnscr, lncols, p512[0][:, :], p512[1][:, :])
        ctx.wait("V", ev)
        for dt in range(NDT):
            xsl = n.x0f[dt][:, q4 * 512:(q4 + 1) * 512]
            V.tensor_mul(facc[dt], xsl, p512[0][:, :])
            V.drain()
            ins = V.tensor_add(n2[dt], facc[dt], p512[1][:, :])
        ev_n2 = ctx.inc(ins, "V")
        ctx.fence()

        for s8 in range(8):
            b = s8 % 2
            ctx.wait("Sy", ev_wt_free[b])
            for kcr in range(16):
                kc = s8 * 16 + kcr
                ins = Sy.dma_start(out=wtbuf[b][kcr],
                                   in_=n.wt_hbm[kc * 128:(kc + 1) * 128, :])
                ev_wt[b] = ctx.inc(ins, "dS", 16)
            hh = s8 // 2
            ev_y = []
            for gi in range(4):
                gg = (s8 % 2) * 4 + gi
                ev = ctx.inc(T.matmul(pzz[:, 0:512],
                                      ymr[:, (hh * 8 + gg) * P:
                                          (hh * 8 + gg + 1) * P],
                                      n2[gg], start=True, stop=True), "T")
                ctx.wait("V", ev)
                evy = ctx.inc(V.tensor_copy(yring[gi], pzz[:, 0:512]), "V")
                ctx.wait("T", evy)
                ev_y.append(evy)
            ctx.wait("T", ev_wt[b], *ev_y)
            ev_gelu = None
            for kcr in range(16):
                kc = s8 * 16 + kcr
                a4 = kc % 4
                tp = (96, 0) if a4 == 3 else None
                ctx.wait("T", ev_gelu)
                ins = T.matmul(pzz[:, (kcr % 2) * 512:(kcr % 2) * 512 + 512],
                               zl[32 * a4:32 * a4 + 32,
                                  (kc // 4) * 128:(kc // 4 + 1) * 128],
                               yring[kcr // 4][32 * a4:32 * a4 + 32, :],
                               start=True, stop=True, tile_position=tp)
                ev_z = ctx.inc(ins, "T")
                if kcr % 2 == 1:
                    ctx.wait("Sc", ev_z)
                    ev_gelu = ctx.inc(Sc.activation(
                        hsup[:, (kcr - 1) * 512:(kcr + 1) * 512],
                        pzz[:, 0:1024], AF.Gelu), "Sc")
                    ctx.wait("T", ev_gelu)
                    for kk in (kcr - 1, kcr):
                        for ot in range(3):
                            ins = T.matmul(
                                p512[ot][:, :],
                                wtbuf[b][kk][:, ot * 128:(ot + 1) * 128],
                                hsup[:, kk * 512:(kk + 1) * 512],
                                start=(kk == 0), stop=(kk == 15))
                    ctx.inc(ins, "T")
            ev_mm = ctx.ev("T")
            for ot in range(3):
                ctx.wait("V", ev_mm)
                if s8 == 0:
                    ins = V.tensor_copy(facc[ot], p512[ot][:, :])
                else:
                    ins = V.tensor_add(facc[ot], facc[ot], p512[ot][:, :])
            ev_fa = ctx.inc(ins, "V")
            ctx.wait("T", ev_fa)
            for pg, ots in ((1, (3, 4, 5)), (2, (6, 7))):
                for kcr in range(16):
                    for oi, ot in enumerate(ots):
                        ins = T.matmul(
                            p512[oi][:, :],
                            wtbuf[b][kcr][:, ot * 128:(ot + 1) * 128],
                            hsup[:, kcr * 512:(kcr + 1) * 512],
                            start=(kcr == 0), stop=(kcr == 15))
                ev_mm = ctx.inc(ins, "T")
                for oi, ot in enumerate(ots):
                    ctx.wait("V", ev_mm)
                    if s8 == 0:
                        ins = V.tensor_copy(facc[ot], p512[oi][:, :])
                    else:
                        ins = V.tensor_add(facc[ot], facc[ot], p512[oi][:, :])
                ev_fa = ctx.inc(ins, "V")
                ctx.wait("T", ev_fa)
            ev_wt_free[b] = ctx.ev("T")
        ctx.fence()

        if "aout" in n.dbg and q4 == 0:
            ctx.wait("G", ctx.ev("V"))
            for dt in range(NDT):
                ctx.inc(G.dma_start(
                    out=n.dbg["aout"][dt * P:(dt + 1) * P, 0:512],
                    in_=facc[dt]), "dG", 16)
            ctx.wait("T", ctx.ev("dG"))
            ctx.wait("V", ctx.ev("dG"))
            ctx.wait("Sc", ctx.ev("dG"))
        fbf = [RD.v(16384 + dt * 1024, BF16, P, 512) for dt in range(NDT)]
        sqf = RD.v(24576, BF16, P, 512)
        for dt in range(NDT):
            V.tensor_copy(fbf[dt], facc[dt])
            V.drain()
            V.tensor_mul(sqf, fbf[dt], fbf[dt])
            ev = ctx.incd("V")
            ctx.wait("T", ev)
            T.matmul(pzz[0:1, 0:512], n.vOnesColB, fbf[dt],
                     start=(dt == 0), stop=(dt == NDT - 1))
            ev = ctx.inc(T.matmul(pzz[0:1, 512:1024], n.vOnesColB, sqf,
                                  start=(dt == 0), stop=(dt == NDT - 1)), "T")
            ctx.wait("V", ev)
        lnscr2 = RD.v(25600, F32, 1, 1024)
        lncols2 = RD.v(29696, F32, P, 16)
        ev = emit_ln_rows(nc, ctx, n, pzz[0:1, 0:512], pzz[0:1, 512:1024],
                          lnscr2, lncols2, p512[0][:, :], p512[1][:, :])
        ctx.wait("V", ev)
        for dt in range(NDT):
            xsl = n.x0f[dt][:, q4 * 512:(q4 + 1) * 512]
            V.tensor_mul(facc[dt], facc[dt], p512[0][:, :])
            V.drain()
            V.tensor_add(facc[dt], facc[dt], p512[1][:, :])
            V.drain()
            V.tensor_add(xsl, xsl, facc[dt])
            ev = ctx.incd("V")
            ctx.wait("G", ev)
            ctx.inc(G.dma_start(out=n.out[dt * P:(dt + 1) * P,
                                         q4 * 512:(q4 + 1) * 512],
                                in_=xsl), "dG", 16)


def _mk_consts():
    c = {}
    c["cI128f"] = np.eye(P, dtype=np.float32)
    c["cI128b"] = np.eye(P, dtype=np.float32).astype(ml_dtypes.bfloat16)
    iota = np.arange(P, dtype=np.float32)
    c["cIotaNeg"] = np.tile(-iota[None, :], (P, 1))
    m0 = np.zeros((P, 8), dtype=np.float32)
    for p in range(P):
        m0[p, p // 16:] = 1.0
    c["cM0"] = m0.astype(ml_dtypes.bfloat16)
    sel = np.zeros((S, P), dtype=np.float32)
    for s in range(S):
        for d8 in range(8):
            sel[s, d8 * 16 + s] = 1.0
    c["cSelRep"] = sel
    selw = np.zeros((S, 8 * P), dtype=np.float32)
    for d8 in range(8):
        for s in range(S):
            selw[s, d8 * P + d8 * 16 + s] = 1.0
    c["cSelW"] = selw.astype(ml_dtypes.bfloat16)
    c["cOnes1"] = np.ones((1, P), dtype=np.float32)
    c["cOnes1b"] = np.ones((1, P), dtype=np.float32).astype(ml_dtypes.bfloat16)
    c["cNegD8"] = (-(np.arange(P) // 16).astype(np.float32))[:, None]
    c["cRevJ"] = (-(127.0 - np.arange(P, dtype=np.float32)))[:, None]
    c["cOnesColF"] = np.ones((P, 1), dtype=np.float32)
    c["cOnesColB"] = np.ones((P, 1), dtype=np.float32).astype(
        ml_dtypes.bfloat16)
    c["cEps"] = np.full((P, 1), EPS, dtype=np.float32)
    return c


def _host_inputs(x, logA, Bw, Cw, Dv, rm, ru, wcp):
    x = np.asarray(x)
    consts = _mk_consts()
    bwe = np.zeros((NDT, P, 18), dtype=np.float32)
    for dt in range(NDT):
        bwe[dt, :, 0:S] = np.asarray(Bw)[:, dt * P:(dt + 1) * P].T
        bwe[dt, :, 16] = 1.0
    ym = np.zeros((32, P, P), dtype=np.float32)
    rm = np.asarray(rm)
    for h in range(4):
        for g in range(8):
            for rr in range(8):
                ym[h * 8 + g, rr * 16:(rr + 1) * 16,
                   rr * 16:(rr + 1) * 16] = rm[h, g * 8 + rr]
    zl = np.zeros((P, 32 * P), dtype=np.float32)
    ru = np.asarray(ru)
    for t in range(128):
        h, r2 = t // 32, t % 32
        blk, a = t // 4, t % 4
        for rr2 in range(2):
            r_ = r2 * 2 + rr2
            zl[a * 32 + rr2 * 16: a * 32 + (rr2 + 1) * 16,
               blk * P + rr2 * 64: blk * P + (rr2 + 1) * 64] = ru[h, r_].T
    dv = np.asarray(Dv).reshape(NDT, P)

    maps = []
    for core in range(8):
        b, q = core // 4, core % 4
        t0 = q * NT
        xo = np.ascontiguousarray(x[b, t0:t0 + NT])
        pr = np.zeros((NPR, D), dtype=np.float32)
        lo = max(0, t0 - NPR)
        if t0 > 0:
            pr[NPR - (t0 - lo):] = x[b, lo:t0]
        maps.append(dict(
            xo=xo, xp=pr, logA=np.asarray(logA), CwIn=np.asarray(Cw),
            CwIn2=np.asarray(Cw), BwEIn=bwe, DvIn=dv, ymIn=ym, zlIn=zl,
            wc=np.asarray(wcp), **consts))
    return maps


def kernel(L0_attn_x, L0_attn_log_A_diag, L0_attn_B_w, L0_attn_C_w,
           L0_attn_D_vec, L0_ffn_rail_maker, L0_ffn_rail_up_proj,
           L0_ffn_w_cproj):
    maps = _host_inputs(L0_attn_x, L0_attn_log_A_diag, L0_attn_B_w,
                        L0_attn_C_w, L0_attn_D_vec, L0_ffn_rail_maker,
                        L0_ffn_rail_up_proj, L0_ffn_w_cproj)
    nc, _ = build_program()
    res = run_bass_kernel_spmd(nc, maps, core_ids=list(range(8)))
    full = np.zeros((2, 8192, D), dtype=np.float32)
    for core in range(8):
        b, q = core // 4, core % 4
        full[b, q * NT:(q + 1) * NT] = np.asarray(res.results[core]["out"]).T
    return full


# version 07
# speedup vs baseline: 1.4005x; 1.0686x over previous
import contextlib
import numpy as np
import ml_dtypes
import concourse.bass as bass
import concourse.mybir as mybir
from concourse.bass_utils import run_bass_kernel_spmd

F32 = mybir.dt.float32
BF16 = mybir.dt.bfloat16
U8 = mybir.dt.uint8
AF = mybir.ActivationFunctionType
ALU = mybir.AluOpType

P = 128
S = 16
D = 1024
NDT = 8
NT = 2048
NPR = 6144
C = 128
NCH_OWN = 16
NCH_ALL = 64
BT_PAD = 0
BT_COLS = NPR + NT
OWN0 = NPR
EPS = 1e-5
KF = 16384

DBG = set()
BUILD_PHASES = 99
ENTRY_SUB = 99
SKIP_TABLES = False
N_GROUPS = 4
TAPG = "own"
INTER_G = True
DBG_VTS = 0
DBG_VTD = 0
N1DT = 0
SKIP_INTRA = False
SKIP_DSKIP = False
SKIP_INTER = False


class Ctx:
    def __init__(self, nc, stack):
        self.nc = nc
        self.eng = {"T": nc.tensor, "V": nc.vector, "Sc": nc.scalar,
                    "G": nc.gpsimd, "Sy": nc.sync}
        self.sems = {}
        self.cnt = {}
        self.waited = {}
        for name in ("dS", "dG", "T", "V", "Sc", "G"):
            self.sems[name] = stack.enter_context(nc.semaphore(name))
            self.cnt[name] = 0

    def inc(self, ins, sem, amt=1):
        self.cnt[sem] += amt
        ins.then_inc(self.sems[sem], amt)
        return (sem, self.cnt[sem])

    def ev(self, sem):
        return (sem, self.cnt[sem])

    def incd(self, engname, sem=None):
        return self.inc(self.eng[engname].drain(), sem or engname)

    def wait(self, engname, *evs):
        for ev in evs:
            if ev is None:
                continue
            sem, val = ev
            key = (engname, sem)
            if self.waited.get(key, 0) >= val:
                continue
            self.eng[engname].wait_ge(self.sems[sem], val)
            self.waited[key] = val

    def fence(self):
        for en in ("V", "Sc", "G"):
            self.inc(self.eng[en].drain(), en)
        evs = [self.ev(s) for s in ("dS", "dG", "T", "V", "Sc", "G")]
        for en in ("T", "V", "Sc", "G", "Sy"):
            self.wait(en, *evs)


class Region:
    def __init__(self, nc, name, nbytes):
        self.t = nc.alloc_sbuf_tensor(name, [P, nbytes], U8)
        self.nbytes = nbytes
        self.name = name

    def v(self, off, dtype, rows, cols):
        esz = 2 if dtype == BF16 else 4
        assert off % 4 == 0 and off + cols * esz <= self.nbytes, \
            f"{self.name} off={off} cols={cols}"
        return self.t.ap()[0:rows, off:off + cols * esz].bitcast(dtype)


class NS:
    def __init__(self, **kw):
        self.__dict__.update(kw)


def build_program():
    nc = bass.Bass("TRN2", target_bir_lowering=False, debug=False)

    def din(name, shape, dt=F32):
        return nc.dram_tensor(name, list(shape), dt, kind="ExternalInput").ap()

    ins_d = dict(
        xo=din("xo", [NT, D]),
        xp=din("xp", [NPR, D]),
        logA=din("logA", [D, S]),
        CwIn=din("CwIn", [D, S]),
        CwIn2=din("CwIn2", [D, S]),
        BwEIn=din("BwEIn", [NDT, P, 18]),
        DvIn=din("DvIn", [NDT, P]),
        ymIn=din("ymIn", [32, P, P]),
        zlIn=din("zlIn", [P, 32 * P]),
        wc=din("wc", [D, KF]),
        cI128f=din("cI128f", [P, P]),
        cI128b=din("cI128b", [P, P], BF16),
        cIotaNeg=din("cIotaNeg", [P, P]),
        cM0=din("cM0", [P, 8], BF16),
        cSelRep=din("cSelRep", [S, P]),
        cSelW=din("cSelW", [S, 8 * P], BF16),
        cOnes1=din("cOnes1", [1, P]),
        cOnes1b=din("cOnes1b", [1, P], BF16),
        cNegD8=din("cNegD8", [P, 1]),
        cRevJ=din("cRevJ", [P, 1]),
        cOnesColF=din("cOnesColF", [P, 1]),
        cOnesColB=din("cOnesColB", [P, 1], BF16),
        cEps=din("cEps", [P, 1]),
    )
    out = nc.dram_tensor("out", [D, NT], F32, kind="ExternalOutput").ap()
    wt_hbm = nc.dram_tensor("wt_hbm", [KF, D], BF16, kind="Internal").ap()

    dbg_out = {}
    for name, shape, dt in [
        ("bt", [S, BT_COLS], BF16),
        ("x0f", [D, NT], F32),
        ("rows", [2, NT], BF16),
        ("est", [P, P * S], F32),
        ("aout", [D, NT], F32),
        ("x1", [D, NT], F32),
        ("hdbg", [P, 512], BF16),
        ("stat", [P, 64], F32),
        ("est2", [P, P * S], F32),
        ("sqa", [P, 8], F32),
        ("statp", [18, 512], F32),
        ("ddbg", [P, NDT * P + 2 * NDT], BF16),
        ("xpf", [D, 512], BF16),
    ]:
        if name in DBG:
            dbg_out[name] = nc.dram_tensor(
                "dbg_" + name, list(shape), dt, kind="ExternalOutput").ap()

    stack = contextlib.ExitStack()
    with stack:
        RX = Region(nc, "RX", 65536)
        RA = Region(nc, "RA", 32768)
        RB = Region(nc, "RB", 32768)
        RC = Region(nc, "RC", 32768)
        RD = Region(nc, "RD", 32768)

        sb = dict(
            e1T=nc.alloc_sbuf_tensor("e1T", [P, NDT * S], F32),
            a1c=nc.alloc_sbuf_tensor("a1c", [P, NDT * S], F32),
            
            c16=nc.alloc_sbuf_tensor("c16", [S, 1], F32),
            dcols=nc.alloc_sbuf_tensor("dcols", [P, NDT], F32),
            ddiag=nc.alloc_sbuf_tensor("ddiag", [P, NDT * P], BF16),
            consts=nc.alloc_sbuf_tensor("consts", [P, 520], F32),
            constsb=nc.alloc_sbuf_tensor("constsb", [P, 2600], U8),
            rows=nc.alloc_sbuf_tensor("rows", [1, 2 * NT], BF16),
            c16rep=nc.alloc_sbuf_tensor("c16rep", [P, S], F32),
            sqacc=nc.alloc_sbuf_tensor("sqacc", [P, 8], F32),
        )

        p512 = [stack.enter_context(nc.psum_tensor(f"p512_{i}", [P, 512], F32))
                for i in range(3)]
        pzz = stack.enter_context(nc.psum_tensor("pzz", [P, 1024], F32))
        pTb = stack.enter_context(nc.psum_tensor("pTb", [P, 2048], BF16))

        ctx = Ctx(nc, stack)
        block = stack.enter_context(nc.Block())

        n = NS(**ins_d, **sb, out=out, wt_hbm=wt_hbm, dbg=dbg_out,
               RX=RX, RA=RA, RB=RB, RC=RC, RD=RD,
               p512=p512, pzz=pzz, pTb=pTb)
        cs = n.consts.ap()
        cb = n.constsb.ap()
        n.vI128f = cs[:, 0:128]
        n.vIotaNeg = cs[:, 128:256]
        n.vSelRep = cs[0:S, 256:384]
        n.vOnes1 = cs[0:1, 384:512]
        n.vNegD8 = cs[:, 512:513]
        n.vRevJ = cs[:, 513:514]
        n.vOnesColF = cs[:, 514:515]
        n.vEps = cs[:, 515:516]
        n.vI128b = cb[:, 0:256].bitcast(BF16)
        n.vM0 = cb[:, 256:272].bitcast(BF16)
        n.vSelW = cb[0:S, 272:2320].bitcast(BF16)
        n.vOnes1b = cb[0:1, 2320:2576].bitcast(BF16)
        n.vOnesColB = cb[:, 2576:2578].bitcast(BF16)

        @block.sync
        def _(sy):
            emit_all(nc, ctx, n)

    return nc, dbg_out


def emit_all(nc, ctx, n):
    T, V, Sc, G, Sy = (ctx.eng[k] for k in ("T", "V", "Sc", "G", "Sy"))
    p512, pzz, pTb = n.p512, n.pzz, n.pTb
    RX, RA, RB, RC, RD = n.RX, n.RA, n.RB, n.RC, n.RD

    x0f = [RX.v(dt * 8192, F32, P, NT) for dt in range(NDT)]
    n.x0f = x0f
    n.cwT = RC.v(30720, F32, P, NDT * S)
    n.a128 = n.RA.t
    n.a128 = n.RA.v(27648, F32, P, NDT * S)
    n.bwe = n.RA.v(28160 + 32 if False else 28672, BF16, P, NDT * 18)

    RCdv = RC.v(28672, F32, 8, P)
    for ap_dst, src in [
        (n.vI128f, n.cI128f[:]), (n.vIotaNeg, n.cIotaNeg[:]),
        (n.vSelRep, n.cSelRep[:]), (n.vOnes1, n.cOnes1[:]),
        (n.vNegD8, n.cNegD8[:]), (n.vRevJ, n.cRevJ[:]),
        (n.vOnesColF, n.cOnesColF[:]), (n.vI128b, n.cI128b[:]),
        (n.vM0, n.cM0[:]), (n.vSelW, n.cSelW[:]),
        (n.vOnes1b, n.cOnes1b[:]), (n.vOnesColB, n.cOnesColB[:]),
        (n.vEps, n.cEps[:]),
        (n.e1T[:].rearrange("p (n s) -> p n s", n=NDT),
         n.logA.rearrange("(n p) s -> p n s", p=P)),
        (n.a1c[:].rearrange("p (n s) -> p n s", n=NDT),
         n.CwIn.rearrange("(n p) s -> p n s", p=P)),
        (n.cwT.rearrange("p (n s) -> p n s", n=NDT),
         n.CwIn2.rearrange("(n p) s -> p n s", p=P)),
        (RCdv, n.DvIn[:]),
    ]:
        ctx.inc(Sy.dma_start(out=ap_dst, in_=src), "dS", 16)
    ctx.inc(G.dma_start(out=n.bwe.rearrange("p (n c) -> p n c", n=NDT),
                        in_=n.BwEIn.rearrange("n p c -> p n c")), "dG", 16)
    ctx.fence()

    if not SKIP_TABLES:
        ctx.inc(Sc.activation(n.e1T[:], n.e1T[:], AF.Exp), "Sc")
        ctx.inc(Sc.activation(n.a128, n.e1T[:], AF.Exp, scale=-128.0), "Sc")
        sc_a1 = RC.v(24576, F32, P, NDT * S)
        ev = ctx.inc(Sc.activation(sc_a1, n.e1T[:], AF.Exp, scale=-1.0), "Sc")
        ctx.wait("V", ev)
        ctx.inc(V.tensor_mul(n.a1c[:], n.a1c[:], sc_a1), "V")

        ev = ctx.inc(T.transpose(pzz[0:P, 0:8], RCdv[0:8, 0:P],
                                 n.vI128f[0:8, 0:8]), "T")
        ctx.wait("V", ev)
        V.tensor_copy(n.dcols[:], pzz[:, 0:8])
        ctx.incd("V")
        for dt in range(NDT):
            ctx.inc(V.tensor_scalar_mul(n.ddiag[:, dt * P:(dt + 1) * P],
                                        n.vI128b, n.dcols[:, dt:dt + 1]), "V")

        for dt in range(NDT):
            ins = T.matmul(p512[2][0:18, 0:1], n.bwe[:, dt * 18:(dt + 1) * 18],
                           n.vOnesColB, start=(dt == 0), stop=(dt == NDT - 1))
        ev = ctx.inc(ins, "T")
        ctx.wait("V", ev)
        ev = ctx.inc(V.tensor_copy(n.c16[:], p512[2][0:S, 0:1]), "V")
        ctx.wait("T", ev)
        ev = ctx.inc(T.transpose(pzz[0:1, 8:24], n.c16[:], n.vI128f[0:S, 0:S]),
                     "T")
        ctx.wait("V", ev)
        c16row = RC.v(29696, F32, 1, S)
        ev = ctx.inc(V.tensor_copy(c16row, pzz[0:1, 8:24]), "V")
        ctx.wait("T", ev)
        ev = ctx.inc(T.matmul(pzz[0:P, 32:48], n.vOnes1, c16row,
                              start=True, stop=True), "T")
        ctx.wait("V", ev)
        ctx.inc(V.tensor_copy(n.c16rep[:], pzz[:, 32:48]), "V")
        ctx.fence()

        e1sm = RC.v(16384, F32, S, D)
        cwsm = RC.v(20480, F32, S, D)
        for src, dst in [(n.e1T[:], e1sm), (n.cwT, cwsm)]:
            for dt in range(NDT):
                ins = T.transpose(pzz[0:S, dt * P:(dt + 1) * P],
                                  src[:, dt * S:(dt + 1) * S], n.vI128f)
            ev = ctx.inc(ins, "T")
            ctx.wait("V", ev)
            ev = ctx.inc(V.tensor_copy(dst, pzz[0:S, :]), "V")
            ctx.wait("T", ev)

        e1rep = RC.v(0, F32, P, D)
        crep = RC.v(4096, F32, P, D)
        for src, dst in [(e1sm, e1rep), (cwsm, crep)]:
            for half in range(2):
                ins = T.matmul(pzz[:, half * 512:(half + 1) * 512], n.vSelRep,
                               src[0:S, half * 512:(half + 1) * 512],
                               start=True, stop=True)
            ev = ctx.inc(ins, "T")
            ctx.wait("V", ev)
            ev = ctx.inc(V.tensor_copy(dst, pzz[:, :]), "V")
            ctx.wait("T", ev)

        a8 = RC.v(8192, F32, P, D)
        tg = RC.v(12288, F32, P, D)
        ctx.wait("Sc", ctx.ev("V"))
        ev = ctx.inc(Sc.activation(a8, e1rep, AF.Exp, scale=-8.0), "Sc")
        ev2 = ctx.inc(V.tensor_scalar_mul(tg, e1rep, n.vNegD8), "V")
        ctx.wait("Sc", ev2)
        ev3 = ctx.inc(Sc.activation(tg, tg, AF.Exp), "Sc")
        ctx.wait("V", ev, ev3)
        W2 = [RB.v(g * 2048, BF16, P, D) for g in range(16)]
        n.W2 = W2
        for g in range(16):
            ctx.inc(V.tensor_mul(W2[g], tg, crep), "V")
            if g < 15:
                ctx.inc(V.tensor_mul(tg, tg, a8), "V")
        ctx.fence()

    if BUILD_PHASES <= 1:
        finish(nc, ctx, n, partial=True)
        return
    Btbf = RA.v(0, BF16, S, BT_COLS)
    n.Btbf = Btbf
    xbf = [RC.v(dt * 4096, BF16, P, NT) for dt in range(NDT)]
    n.xbf = xbf
    stg = [[RD.v((b * 4 + j) * 4096, F32, P, D) for j in range(4)]
           for b in range(2)]
    xorr = n.xo.rearrange("(n p) d -> n p d", p=P)
    for grp in range(N_GROUPS):
        b = grp % 2
        for j in range(4):
            ev_ld = ctx.inc(Sy.dma_start(out=stg[b][j],
                                         in_=xorr[grp * 4 + j]), "dS", 16)
        ctx.wait("T", ev_ld)
        for dt in range(NDT):
            if ENTRY_SUB == -4:
                break
            ph = (dt % 2) * 512
            for j in range(4):
                ins = T.transpose(pzz[:, ph + j * P:ph + (j + 1) * P],
                                  stg[b][j][:, dt * P:(dt + 1) * P], n.vI128f)
            ev_tr = ctx.inc(ins, "T")
            if ENTRY_SUB == -3:
                continue
            ctx.wait("Sc", ev_tr)
            ev_a = ctx.inc(Sc.copy(x0f[dt][:, grp * 512:(grp + 1) * 512],
                                   pzz[:, ph:ph + 512]), "Sc")
            ctx.wait("T", ev_a)
            if ENTRY_SUB == -2:
                continue
            ctx.wait("V", ev_tr, ev_a)
            ev_b = ctx.inc(V.tensor_copy(xbf[dt][:, grp * 512:(grp + 1) * 512],
                                         pzz[:, ph:ph + 512]), "V")
            ctx.wait("T", ev_a, ev_b)
        if ENTRY_SUB >= 2 and ENTRY_SUB > 0:
            for j in range(4):
                ctx.inc(Sc.activation(stg[b][j], stg[b][j], AF.Square,
                                      accum_out=n.sqacc[:, j:j + 1]), "Sc")
        if ENTRY_SUB >= 3 and ENTRY_SUB > 0:
            for dt in range(NDT):
                ins = T.matmul(p512[2][0:18, :],
                               n.bwe[:, dt * 18:(dt + 1) * 18],
                               xbf[dt][:, grp * 512:(grp + 1) * 512],
                               start=(dt == 0), stop=(dt == NDT - 1))
            ev_bx = ctx.inc(ins, "T")
        if ENTRY_SUB >= 4 and ENTRY_SUB > 0:
            emit_rows_and_bt(
                nc, ctx, n, ev_bx, stg[b][0],
                bt_dst=Btbf[0:S, OWN0 + grp * 512:OWN0 + (grp + 1) * 512],
                rows_off=grp * 512)
        ctx.fence()

    if BUILD_PHASES <= 2:
        finish(nc, ctx, n, partial=True)
        return
    pstg = [[RD.v((b * 4 + j) * 2048, BF16, P, D) for j in range(4)]
            for b in range(2)]
    xpfT = [RD.v(16384 + dt * 1024, BF16, P, 512) for dt in range(NDT)]
    pscr = RD.v(24576, F32, P, 1024)
    xprr = n.xp.rearrange("(n p) d -> n p d", p=P)
    for grp in range(12):
        b = grp % 2
        for j in range(4):
            ev_ld = ctx.inc(G.dma_start(out=pstg[b][j],
                                        in_=xprr[grp * 4 + j]), "dG", 16)
        ctx.wait("T", ev_ld)
        for dt in range(NDT):
            ph = (dt % 2) * 1024
            for j in range(4):
                ins = T.transpose(pTb[:, ph + j * P:ph + (j + 1) * P],
                                  pstg[b][j][:, dt * P:(dt + 1) * P],
                                  n.vI128b)
            ev_tr = ctx.inc(ins, "T")
            ctx.wait("V", ev_tr)
            ev_b = ctx.inc(V.tensor_copy(xpfT[dt], pTb[:, ph:ph + 512]), "V")
            ctx.wait("T", ev_b)
        ctx.wait("Sc", ctx.ev("T"))
        for j in range(4):
            ctx.inc(Sc.activation(pstg[b][j], pstg[b][j], AF.Square,
                                  accum_out=n.sqacc[:, j:j + 1]), "Sc")
        for dt in range(NDT):
            ins = T.matmul(p512[2][0:18, :], n.bwe[:, dt * 18:(dt + 1) * 18],
                           xpfT[dt], start=(dt == 0), stop=(dt == NDT - 1))
        ev_bx = ctx.inc(ins, "T")
        if "xpf" in n.dbg and grp == 11:
            ctx.wait("G", ctx.ev("V"))
            for dt in range(NDT):
                ctx.inc(G.dma_start(out=n.dbg["xpf"][dt * P:(dt + 1) * P, :],
                                    in_=xpfT[dt]), "dG", 16)
        emit_rows_and_bt(
            nc, ctx, n, ev_bx, pscr,
            bt_dst=Btbf[0:S, BT_PAD + grp * 512:BT_PAD + (grp + 1) * 512],
            rows_off=None, tap=(grp == 11 and TAPG == "pre"))
        ctx.fence()
    if BT_PAD:
        ctx.inc(V.memset(Btbf[0:S, 0:BT_PAD], 0.0), "V")
    ctx.fence()

    if "bt" in n.dbg:
        ctx.inc(G.dma_start(out=n.dbg["bt"][:], in_=Btbf), "dG", 16)
    if "x0f" in n.dbg:
        for dt in range(NDT):
            ctx.inc(G.dma_start(out=n.dbg["x0f"][dt * P:(dt + 1) * P, :],
                                in_=x0f[dt]), "dG", 16)
    if "rows" in n.dbg:
        ctx.inc(G.dma_start(out=n.dbg["rows"][:], in_=n.rows[:]), "dG", 16)
    if BUILD_PHASES <= 3:
        finish(nc, ctx, n, partial=True)
        return

    emit_scan_phase(nc, ctx, n)
    if BUILD_PHASES <= 4:
        finish(nc, ctx, n, partial=True)
        return
    emit_wprep_phase(nc, ctx, n)
    if BUILD_PHASES <= 5:
        finish(nc, ctx, n, partial=True)
        return
    emit_ssm_main(nc, ctx, n)
    if BUILD_PHASES <= 6:
        finish(nc, ctx, n, partial=True)
        return
    emit_ffn(nc, ctx, n)
    finish(nc, ctx, n, partial=False)


def finish(nc, ctx, n, partial):
    G = ctx.eng["G"]
    ctx.fence()
    if partial:
        ev = ctx.inc(G.dma_start(out=n.out[0:P, :], in_=n.x0f[0]), "dG", 16)
        ctx.wait("G", ev)
    ctx.wait("G", ctx.ev("dG"), ctx.ev("dS"))


def emit_rows_and_bt(nc, ctx, n, ev_bx, scr, bt_dst, rows_off, tap=False):
    V, Sc, T = ctx.eng["V"], ctx.eng["Sc"], ctx.eng["T"]
    p512, pzz = n.p512, n.pzz
    scr18 = scr[0:18, 0:512]
    ctx.wait("V", ev_bx, ctx.ev("Sc"))
    ev = ctx.inc(V.tensor_copy(scr18, p512[2][0:18, :]), "V")
    ctx.wait("T", ev)
    for j in range(4):
        ins = T.transpose(pzz[0:P, j * 32:j * 32 + 18],
                          scr18[0:18, j * 128:(j + 1) * 128],
                          n.vI128f[0:18, 0:18])
    ev_cols = ctx.inc(ins, "T")
    ctx.wait("V", ev_cols)
    pzz3 = pzz.rearrange("p (j c) -> p j c", c=32)
    sumc4 = pzz3[:, 0:4, 16:17].squeeze(2)
    Bxc = pzz[:, 0:128]
    mc4 = scr[:, 512:516]
    vc4 = scr[:, 516:520]
    rc4 = scr[:, 520:524]
    mrc4 = scr[:, 524:528]
    mm24 = scr[:, 528:532]
    V.tensor_scalar_mul(mc4, sumc4, 1.0 / D)
    V.tensor_scalar_mul(vc4, n.sqacc[:, 0:4], 1.0 / D)
    V.drain()
    V.tensor_copy(mm24, mc4)
    V.drain()
    V.tensor_mul(mm24, mm24, mc4)
    V.drain()
    V.tensor_sub(vc4, vc4, mm24)
    ev = ctx.incd("V")
    ctx.wait("Sc", ev)
    Sc.activation(rc4, vc4, AF.Ln, bias=n.vEps)
    Sc.drain()
    Sc.activation(rc4, rc4, AF.Exp, scale=-0.5)
    ev = ctx.incd("Sc")
    ctx.wait("V", ev)
    V.tensor_mul(mrc4, mc4, rc4)
    V.drain()
    ev_tr = None
    for j in range(4):
        t1 = scr[:, 576 + j * 32:576 + j * 32 + 16]
        t2 = scr[:, 576 + j * 32 + 16:576 + j * 32 + 32]
        V.tensor_scalar_mul(t1, pzz[:, j * 32:j * 32 + 16],
                            rc4[:, j:j + 1])
        V.tensor_scalar_mul(t2, n.c16rep[:], mrc4[:, j:j + 1])
        V.drain()
        V.tensor_sub(t1, t1, t2)
        ev = ctx.incd("V")
        ctx.wait("T", ev)
        ev_tr = ctx.inc(T.transpose(
            pzz[0:S, 256 + j * 128:256 + (j + 1) * 128], t1, n.vI128f), "T")
        if rows_off is not None:
            T.transpose(pzz[0:1, 768:896], mrc4[:, j:j + 1], n.vI128f)
            ev2 = ctx.inc(T.transpose(pzz[0:1, 896:1024], rc4[:, j:j + 1],
                                      n.vI128f), "T")
            ctx.wait("V", ev2)
            rb = n.rows.ap()
            V.tensor_copy(rb[0:1, rows_off + j * 128:rows_off + (j + 1) * 128],
                          pzz[0:1, 768:896])
            V.tensor_copy(
                rb[0:1, NT + rows_off + j * 128:NT + rows_off + (j + 1) * 128],
                pzz[0:1, 896:1024])
            ev3 = ctx.incd("V")
            ctx.wait("T", ev3)
    ctx.wait("V", ev_tr)
    ctx.inc(V.tensor_copy(bt_dst, pzz[0:S, 256:768]), "V")

def emit_ln_rows(nc, ctx, n, sum_ap, sq_ap, scr, scrcols, rrep_ps, qrep_ps,
                 ntok=512):
    V, Sc, T = ctx.eng["V"], ctx.eng["Sc"], ctx.eng["T"]
    pzz = n.pzz
    nj = ntok // 128
    V.tensor_copy(scr[0:1, 0:ntok], sum_ap)
    V.tensor_copy(scr[0:1, ntok:2 * ntok], sq_ap)
    ev = ctx.incd("V")
    ctx.wait("T", ev)
    for j in range(nj):
        T.transpose(pzz[0:P, j * 4:j * 4 + 1],
                    scr[0:1, j * 128:(j + 1) * 128], n.vI128f[0:1, 0:1])
        ins = T.transpose(pzz[0:P, j * 4 + 1:j * 4 + 2],
                          scr[0:1, ntok + j * 128:ntok + (j + 1) * 128],
                          n.vI128f[0:1, 0:1])
    ev = ctx.inc(ins, "T")
    ctx.wait("V", ev)
    pz3 = pzz.rearrange("p (j c) -> p j c", c=4)
    sum4 = pz3[:, 0:nj, 0:1].squeeze(2)
    sq4 = pz3[:, 0:nj, 1:2].squeeze(2)
    mc4 = scrcols[:, 0:nj]
    vc4 = scrcols[:, 4:4 + nj]
    rc4 = scrcols[:, 8:8 + nj]
    qc4 = scrcols[:, 12:12 + nj]
    V.tensor_scalar_mul(mc4, sum4, 1.0 / D)
    V.tensor_scalar_mul(vc4, sq4, 1.0 / D)
    V.drain()
    V.tensor_copy(qc4, mc4)
    V.drain()
    V.tensor_mul(qc4, qc4, mc4)
    V.drain()
    V.tensor_sub(vc4, vc4, qc4)
    ev = ctx.incd("V")
    ctx.wait("Sc", ev)
    Sc.activation(rc4, vc4, AF.Ln, bias=n.vEps)
    Sc.drain()
    Sc.activation(rc4, rc4, AF.Exp, scale=-0.5)
    ev = ctx.incd("Sc")
    ctx.wait("V", ev)
    V.tensor_mul(qc4, qc4 if False else mc4, rc4)
    V.drain()
    V.tensor_scalar(qc4, qc4, -1.0, None, ALU.mult)
    ev = ctx.incd("V")
    ctx.wait("T", ev)
    for j in range(nj):
        T.transpose(pzz[0:1, 64:192], rc4[:, j:j + 1], n.vI128f)
        ev = ctx.inc(T.transpose(pzz[0:1, 192:320], qc4[:, j:j + 1],
                                 n.vI128f), "T")
        ctx.wait("V", ev)
        V.tensor_copy(scr[0:1, j * 128:(j + 1) * 128], pzz[0:1, 64:192])
        V.tensor_copy(scr[0:1, ntok + j * 128:ntok + (j + 1) * 128],
                      pzz[0:1, 192:320])
        ev = ctx.incd("V")
        ctx.wait("T", ev)
    for j in range(nj):
        T.matmul(rrep_ps[:, j * 128:(j + 1) * 128], n.vOnes1,
                 scr[0:1, j * 128:(j + 1) * 128], start=True, stop=True)
        ins = T.matmul(qrep_ps[:, j * 128:(j + 1) * 128], n.vOnes1,
                       scr[0:1, ntok + j * 128:ntok + (j + 1) * 128],
                       start=True, stop=True)
    return ctx.inc(ins, "T")

def emit_scan_phase(nc, ctx, n):
    T, V, Sc, G, Sy = (ctx.eng[k] for k in ("T", "V", "Sc", "G", "Sy"))
    p512, pzz, pTb = n.p512, n.pzz, n.pTb
    RA, RC, RD = n.RA, n.RC, n.RD

    Rtab = [RC.v(i * 256, BF16, P, P) for i in range(128)]
    n.Rtab = Rtab
    tmpf = [RD.v(i * 512, F32, P, P) for i in range(8)]
    evp = [None, None]
    for q in range(32):
        b = q % 2
        ctx.wait("V", evp[b])
        for u in range(4):
            V.tensor_scalar_mul(tmpf[b * 4 + u], n.vIotaNeg,
                                n.e1T[:, q * 4 + u:q * 4 + u + 1])
        ev = ctx.incd("V")
        ctx.wait("Sc", ev)
        for u in range(4):
            ins = Sc.activation(Rtab[q * 4 + u], tmpf[b * 4 + u], AF.Exp)
        evp[b] = ctx.inc(ins, "Sc")
    ctx.fence()

    Vtab = [RD.v(s * 2048, BF16, P, D) for s in range(16)]
    e1piece = RA.v(24576, F32, 1, P)
    tmpV = RA.v(16384, F32, P, D)
    for s in range(16):
        for dt in range(NDT):
            ev = ctx.inc(T.transpose(
                pzz[0:1, 0:128],
                n.e1T[:, dt * S + s:dt * S + s + 1], n.vI128f), "T")
            ctx.wait("V", ev)
            V.tensor_copy(e1piece, pzz[0:1, 0:128])
            ev = ctx.incd("V")
            ctx.wait("T", ev)
            ins = T.matmul(p512[dt // 4][:, (dt % 4) * 128:(dt % 4 + 1) * 128],
                           n.vOnes1, e1piece, start=True, stop=True)
            ev = ctx.inc(ins, "T")
        ctx.wait("V", ev)
        V.tensor_scalar_mul(tmpV[:, 0:512], p512[0][:, :], n.vRevJ)
        ev = ctx.inc(V.tensor_scalar_mul(tmpV[:, 512:1024], p512[1][:, :],
                                         n.vRevJ), "V")
        ctx.wait("Sc", ev)
        ev = ctx.inc(Sc.activation(Vtab[s], tmpV, AF.Exp), "Sc")
        ctx.wait("T", ev)
        ctx.wait("V", ev)
    ctx.fence()

    BtT = RA.v(24576, BF16, P, NCH_ALL * S)
    n.BtT = BtT
    evb = {}
    for batch in range(8):
        ph = (batch % 2) * 1024
        if batch >= 2:
            ctx.wait("T", evb[batch % 2])
        for cc in range(8):
            c = batch * 8 + cc
            ins = T.transpose(
                pTb[:, ph + cc * S:ph + (cc + 1) * S],
                n.Btbf[0:S, BT_PAD + c * 128:BT_PAD + (c + 1) * 128],
                n.vI128b[0:S, 0:S])
        ev = ctx.inc(ins, "T")
        ctx.wait("V", ev)
        evb[batch % 2] = ctx.inc(
            V.tensor_copy(BtT[:, batch * 128:(batch + 1) * 128],
                          pTb[:, ph:ph + 128]), "V")
    ctx.fence()

    Estore = RA.v(16384, F32, P, 128 * S)
    n.Estore = Estore
    vseg = RA.v(26624, F32, P, 64)
    a128rep = RA.v(26880, F32, P, 64)
    hscan = RA.v(27136, F32, P, 64)
    btt3 = BtT.rearrange("p (c u) -> p c u", u=S)
    evseg = {0: None, 1: None, 2: None}
    for dt in range(NDT):
        for s in range(S):
            seg = dt * S + s
            b = seg % 3
            pb = p512[b]
            ctx.wait("T", evseg[b])
            rhs = btt3[:, :, s:s + 1].squeeze(2)
            ev = ctx.inc(T.matmul(pb[:, 0:64],
                                  Vtab[s][:, dt * 128:(dt + 1) * 128],
                                  rhs, start=True, stop=True), "T")
            ctx.wait("V", ev)
            V.tensor_tensor_scan(
                hscan, n.a128[:, seg:seg + 1].to_broadcast((P, 64)),
                pb[:, 0:64], 0.0, ALU.mult, ALU.add)
            V.tensor_scalar_mul(Estore[:, seg * S:(seg + 1) * S],
                                hscan[:, 47:63], n.a1c[:, seg:seg + 1])
            evseg[b] = ctx.incd("V")
    ctx.fence()
    if "ddbg" in n.dbg:
        ctx.inc(ctx.eng["G"].dma_start(out=n.dbg["ddbg"][:, 0:NDT * P],
                                       in_=n.ddiag[:]), "dG", 16)
        ctx.inc(ctx.eng["G"].dma_start(
            out=n.dbg["ddbg"][:, NDT * P:NDT * P + NDT].bitcast(F32)
            if False else n.dbg["ddbg"][:, NDT * P:NDT * P + 2 * NDT],
            in_=n.dcols[:].bitcast(BF16)), "dG", 16)
        ctx.fence()
    if "hdbg" in n.dbg:
        for i in range(4):
            ctx.inc(ctx.eng["G"].dma_start(
                out=n.dbg["hdbg"][:, i * 128:(i + 1) * 128],
                in_=Vtab[DBG_VTS + i][:, DBG_VTD * 128:(DBG_VTD + 1) * 128]),
                "dG", 16)
        ctx.fence()
    if "est" in n.dbg:
        ctx.inc(G.dma_start(out=n.dbg["est"][:], in_=Estore), "dG", 16)
        ctx.fence()


def emit_wprep_phase(nc, ctx, n):
    T, V, Sc, G, Sy = (ctx.eng[k] for k in ("T", "V", "Sc", "G", "Sy"))
    pTb = n.pTb
    RD = n.RD
    wn = [RD.v(i * 8192, BF16, P, 4096) for i in range(2)]
    wto = [RD.v(16384 + i * 2048, BF16, P, 1024) for i in range(2)]
    wcr = n.wc.rearrange("(ot p) k -> p ot k", p=P)
    ev_tr4 = {0: None, 1: None}
    ev_st = {0: None, 1: None}
    ev_cp = {0: None, 1: None}
    for kb in range(32):
        b4 = kb % 2
        ctx.wait("G", ev_tr4[b4])
        ev_ld = ctx.inc(G.dma_start(
            out=wn[b4].rearrange("p (ot k) -> p ot k", ot=NDT),
            in_=wcr[:, :, kb * 512:(kb + 1) * 512]), "dG", 16)
        ctx.wait("T", ev_ld)
        for sub in range(4):
            kc = kb * 4 + sub
            b = kc % 2
            ph = b * 1024
            ctx.wait("T", ev_cp[b])
            for ot in range(NDT):
                ins = T.transpose(pTb[:, ph + ot * 128:ph + (ot + 1) * 128],
                                  wn[b4][:, ot * 512 + sub * 128:
                                         ot * 512 + (sub + 1) * 128],
                                  n.vI128b)
            ev_tr = ctx.inc(ins, "T")
            if sub == 3:
                ev_tr4[b4] = ev_tr
            ctx.wait("V", ev_tr, ev_st[b])
            ev = ctx.inc(V.tensor_copy(wto[b], pTb[:, ph:ph + 1024]), "V")
            ev_cp[b] = ev
            ctx.wait("Sy", ev)
            ev_st[b] = ctx.inc(Sy.dma_start(
                out=n.wt_hbm[kc * 128:(kc + 1) * 128, :], in_=wto[b]),
                "dS", 16)
    ctx.fence()


def emit_ssm_main(nc, ctx, n):
    T, V, Sc, G, Sy = (ctx.eng[k] for k in ("T", "V", "Sc", "G", "Sy"))
    p512, pzz, pTb = n.p512, n.pzz, n.pTb
    RA, RD = n.RA, n.RD

    Bwide = RA.v(27648, BF16, P, 2176)
    for blk5 in range(5):
        width = 512 if blk5 < 4 else 128
        for d8 in range(8):
            src = n.Btbf[0:S,
                         OWN0 - 128 + blk5 * 512 - d8:
                         OWN0 - 128 + blk5 * 512 - d8 + width]
            ins = T.matmul(pzz[:, 0:width], n.vSelW[:, d8 * 128:(d8 + 1) * 128],
                           src, start=(d8 == 0), stop=(d8 == 7))
        ev = ctx.inc(ins, "T")
        ctx.wait("V", ev)
        ev = ctx.inc(V.tensor_copy(Bwide[:, blk5 * 512:blk5 * 512 + width],
                                   pzz[:, 0:width]), "V")
        ctx.wait("T", ev)

    bwp = [RA.v(24576 + i * 1024, BF16, P, 512) for i in range(2)]
    ablock = [RD.v(dt * 2048, F32, P, 512) for dt in range(NDT)]
    abf = [RD.v(24576 + dt * 1024, BF16, P, 512) for dt in range(NDT)]
    n1bf = [RD.v(16384 + dt * 1024, BF16, P, 512) for dt in range(NDT)]
    sqscr = RA.v(26624, BF16, P, 512)
    rb = n.rows.ap()

    if "est2" in n.dbg:
        ctx.inc(G.dma_start(out=n.dbg["est2"][:], in_=n.Estore), "dG", 16)
        ctx.fence()
    for blk in range(4):
        ctx.fence()
        T.matmul(p512[0][:, :], n.vOnes1b,
                 rb[0:1, NT + blk * 512:NT + (blk + 1) * 512],
                 start=True, stop=True)
        ev = ctx.inc(T.matmul(p512[1][:, :], n.vOnes1b,
                              rb[0:1, blk * 512:(blk + 1) * 512],
                              start=True, stop=True), "T")
        ctx.wait("V", ev)
        for dt in range(NDT):
            xsl = n.x0f[dt][:, blk * 512:(blk + 1) * 512]
            V.tensor_mul(ablock[dt], xsl, p512[0][:, :])
            V.drain()
            V.tensor_sub(ablock[dt], ablock[dt], p512[1][:, :])
            V.drain()
            ins = V.tensor_copy(n1bf[dt], ablock[dt])
        ev_n1 = ctx.incd("V")
        if "stat" in n.dbg and blk == 0:
            ctx.wait("G", ev_n1)
            ctx.inc(G.dma_start(out=n.dbg["est2"][:, 0:512],
                                in_=ablock[0]), "dG", 16)
            V.tensor_copy(ablock[7], p512[1][:, :])
            ev_t = ctx.incd("V")
            ctx.wait("G", ev_t)
            ctx.inc(G.dma_start(out=n.dbg["est2"][:, 512:1024],
                                in_=ablock[7]), "dG", 16)
            ctx.wait("T", ctx.ev("dG"))
            ctx.wait("V", ctx.ev("dG"))
        ctx.wait("T", ev_n1)
        if "hdbg" in n.dbg and blk == 0:
            ctx.wait("G", ev_n1)
            ctx.inc(G.dma_start(out=n.dbg["hdbg"][:], in_=n1bf[N1DT]),
                    "dG", 16)
            ctx.wait("T", ctx.ev("dG"))
            ctx.wait("V", ctx.ev("dG"))

        for ps in range(2):
            dts = list(range(ps * 4, ps * 4 + 4))
            banks = [p512[0][:, :], p512[1][:, :], p512[2][:, :],
                     pzz[:, 0:512]]
            started = [False] * 4
            evbw = [None, None]
            for g in range(16 if not SKIP_INTRA else 0):
                gb = g % 2
                ctx.wait("V", evbw[gb])
                V.tensor_copy(bwp[gb], Bwide[:, 128 + blk * 512 - 8 * g:
                                             128 + blk * 512 - 8 * g + 512])
                V.drain()
                for cc in range(4):
                    if g > 0:
                        V.memset(bwp[gb][:, cc * 128:cc * 128 + 8 * g], 0.0)
                    mslice = bwp[gb][:, cc * 128 + 8 * g:cc * 128 + 8 * g + 8]
                    V.tensor_mul(mslice, mslice, n.vM0)
                ev = ctx.incd("V")
                ctx.wait("T", ev)
                for i, dt in enumerate(dts):
                    ins = T.matmul(banks[i], n.W2[g][:, dt * 128:(dt + 1) * 128],
                                   bwp[gb], start=(not started[i]), stop=False)
                    started[i] = True
                evbw[gb] = ctx.inc(ins, "T")
            ev_dskip = None
            if not SKIP_DSKIP:
                for i, dt in enumerate(dts):
                    ins = T.matmul(banks[i], n.ddiag[:, dt * P:(dt + 1) * P],
                                   n1bf[dt], start=(not started[i]), stop=False)
                    started[i] = True
                ev_dskip = ctx.inc(ins, "T")
            ring = RD.v(16384, BF16, P, 512)
            if SKIP_INTER:
                ctx.wait("V", ev_dskip)
                V.memset(ring[:, 0:128], 0.0)
                ev = ctx.incd("V")
                ctx.wait("T", ev)
            ev_q = ev_dskip
            for i, dt in enumerate(dts):
                if SKIP_INTER:
                    ins = T.matmul(banks[i][:, 0:128], n.vI128b,
                                   ring[:, 0:128],
                                   start=(not started[i]), stop=True)
                    ctx.inc(ins, "T")
                    started[i] = True
                    continue
                for cc in range(4):
                    cown = blk * 4 + cc
                    for sq in range(4):
                        ctx.wait("V", ev_q)
                        for u in range(4):
                            s = sq * 4 + u
                            seg = dt * S + s
                            V.tensor_scalar_mul(
                                ring[:, u * 128:(u + 1) * 128],
                                n.Rtab[seg],
                                n.Estore[:, seg * S + cown:
                                          seg * S + cown + 1])
                        ev = ctx.incd("V")
                        ctx.wait("T", ev)
                        for u in range(4):
                            last = (cc == 3 and sq == 3 and u == 3)
                            ins = T.matmul(
                                banks[i][:, cc * 128:(cc + 1) * 128],
                                n.vI128b, ring[:, u * 128:(u + 1) * 128],
                                start=(not started[i]), stop=last)
                            started[i] = True
                        ev_q = ctx.inc(ins, "T")
            ev_mm = ctx.ev("T")
            for i, dt in enumerate(dts):
                ctx.wait("Sc", ev_mm)
                ev_a = ctx.inc(Sc.copy(ablock[dt], banks[i]), "Sc")
                ctx.wait("V", ev_a)
                ctx.inc(V.tensor_copy(abf[dt], banks[i]), "V")
            ctx.fence()

        if "aout" in n.dbg and blk == 0:
            ctx.wait("G", ctx.ev("Sc"), ctx.ev("V"))
            for dt in range(NDT):
                ctx.inc(G.dma_start(
                    out=n.dbg["aout"][dt * P:(dt + 1) * P, 0:512],
                    in_=ablock[dt]), "dG", 16)
        for dt in range(NDT):
            V.tensor_mul(sqscr, abf[dt], abf[dt])
            ev = ctx.incd("V")
            ctx.wait("T", ev)
            T.matmul(pzz[0:1, 0:512], n.vOnesColB, abf[dt],
                     start=(dt == 0), stop=(dt == NDT - 1))
            ev = ctx.inc(T.matmul(pzz[0:1, 512:1024], n.vOnesColB, sqscr,
                                  start=(dt == 0), stop=(dt == NDT - 1)), "T")
            ctx.wait("V", ev)
        lnscr = RD.v(24576, F32, 1, 1024)
        lncols = RD.v(28672, F32, P, 16)
        ev = emit_ln_rows(nc, ctx, n, pzz[0:1, 0:512], pzz[0:1, 512:1024],
                          lnscr, lncols, p512[0][:, :], p512[1][:, :])
        ctx.wait("V", ev)
        for dt in range(NDT):
            xsl = n.x0f[dt][:, blk * 512:(blk + 1) * 512]
            V.tensor_mul(ablock[dt], ablock[dt], p512[0][:, :])
            V.drain()
            V.tensor_add(ablock[dt], ablock[dt], p512[1][:, :])
            V.drain()
            ins = V.tensor_add(xsl, xsl, ablock[dt])
        ctx.inc(ins, "V")
        ctx.fence()
    if "x1" in n.dbg:
        for dt in range(NDT):
            ctx.inc(G.dma_start(out=n.dbg["x1"][dt * P:(dt + 1) * P, :],
                                in_=n.x0f[dt]), "dG", 16)
        ctx.fence()


def emit_ffn(nc, ctx, n):
    T, V, Sc, G, Sy = (ctx.eng[k] for k in ("T", "V", "Sc", "G", "Sy"))
    p512, pzz = n.p512, n.pzz
    RA, RB, RC, RD = n.RA, n.RB, n.RC, n.RD

    zl = RA.v(0, BF16, P, 4096)
    ymr = RA.v(8192, BF16, P, 32 * P)
    n2 = [RA.v(16384 + dt * 1024, BF16, P, 512) for dt in range(NDT)]
    yring = [RA.v(24576 + i * 1024, BF16, P, 512) for i in range(6)]
    facc = [RD.v(dt * 2048, F32, P, 512) for dt in range(NDT)]
    hsup = RD.v(16384, BF16, P, 16 * 512)
    x1bfs = RD.v(0, BF16, P, 512)
    sqs = RD.v(1024, BF16, P, 512)

    ctx.inc(G.dma_start(out=zl, in_=n.zlIn[:]), "dG", 16)
    ctx.inc(G.dma_start(out=ymr.rearrange("p (t c) -> p t c", t=32),
                        in_=n.ymIn.rearrange("t p c -> p t c")), "dG", 16)
    ctx.fence()

    wtbuf = [[RB.v(i * 2048, BF16, P, 1024) for i in range(16)],
             [RC.v(i * 2048, BF16, P, 1024) for i in range(16)]]
    ev_wt = [None, None]
    ev_wt_free = [None, None]

    for q4 in range(4):
        ctx.fence()
        for dt in range(NDT):
            xsl = n.x0f[dt][:, q4 * 512:(q4 + 1) * 512]
            V.tensor_copy(x1bfs, xsl)
            V.drain()
            V.tensor_mul(sqs, x1bfs, x1bfs)
            ev = ctx.incd("V")
            ctx.wait("T", ev)
            T.matmul(pzz[0:1, 0:512], n.vOnesColB, x1bfs,
                     start=(dt == 0), stop=(dt == NDT - 1))
            ev = ctx.inc(T.matmul(pzz[0:1, 512:1024], n.vOnesColB, sqs,
                                  start=(dt == 0), stop=(dt == NDT - 1)), "T")
            ctx.wait("V", ev)
        lnscr = RD.v(16384, F32, 1, 1024)
        lncols = RD.v(20480, F32, P, 16)
        ev = emit_ln_rows(nc, ctx, n, pzz[0:1, 0:512], pzz[0:1, 512:1024],
                          lnscr, lncols, p512[0][:, :], p512[1][:, :])
        ctx.wait("V", ev)
        for dt in range(NDT):
            xsl = n.x0f[dt][:, q4 * 512:(q4 + 1) * 512]
            V.tensor_mul(facc[dt], xsl, p512[0][:, :])
            V.drain()
            ins = V.tensor_add(n2[dt], facc[dt], p512[1][:, :])
        ev_n2 = ctx.inc(ins, "V")
        ctx.fence()

        for s8 in range(8):
            b = s8 % 2
            ctx.wait("Sy", ev_wt_free[b])
            for kcr in range(16):
                kc = s8 * 16 + kcr
                ins = Sy.dma_start(out=wtbuf[b][kcr],
                                   in_=n.wt_hbm[kc * 128:(kc + 1) * 128, :])
                ev_wt[b] = ctx.inc(ins, "dS", 16)
            hh = s8 // 2
            ev_y = []
            for gi in range(4):
                gg = (s8 % 2) * 4 + gi
                ev = ctx.inc(T.matmul(pzz[:, 0:512],
                                      ymr[:, (hh * 8 + gg) * P:
                                          (hh * 8 + gg + 1) * P],
                                      n2[gg], start=True, stop=True), "T")
                ctx.wait("V", ev)
                evy = ctx.inc(V.tensor_copy(yring[gi], pzz[:, 0:512]), "V")
                ctx.wait("T", evy)
                ev_y.append(evy)
            ctx.wait("T", ev_wt[b], *ev_y)
            ev_gelu = None
            for kcr in range(16):
                kc = s8 * 16 + kcr
                a4 = kc % 4
                tp = (96, 0) if a4 == 3 else None
                ctx.wait("T", ev_gelu)
                ins = T.matmul(pzz[:, (kcr % 2) * 512:(kcr % 2) * 512 + 512],
                               zl[32 * a4:32 * a4 + 32,
                                  (kc // 4) * 128:(kc // 4 + 1) * 128],
                               yring[kcr // 4][32 * a4:32 * a4 + 32, :],
                               start=True, stop=True, tile_position=tp)
                ev_z = ctx.inc(ins, "T")
                if kcr % 2 == 1:
                    ctx.wait("Sc", ev_z)
                    ev_gelu = ctx.inc(Sc.activation(
                        hsup[:, (kcr - 1) * 512:(kcr + 1) * 512],
                        pzz[:, 0:1024], AF.Gelu), "Sc")
                    ctx.wait("T", ev_gelu)
                    for kk in (kcr - 1, kcr):
                        for ot in range(3):
                            ins = T.matmul(
                                p512[ot][:, :],
                                wtbuf[b][kk][:, ot * 128:(ot + 1) * 128],
                                hsup[:, kk * 512:(kk + 1) * 512],
                                start=(kk == 0), stop=(kk == 15))
                    ctx.inc(ins, "T")
            ev_mm = ctx.ev("T")
            for kcr in range(16):
                for oi, ot in enumerate((3, 4)):
                    ins = T.matmul(
                        pzz[:, oi * 512:(oi + 1) * 512],
                        wtbuf[b][kcr][:, ot * 128:(ot + 1) * 128],
                        hsup[:, kcr * 512:(kcr + 1) * 512],
                        start=(kcr == 0), stop=(kcr == 15))
            ev_mm1 = ctx.inc(ins, "T")
            for ot in range(3):
                ctx.wait("V", ev_mm)
                if s8 == 0:
                    ins = V.tensor_copy(facc[ot], p512[ot][:, :])
                else:
                    ins = V.tensor_add(facc[ot], facc[ot], p512[ot][:, :])
            ev_fa = ctx.inc(ins, "V")
            ctx.wait("T", ev_fa)
            for kcr in range(16):
                for oi, ot in enumerate((5, 6, 7)):
                    ins = T.matmul(
                        p512[oi][:, :],
                        wtbuf[b][kcr][:, ot * 128:(ot + 1) * 128],
                        hsup[:, kcr * 512:(kcr + 1) * 512],
                        start=(kcr == 0), stop=(kcr == 15))
            ev_mm2 = ctx.inc(ins, "T")
            for oi, ot in enumerate((3, 4)):
                ctx.wait("V", ev_mm1)
                if s8 == 0:
                    ins = V.tensor_copy(facc[ot], pzz[:, oi * 512:(oi + 1) * 512])
                else:
                    ins = V.tensor_add(facc[ot], facc[ot],
                                       pzz[:, oi * 512:(oi + 1) * 512])
            ev_fa1 = ctx.inc(ins, "V")
            for oi, ot in enumerate((5, 6, 7)):
                ctx.wait("V", ev_mm2)
                if s8 == 0:
                    ins = V.tensor_copy(facc[ot], p512[oi][:, :])
                else:
                    ins = V.tensor_add(facc[ot], facc[ot], p512[oi][:, :])
            ev_fa2 = ctx.inc(ins, "V")
            ctx.wait("T", ev_fa1, ev_fa2)
            ev_wt_free[b] = ctx.ev("T")
        ctx.fence()

        if "aout" in n.dbg and q4 == 0:
            ctx.wait("G", ctx.ev("V"))
            for dt in range(NDT):
                ctx.inc(G.dma_start(
                    out=n.dbg["aout"][dt * P:(dt + 1) * P, 0:512],
                    in_=facc[dt]), "dG", 16)
            ctx.wait("T", ctx.ev("dG"))
            ctx.wait("V", ctx.ev("dG"))
            ctx.wait("Sc", ctx.ev("dG"))
        fbf = [RD.v(16384 + dt * 1024, BF16, P, 512) for dt in range(NDT)]
        sqf = RD.v(24576, BF16, P, 512)
        for dt in range(NDT):
            V.tensor_copy(fbf[dt], facc[dt])
            V.drain()
            V.tensor_mul(sqf, fbf[dt], fbf[dt])
            ev = ctx.incd("V")
            ctx.wait("T", ev)
            T.matmul(pzz[0:1, 0:512], n.vOnesColB, fbf[dt],
                     start=(dt == 0), stop=(dt == NDT - 1))
            ev = ctx.inc(T.matmul(pzz[0:1, 512:1024], n.vOnesColB, sqf,
                                  start=(dt == 0), stop=(dt == NDT - 1)), "T")
            ctx.wait("V", ev)
        lnscr2 = RD.v(25600, F32, 1, 1024)
        lncols2 = RD.v(29696, F32, P, 16)
        ev = emit_ln_rows(nc, ctx, n, pzz[0:1, 0:512], pzz[0:1, 512:1024],
                          lnscr2, lncols2, p512[0][:, :], p512[1][:, :])
        ctx.wait("V", ev)
        for dt in range(NDT):
            xsl = n.x0f[dt][:, q4 * 512:(q4 + 1) * 512]
            V.tensor_mul(facc[dt], facc[dt], p512[0][:, :])
            V.drain()
            V.tensor_add(facc[dt], facc[dt], p512[1][:, :])
            V.drain()
            V.tensor_add(xsl, xsl, facc[dt])
            ev = ctx.incd("V")
            ctx.wait("G", ev)
            ctx.inc(G.dma_start(out=n.out[dt * P:(dt + 1) * P,
                                         q4 * 512:(q4 + 1) * 512],
                                in_=xsl), "dG", 16)


def _mk_consts():
    c = {}
    c["cI128f"] = np.eye(P, dtype=np.float32)
    c["cI128b"] = np.eye(P, dtype=np.float32).astype(ml_dtypes.bfloat16)
    iota = np.arange(P, dtype=np.float32)
    c["cIotaNeg"] = np.tile(-iota[None, :], (P, 1))
    m0 = np.zeros((P, 8), dtype=np.float32)
    for p in range(P):
        m0[p, p // 16:] = 1.0
    c["cM0"] = m0.astype(ml_dtypes.bfloat16)
    sel = np.zeros((S, P), dtype=np.float32)
    for s in range(S):
        for d8 in range(8):
            sel[s, d8 * 16 + s] = 1.0
    c["cSelRep"] = sel
    selw = np.zeros((S, 8 * P), dtype=np.float32)
    for d8 in range(8):
        for s in range(S):
            selw[s, d8 * P + d8 * 16 + s] = 1.0
    c["cSelW"] = selw.astype(ml_dtypes.bfloat16)
    c["cOnes1"] = np.ones((1, P), dtype=np.float32)
    c["cOnes1b"] = np.ones((1, P), dtype=np.float32).astype(ml_dtypes.bfloat16)
    c["cNegD8"] = (-(np.arange(P) // 16).astype(np.float32))[:, None]
    c["cRevJ"] = (-(127.0 - np.arange(P, dtype=np.float32)))[:, None]
    c["cOnesColF"] = np.ones((P, 1), dtype=np.float32)
    c["cOnesColB"] = np.ones((P, 1), dtype=np.float32).astype(
        ml_dtypes.bfloat16)
    c["cEps"] = np.full((P, 1), EPS, dtype=np.float32)
    return c


def _host_inputs(x, logA, Bw, Cw, Dv, rm, ru, wcp):
    x = np.asarray(x)
    consts = _mk_consts()
    bwe = np.zeros((NDT, P, 18), dtype=np.float32)
    for dt in range(NDT):
        bwe[dt, :, 0:S] = np.asarray(Bw)[:, dt * P:(dt + 1) * P].T
        bwe[dt, :, 16] = 1.0
    ym = np.zeros((32, P, P), dtype=np.float32)
    rm = np.asarray(rm)
    for h in range(4):
        for g in range(8):
            for rr in range(8):
                ym[h * 8 + g, rr * 16:(rr + 1) * 16,
                   rr * 16:(rr + 1) * 16] = rm[h, g * 8 + rr]
    zl = np.zeros((P, 32 * P), dtype=np.float32)
    ru = np.asarray(ru)
    for t in range(128):
        h, r2 = t // 32, t % 32
        blk, a = t // 4, t % 4
        for rr2 in range(2):
            r_ = r2 * 2 + rr2
            zl[a * 32 + rr2 * 16: a * 32 + (rr2 + 1) * 16,
               blk * P + rr2 * 64: blk * P + (rr2 + 1) * 64] = ru[h, r_].T
    dv = np.asarray(Dv).reshape(NDT, P)

    maps = []
    for core in range(8):
        b, q = core // 4, core % 4
        t0 = q * NT
        xo = np.ascontiguousarray(x[b, t0:t0 + NT])
        pr = np.zeros((NPR, D), dtype=np.float32)
        lo = max(0, t0 - NPR)
        if t0 > 0:
            pr[NPR - (t0 - lo):] = x[b, lo:t0]
        maps.append(dict(
            xo=xo, xp=pr, logA=np.asarray(logA), CwIn=np.asarray(Cw),
            CwIn2=np.asarray(Cw), BwEIn=bwe, DvIn=dv, ymIn=ym, zlIn=zl,
            wc=np.asarray(wcp), **consts))
    return maps


def kernel(L0_attn_x, L0_attn_log_A_diag, L0_attn_B_w, L0_attn_C_w,
           L0_attn_D_vec, L0_ffn_rail_maker, L0_ffn_rail_up_proj,
           L0_ffn_w_cproj):
    maps = _host_inputs(L0_attn_x, L0_attn_log_A_diag, L0_attn_B_w,
                        L0_attn_C_w, L0_attn_D_vec, L0_ffn_rail_maker,
                        L0_ffn_rail_up_proj, L0_ffn_w_cproj)
    nc, _ = build_program()
    res = run_bass_kernel_spmd(nc, maps, core_ids=list(range(8)))
    full = np.zeros((2, 8192, D), dtype=np.float32)
    for core in range(8):
        b, q = core // 4, core % 4
        full[b, q * NT:(q + 1) * NT] = np.asarray(res.results[core]["out"]).T
    return full


# version 15 (final)
# speedup vs baseline: 1.6367x; 1.1687x over previous
import contextlib
import numpy as np
import ml_dtypes
import concourse.bass as bass
import concourse.mybir as mybir
from concourse.bass_utils import run_bass_kernel_spmd

F32 = mybir.dt.float32
BF16 = mybir.dt.bfloat16
U8 = mybir.dt.uint8
AF = mybir.ActivationFunctionType
ALU = mybir.AluOpType

P = 128
S = 16
D = 1024
NDT = 8
NT = 2048
NPR = 6144
C = 128
NCH_OWN = 16
NCH_ALL = 64
BT_PAD = 0
BT_COLS = NPR + NT
OWN0 = NPR
EPS = 1e-5
KF = 16384

DBG = set()
BUILD_PHASES = 99
ENTRY_SUB = 99
SKIP_TABLES = False
N_GROUPS = 4
TAPG = "own"
INTER_G = True
DBG_VTS = 0
DBG_VTD = 0
N1DT = 0
SKIP_INTRA = False
SKIP_DSKIP = False
SKIP_INTER = False
PREFETCH_WT = False


class Ctx:
    def __init__(self, nc, stack):
        self.nc = nc
        self.eng = {"T": nc.tensor, "V": nc.vector, "Sc": nc.scalar,
                    "G": nc.gpsimd, "Sy": nc.sync}
        self.sems = {}
        self.cnt = {}
        self.waited = {}
        for name in ("dS", "dG", "T", "V", "Sc", "G"):
            self.sems[name] = stack.enter_context(nc.semaphore(name))
            self.cnt[name] = 0

    def inc(self, ins, sem, amt=1):
        self.cnt[sem] += amt
        ins.then_inc(self.sems[sem], amt)
        return (sem, self.cnt[sem])

    def ev(self, sem):
        return (sem, self.cnt[sem])

    def incd(self, engname, sem=None):
        return self.inc(self.eng[engname].drain(), sem or engname)

    def wait(self, engname, *evs):
        for ev in evs:
            if ev is None:
                continue
            sem, val = ev
            key = (engname, sem)
            if self.waited.get(key, 0) >= val:
                continue
            self.eng[engname].wait_ge(self.sems[sem], val)
            self.waited[key] = val

    def fence(self):
        for en in ("V", "Sc", "G"):
            self.inc(self.eng[en].drain(), en)
        evs = [self.ev(s) for s in ("dS", "dG", "T", "V", "Sc", "G")]
        for en in ("T", "V", "Sc", "G", "Sy"):
            self.wait(en, *evs)


class Region:
    def __init__(self, nc, name, nbytes):
        self.t = nc.alloc_sbuf_tensor(name, [P, nbytes], U8)
        self.nbytes = nbytes
        self.name = name

    def v(self, off, dtype, rows, cols):
        esz = 2 if dtype == BF16 else 4
        assert off % 4 == 0 and off + cols * esz <= self.nbytes, \
            f"{self.name} off={off} cols={cols}"
        return self.t.ap()[0:rows, off:off + cols * esz].bitcast(dtype)


class NS:
    def __init__(self, **kw):
        self.__dict__.update(kw)


def build_program():
    nc = bass.Bass("TRN2", target_bir_lowering=False, debug=False)

    def din(name, shape, dt=F32):
        return nc.dram_tensor(name, list(shape), dt, kind="ExternalInput").ap()

    ins_d = dict(
        xo=din("xo", [NT, D]),
        xp=din("xp", [NPR, D]),
        logA=din("logA", [D, S]),
        CwIn=din("CwIn", [D, S]),
        CwIn2=din("CwIn2", [D, S]),
        BwEIn=din("BwEIn", [NDT, P, 18]),
        DvIn=din("DvIn", [NDT, P]),
        ymIn=din("ymIn", [32, P, P]),
        zlIn=din("zlIn", [P, 32 * P]),
        wc=din("wc", [D, KF]),
        cI128f=din("cI128f", [P, P]),
        cI128b=din("cI128b", [P, P], BF16),
        cIotaNeg=din("cIotaNeg", [P, P]),
        cM0=din("cM0", [P, 8], BF16),
        cSelRep=din("cSelRep", [S, P]),
        cSelW=din("cSelW", [S, 8 * P], BF16),
        cOnes1=din("cOnes1", [1, P]),
        cOnes1b=din("cOnes1b", [1, P], BF16),
        cNegD8=din("cNegD8", [P, 1]),
        cRevJ=din("cRevJ", [P, 1]),
        cOnesColF=din("cOnesColF", [P, 1]),
        cOnesColB=din("cOnesColB", [P, 1], BF16),
        cEps=din("cEps", [P, 1]),
    )
    out = nc.dram_tensor("out", [D, NT], F32, kind="ExternalOutput").ap()
    wt_hbm = nc.dram_tensor("wt_hbm", [KF, D], BF16, kind="Internal").ap()

    dbg_out = {}
    for name, shape, dt in [
        ("bt", [S, BT_COLS], BF16),
        ("x0f", [D, NT], F32),
        ("rows", [2, NT], BF16),
        ("est", [P, P * S], F32),
        ("aout", [D, NT], F32),
        ("x1", [D, NT], F32),
        ("hdbg", [P, 512], BF16),
        ("stat", [P, 64], F32),
        ("est2", [P, P * S], F32),
        ("sqa", [P, 8], F32),
        ("statp", [18, 512], F32),
        ("ddbg", [P, NDT * P + 2 * NDT], BF16),
        ("xpf", [D, 512], BF16),
    ]:
        if name in DBG:
            dbg_out[name] = nc.dram_tensor(
                "dbg_" + name, list(shape), dt, kind="ExternalOutput").ap()

    stack = contextlib.ExitStack()
    with stack:
        RX = Region(nc, "RX", 65536)
        RA = Region(nc, "RA", 32768)
        RB = Region(nc, "RB", 32768)
        RC = Region(nc, "RC", 32768)
        RD = Region(nc, "RD", 32768)

        sb = dict(
            e1T=nc.alloc_sbuf_tensor("e1T", [P, NDT * S], F32),
            a1c=nc.alloc_sbuf_tensor("a1c", [P, NDT * S], F32),
            
            c16=nc.alloc_sbuf_tensor("c16", [S, 1], F32),
            dcols=nc.alloc_sbuf_tensor("dcols", [P, NDT], F32),
            ddiag=nc.alloc_sbuf_tensor("ddiag", [P, NDT * P], BF16),
            consts=nc.alloc_sbuf_tensor("consts", [P, 520], F32),
            constsb=nc.alloc_sbuf_tensor("constsb", [P, 2600], U8),
            rows=nc.alloc_sbuf_tensor("rows", [1, 2 * NT], BF16),
            c16rep=nc.alloc_sbuf_tensor("c16rep", [P, S], F32),
            sqacc=nc.alloc_sbuf_tensor("sqacc", [P, 8], F32),
        )

        p512 = [stack.enter_context(nc.psum_tensor(f"p512_{i}", [P, 512], F32))
                for i in range(3)]
        pzz = stack.enter_context(nc.psum_tensor("pzz", [P, 1024], F32))
        pTb = stack.enter_context(nc.psum_tensor("pTb", [P, 2048], BF16))
        pW = stack.enter_context(nc.psum_tensor("pW", [P, 512], F32))

        ctx = Ctx(nc, stack)
        block = stack.enter_context(nc.Block())

        n = NS(**ins_d, **sb, out=out, wt_hbm=wt_hbm, dbg=dbg_out,
               RX=RX, RA=RA, RB=RB, RC=RC, RD=RD,
               p512=p512, pzz=pzz, pTb=pTb, pW=pW)
        cs = n.consts.ap()
        cb = n.constsb.ap()
        n.vI128f = cs[:, 0:128]
        n.vIotaNeg = cs[:, 128:256]
        n.vSelRep = cs[0:S, 256:384]
        n.vOnes1 = cs[0:1, 384:512]
        n.vNegD8 = cs[:, 512:513]
        n.vRevJ = cs[:, 513:514]
        n.vOnesColF = cs[:, 514:515]
        n.vEps = cs[:, 515:516]
        n.vI128b = cb[:, 0:256].bitcast(BF16)
        n.vM0 = cb[:, 256:272].bitcast(BF16)
        n.vSelW = cb[0:S, 272:2320].bitcast(BF16)
        n.vOnes1b = cb[0:1, 2320:2576].bitcast(BF16)
        n.vOnesColB = cb[:, 2576:2578].bitcast(BF16)

        @block.sync
        def _(sy):
            emit_all(nc, ctx, n)

    return nc, dbg_out


def emit_all(nc, ctx, n):
    T, V, Sc, G, Sy = (ctx.eng[k] for k in ("T", "V", "Sc", "G", "Sy"))
    p512, pzz, pTb = n.p512, n.pzz, n.pTb
    RX, RA, RB, RC, RD = n.RX, n.RA, n.RB, n.RC, n.RD

    x0f = [RX.v(dt * 8192, F32, P, NT) for dt in range(NDT)]
    n.x0f = x0f
    n.cwT = RC.v(30720, F32, P, NDT * S)
    n.a128 = n.RA.t
    n.a128 = n.RA.v(27648, F32, P, NDT * S)
    n.bwe = n.RA.v(28160 + 32 if False else 28672, BF16, P, NDT * 18)

    RCdv = RC.v(28672, F32, 8, P)
    for ap_dst, src in [
        (n.vI128f, n.cI128f[:]), (n.vIotaNeg, n.cIotaNeg[:]),
        (n.vSelRep, n.cSelRep[:]), (n.vOnes1, n.cOnes1[:]),
        (n.vNegD8, n.cNegD8[:]), (n.vRevJ, n.cRevJ[:]),
        (n.vOnesColF, n.cOnesColF[:]), (n.vI128b, n.cI128b[:]),
        (n.vM0, n.cM0[:]), (n.vSelW, n.cSelW[:]),
        (n.vOnes1b, n.cOnes1b[:]), (n.vOnesColB, n.cOnesColB[:]),
        (n.vEps, n.cEps[:]),
        (n.e1T[:].rearrange("p (n s) -> p n s", n=NDT),
         n.logA.rearrange("(n p) s -> p n s", p=P)),
        (n.a1c[:].rearrange("p (n s) -> p n s", n=NDT),
         n.CwIn.rearrange("(n p) s -> p n s", p=P)),
        (n.cwT.rearrange("p (n s) -> p n s", n=NDT),
         n.CwIn2.rearrange("(n p) s -> p n s", p=P)),
        (RCdv, n.DvIn[:]),
    ]:
        ctx.inc(Sy.dma_start(out=ap_dst, in_=src), "dS", 16)
    ctx.inc(G.dma_start(out=n.bwe.rearrange("p (n c) -> p n c", n=NDT),
                        in_=n.BwEIn.rearrange("n p c -> p n c")), "dG", 16)
    ctx.fence()

    if not SKIP_TABLES:
        ctx.inc(Sc.activation(n.e1T[:], n.e1T[:], AF.Exp), "Sc")
        ctx.inc(Sc.activation(n.a128, n.e1T[:], AF.Exp, scale=-128.0), "Sc")
        sc_a1 = RC.v(24576, F32, P, NDT * S)
        ev = ctx.inc(Sc.activation(sc_a1, n.e1T[:], AF.Exp, scale=-1.0), "Sc")
        ctx.wait("V", ev)
        ctx.inc(V.tensor_mul(n.a1c[:], n.a1c[:], sc_a1), "V")

        ev = ctx.inc(T.transpose(pzz[0:P, 0:8], RCdv[0:8, 0:P],
                                 n.vI128f[0:8, 0:8]), "T")
        ctx.wait("V", ev)
        V.tensor_copy(n.dcols[:], pzz[:, 0:8])
        ctx.incd("V")
        for dt in range(NDT):
            ctx.inc(V.tensor_scalar_mul(n.ddiag[:, dt * P:(dt + 1) * P],
                                        n.vI128b, n.dcols[:, dt:dt + 1]), "V")

        for dt in range(NDT):
            ins = T.matmul(p512[2][0:18, 0:1], n.bwe[:, dt * 18:(dt + 1) * 18],
                           n.vOnesColB, start=(dt == 0), stop=(dt == NDT - 1))
        ev = ctx.inc(ins, "T")
        ctx.wait("V", ev)
        ev = ctx.inc(V.tensor_copy(n.c16[:], p512[2][0:S, 0:1]), "V")
        ctx.wait("T", ev)
        ev = ctx.inc(T.transpose(pzz[0:1, 8:24], n.c16[:], n.vI128f[0:S, 0:S]),
                     "T")
        ctx.wait("V", ev)
        c16row = RC.v(29696, F32, 1, S)
        ev = ctx.inc(V.tensor_copy(c16row, pzz[0:1, 8:24]), "V")
        ctx.wait("T", ev)
        ev = ctx.inc(T.matmul(pzz[0:P, 32:48], n.vOnes1, c16row,
                              start=True, stop=True), "T")
        ctx.wait("V", ev)
        ctx.inc(V.tensor_copy(n.c16rep[:], pzz[:, 32:48]), "V")
        ctx.fence()

        e1sm = RC.v(16384, F32, S, D)
        cwsm = RC.v(20480, F32, S, D)
        for src, dst in [(n.e1T[:], e1sm), (n.cwT, cwsm)]:
            for dt in range(NDT):
                ins = T.transpose(pzz[0:S, dt * P:(dt + 1) * P],
                                  src[:, dt * S:(dt + 1) * S], n.vI128f)
            ev = ctx.inc(ins, "T")
            ctx.wait("V", ev)
            ev = ctx.inc(V.tensor_copy(dst, pzz[0:S, :]), "V")
            ctx.wait("T", ev)

        e1rep = RC.v(0, F32, P, D)
        crep = RC.v(4096, F32, P, D)
        for src, dst in [(e1sm, e1rep), (cwsm, crep)]:
            for half in range(2):
                ins = T.matmul(pzz[:, half * 512:(half + 1) * 512], n.vSelRep,
                               src[0:S, half * 512:(half + 1) * 512],
                               start=True, stop=True)
            ev = ctx.inc(ins, "T")
            ctx.wait("V", ev)
            ev = ctx.inc(V.tensor_copy(dst, pzz[:, :]), "V")
            ctx.wait("T", ev)

        a8 = RC.v(8192, F32, P, D)
        tg = RC.v(12288, F32, P, D)
        ctx.wait("Sc", ctx.ev("V"))
        ev = ctx.inc(Sc.activation(a8, e1rep, AF.Exp, scale=-8.0), "Sc")
        ev2 = ctx.inc(V.tensor_scalar_mul(tg, e1rep, n.vNegD8), "V")
        ctx.wait("Sc", ev2)
        ev3 = ctx.inc(Sc.activation(tg, tg, AF.Exp), "Sc")
        ctx.wait("V", ev, ev3)
        W2 = [RB.v(g * 2048, BF16, P, D) for g in range(16)]
        n.W2 = W2
        for g in range(16):
            ctx.inc(V.tensor_mul(W2[g], tg, crep), "V")
            if g < 15:
                ctx.inc(V.tensor_mul(tg, tg, a8), "V")
        ctx.fence()

    if BUILD_PHASES <= 1:
        finish(nc, ctx, n, partial=True)
        return
    Btbf = RA.v(0, BF16, S, BT_COLS)
    n.Btbf = Btbf
    xbf = [RC.v(dt * 4096, BF16, P, NT) for dt in range(NDT)]
    n.xbf = xbf
    stg = [[RD.v((b * 4 + j) * 4096, F32, P, D) for j in range(4)]
           for b in range(2)]
    xorr = n.xo.rearrange("(n p) d -> n p d", p=P)
    ev_freeE = [None, None]
    ev_erE = None
    for grp in range(N_GROUPS):
        b = grp % 2
        ctx.wait("Sy", *(ev_freeE[b] or []))
        for j in range(4):
            ev_ld = ctx.inc(Sy.dma_start(out=stg[b][j],
                                         in_=xorr[grp * 4 + j]), "dS", 16)
        ctx.wait("T", ev_ld, ev_erE)
        for dt in range(NDT):
            ph = (dt % 2) * 512
            for j in range(4):
                ins = T.transpose(pzz[:, ph + j * P:ph + (j + 1) * P],
                                  stg[b][j][:, dt * P:(dt + 1) * P], n.vI128f)
            ev_tr = ctx.inc(ins, "T")
            ctx.wait("Sc", ev_tr)
            ev_a = ctx.inc(Sc.copy(x0f[dt][:, grp * 512:(grp + 1) * 512],
                                   pzz[:, ph:ph + 512]), "Sc")
            ctx.wait("V", ev_tr, ev_a)
            ev_b = ctx.inc(V.tensor_copy(xbf[dt][:, grp * 512:(grp + 1) * 512],
                                         pzz[:, ph:ph + 512]), "V")
            ctx.wait("T", ev_a, ev_b)
        ctx.wait("Sc", ev_erE)
        for j in range(4):
            ctx.inc(Sc.activation(stg[b][j], stg[b][j], AF.Square,
                                  accum_out=n.sqacc[:, j:j + 1]), "Sc")
        for dt in range(NDT):
            ins = T.matmul(p512[2][0:18, :],
                           n.bwe[:, dt * 18:(dt + 1) * 18],
                           xbf[dt][:, grp * 512:(grp + 1) * 512],
                           start=(dt == 0), stop=(dt == NDT - 1))
        ev_bx = ctx.inc(ins, "T")
        emit_rows_and_bt(
            nc, ctx, n, ev_bx, stg[b][0],
            bt_dst=Btbf[0:S, OWN0 + grp * 512:OWN0 + (grp + 1) * 512],
            rows_off=grp * 512)
        ev_erE = ctx.ev("V")
        ev_freeE[b] = [ev_erE, ctx.ev("Sc")]
    ctx.fence()

    if BUILD_PHASES <= 2:
        finish(nc, ctx, n, partial=True)
        return
    pstg = [[RD.v((b * 4 + j) * 2048, BF16, P, D) for j in range(4)]
            for b in range(2)]
    xpfT = [RD.v(16384 + dt * 1024, BF16, P, 512) for dt in range(NDT)]
    pscr = RD.v(24576, F32, P, 1024)
    xprr = n.xp.rearrange("(n p) d -> n p d", p=P)
    ev_freeP = [None, None]
    ev_erP = None
    for grp in range(12):
        b = grp % 2
        ctx.wait("G", *(ev_freeP[b] or []))
        ctx.wait("G", ctx.ev("T"))
        for j in range(4):
            ev_ld = ctx.inc(G.dma_start(out=pstg[b][j],
                                        in_=xprr[grp * 4 + j]), "dG", 16)
        ctx.wait("T", ev_ld, ev_erP)
        for dt in range(NDT):
            ph = (dt % 2) * 1024
            for j in range(4):
                ins = T.transpose(pTb[:, ph + j * P:ph + (j + 1) * P],
                                  pstg[b][j][:, dt * P:(dt + 1) * P],
                                  n.vI128b)
            ev_tr = ctx.inc(ins, "T")
            ctx.wait("V", ev_tr)
            ev_b = ctx.inc(V.tensor_copy(xpfT[dt], pTb[:, ph:ph + 512]), "V")
            ctx.wait("T", ev_b)
        ctx.wait("Sc", ctx.ev("T"), ev_erP)
        for j in range(4):
            ctx.inc(Sc.activation(pstg[b][j], pstg[b][j], AF.Square,
                                  accum_out=n.sqacc[:, j:j + 1]), "Sc")
        for dt in range(NDT):
            ins = T.matmul(p512[2][0:18, :], n.bwe[:, dt * 18:(dt + 1) * 18],
                           xpfT[dt], start=(dt == 0), stop=(dt == NDT - 1))
        ev_bx = ctx.inc(ins, "T")
        if "xpf" in n.dbg and grp == 11:
            ctx.wait("G", ctx.ev("V"))
            for dt in range(NDT):
                ctx.inc(G.dma_start(out=n.dbg["xpf"][dt * P:(dt + 1) * P, :],
                                    in_=xpfT[dt]), "dG", 16)
        emit_rows_and_bt(
            nc, ctx, n, ev_bx, pscr,
            bt_dst=Btbf[0:S, BT_PAD + grp * 512:BT_PAD + (grp + 1) * 512],
            rows_off=None, tap=(grp == 11 and TAPG == "pre"))
        ev_erP = ctx.ev("V")
        ev_freeP[b] = [ev_erP, ctx.ev("Sc")]
    ctx.fence()
    if BT_PAD:
        ctx.inc(V.memset(Btbf[0:S, 0:BT_PAD], 0.0), "V")
    ctx.fence()

    if "bt" in n.dbg:
        ctx.inc(G.dma_start(out=n.dbg["bt"][:], in_=Btbf), "dG", 16)
    if "x0f" in n.dbg:
        for dt in range(NDT):
            ctx.inc(G.dma_start(out=n.dbg["x0f"][dt * P:(dt + 1) * P, :],
                                in_=x0f[dt]), "dG", 16)
    if "rows" in n.dbg:
        ctx.inc(G.dma_start(out=n.dbg["rows"][:], in_=n.rows[:]), "dG", 16)
    if BUILD_PHASES <= 3:
        finish(nc, ctx, n, partial=True)
        return

    emit_scan_phase(nc, ctx, n)
    if BUILD_PHASES <= 4:
        finish(nc, ctx, n, partial=True)
        return
    emit_wprep_phase(nc, ctx, n)
    if BUILD_PHASES <= 5:
        finish(nc, ctx, n, partial=True)
        return
    emit_ssm_main(nc, ctx, n)
    if BUILD_PHASES <= 6:
        finish(nc, ctx, n, partial=True)
        return
    emit_ffn(nc, ctx, n)
    finish(nc, ctx, n, partial=False)


def finish(nc, ctx, n, partial):
    G = ctx.eng["G"]
    ctx.fence()
    if partial:
        ev = ctx.inc(G.dma_start(out=n.out[0:P, :], in_=n.x0f[0]), "dG", 16)
        ctx.wait("G", ev)
    ctx.wait("G", ctx.ev("dG"), ctx.ev("dS"))


def emit_rows_and_bt(nc, ctx, n, ev_bx, scr, bt_dst, rows_off, tap=False):
    V, Sc, T = ctx.eng["V"], ctx.eng["Sc"], ctx.eng["T"]
    p512, pzz = n.p512, n.pzz
    scr18 = scr[0:18, 0:512]
    ctx.wait("V", ev_bx, ctx.ev("Sc"))
    ev = ctx.inc(V.tensor_copy(scr18, p512[2][0:18, :]), "V")
    ctx.wait("T", ev)
    for j in range(4):
        ins = T.transpose(pzz[0:P, j * 32:j * 32 + 18],
                          scr18[0:18, j * 128:(j + 1) * 128],
                          n.vI128f[0:18, 0:18])
    ev_cols = ctx.inc(ins, "T")
    ctx.wait("V", ev_cols)
    pzz3 = pzz.rearrange("p (j c) -> p j c", c=32)
    sumc4 = pzz3[:, 0:4, 16:17].squeeze(2)
    Bxc = pzz[:, 0:128]
    mc4 = scr[:, 512:516]
    vc4 = scr[:, 516:520]
    rc4 = scr[:, 520:524]
    mrc4 = scr[:, 524:528]
    mm24 = scr[:, 528:532]
    V.tensor_scalar_mul(mc4, sumc4, 1.0 / D)
    V.tensor_scalar_mul(vc4, n.sqacc[:, 0:4], 1.0 / D)
    V.drain()
    V.tensor_copy(mm24, mc4)
    V.drain()
    V.tensor_mul(mm24, mm24, mc4)
    V.drain()
    V.tensor_sub(vc4, vc4, mm24)
    ev = ctx.incd("V")
    ctx.wait("Sc", ev)
    Sc.activation(rc4, vc4, AF.Ln, bias=n.vEps)
    Sc.drain()
    Sc.activation(rc4, rc4, AF.Exp, scale=-0.5)
    ev = ctx.incd("Sc")
    ctx.wait("V", ev)
    V.tensor_mul(mrc4, mc4, rc4)
    V.drain()
    ev_tr = None
    for j in range(4):
        t1 = scr[:, 576 + j * 32:576 + j * 32 + 16]
        t2 = scr[:, 576 + j * 32 + 16:576 + j * 32 + 32]
        V.tensor_scalar_mul(t1, pzz[:, j * 32:j * 32 + 16],
                            rc4[:, j:j + 1])
        V.tensor_scalar_mul(t2, n.c16rep[:], mrc4[:, j:j + 1])
        V.drain()
        V.tensor_sub(t1, t1, t2)
        ev = ctx.incd("V")
        ctx.wait("T", ev)
        ev_tr = ctx.inc(T.transpose(
            pzz[0:S, 256 + j * 128:256 + (j + 1) * 128], t1, n.vI128f), "T")
        if rows_off is not None:
            T.transpose(pzz[0:1, 768:896], mrc4[:, j:j + 1], n.vI128f)
            ev2 = ctx.inc(T.transpose(pzz[0:1, 896:1024], rc4[:, j:j + 1],
                                      n.vI128f), "T")
            ctx.wait("V", ev2)
            rb = n.rows.ap()
            V.tensor_copy(rb[0:1, rows_off + j * 128:rows_off + (j + 1) * 128],
                          pzz[0:1, 768:896])
            V.tensor_copy(
                rb[0:1, NT + rows_off + j * 128:NT + rows_off + (j + 1) * 128],
                pzz[0:1, 896:1024])
            ev3 = ctx.incd("V")
            ctx.wait("T", ev3)
    ctx.wait("V", ev_tr)
    ctx.inc(V.tensor_copy(bt_dst, pzz[0:S, 256:768]), "V")

def emit_ln_rows(nc, ctx, n, sum_ap, sq_ap, scr, scrcols, rrep_ps, qrep_ps,
                 ntok=512):
    V, Sc, T = ctx.eng["V"], ctx.eng["Sc"], ctx.eng["T"]
    pzz = n.pzz
    nj = ntok // 128
    V.tensor_copy(scr[0:1, 0:ntok], sum_ap)
    V.tensor_copy(scr[0:1, ntok:2 * ntok], sq_ap)
    ev = ctx.incd("V")
    ctx.wait("T", ev)
    for j in range(nj):
        T.transpose(pzz[0:P, j * 4:j * 4 + 1],
                    scr[0:1, j * 128:(j + 1) * 128], n.vI128f[0:1, 0:1])
        ins = T.transpose(pzz[0:P, j * 4 + 1:j * 4 + 2],
                          scr[0:1, ntok + j * 128:ntok + (j + 1) * 128],
                          n.vI128f[0:1, 0:1])
    ev = ctx.inc(ins, "T")
    ctx.wait("V", ev)
    pz3 = pzz.rearrange("p (j c) -> p j c", c=4)
    sum4 = pz3[:, 0:nj, 0:1].squeeze(2)
    sq4 = pz3[:, 0:nj, 1:2].squeeze(2)
    mc4 = scrcols[:, 0:nj]
    vc4 = scrcols[:, 4:4 + nj]
    rc4 = scrcols[:, 8:8 + nj]
    qc4 = scrcols[:, 12:12 + nj]
    V.tensor_scalar_mul(mc4, sum4, 1.0 / D)
    V.tensor_scalar_mul(vc4, sq4, 1.0 / D)
    V.drain()
    V.tensor_copy(qc4, mc4)
    V.drain()
    V.tensor_mul(qc4, qc4, mc4)
    V.drain()
    V.tensor_sub(vc4, vc4, qc4)
    ev = ctx.incd("V")
    ctx.wait("Sc", ev)
    Sc.activation(rc4, vc4, AF.Ln, bias=n.vEps)
    Sc.drain()
    Sc.activation(rc4, rc4, AF.Exp, scale=-0.5)
    ev = ctx.incd("Sc")
    ctx.wait("V", ev)
    V.tensor_mul(qc4, qc4 if False else mc4, rc4)
    V.drain()
    V.tensor_scalar(qc4, qc4, -1.0, None, ALU.mult)
    ev = ctx.incd("V")
    ctx.wait("T", ev)
    for j in range(nj):
        T.transpose(pzz[0:1, 64:192], rc4[:, j:j + 1], n.vI128f)
        ev = ctx.inc(T.transpose(pzz[0:1, 192:320], qc4[:, j:j + 1],
                                 n.vI128f), "T")
        ctx.wait("V", ev)
        V.tensor_copy(scr[0:1, j * 128:(j + 1) * 128], pzz[0:1, 64:192])
        V.tensor_copy(scr[0:1, ntok + j * 128:ntok + (j + 1) * 128],
                      pzz[0:1, 192:320])
        ev = ctx.incd("V")
        ctx.wait("T", ev)
    for j in range(nj):
        T.matmul(rrep_ps[:, j * 128:(j + 1) * 128], n.vOnes1,
                 scr[0:1, j * 128:(j + 1) * 128], start=True, stop=True)
        ins = T.matmul(qrep_ps[:, j * 128:(j + 1) * 128], n.vOnes1,
                       scr[0:1, ntok + j * 128:ntok + (j + 1) * 128],
                       start=True, stop=True)
    return ctx.inc(ins, "T")

def emit_scan_phase(nc, ctx, n):
    T, V, Sc, G, Sy = (ctx.eng[k] for k in ("T", "V", "Sc", "G", "Sy"))
    p512, pzz, pTb = n.p512, n.pzz, n.pTb
    RA, RC, RD = n.RA, n.RC, n.RD

    Rtab = [RC.v(i * 256, BF16, P, P) for i in range(128)]
    n.Rtab = Rtab
    tmpf = [RD.v(i * 512, F32, P, P) for i in range(8)]
    evp = [None, None]
    for q in range(32):
        b = q % 2
        ctx.wait("V", evp[b])
        for u in range(4):
            V.tensor_scalar_mul(tmpf[b * 4 + u], n.vIotaNeg,
                                n.e1T[:, q * 4 + u:q * 4 + u + 1])
        ev = ctx.incd("V")
        ctx.wait("Sc", ev)
        for u in range(4):
            ins = Sc.activation(Rtab[q * 4 + u], tmpf[b * 4 + u], AF.Exp)
        evp[b] = ctx.inc(ins, "Sc")
    ctx.fence()

    Vtab = [RD.v(s * 2048, BF16, P, D) for s in range(16)]
    e1piece = RA.v(24576, F32, 1, P)
    tmpV = RA.v(16384, F32, P, D)
    for s in range(16):
        for dt in range(NDT):
            ev = ctx.inc(T.transpose(
                pzz[0:1, 0:128],
                n.e1T[:, dt * S + s:dt * S + s + 1], n.vI128f), "T")
            ctx.wait("V", ev)
            V.tensor_copy(e1piece, pzz[0:1, 0:128])
            ev = ctx.incd("V")
            ctx.wait("T", ev)
            ins = T.matmul(p512[dt // 4][:, (dt % 4) * 128:(dt % 4 + 1) * 128],
                           n.vOnes1, e1piece, start=True, stop=True)
            ev = ctx.inc(ins, "T")
        ctx.wait("V", ev)
        V.tensor_scalar_mul(tmpV[:, 0:512], p512[0][:, :], n.vRevJ)
        ev = ctx.inc(V.tensor_scalar_mul(tmpV[:, 512:1024], p512[1][:, :],
                                         n.vRevJ), "V")
        ctx.wait("Sc", ev)
        ev = ctx.inc(Sc.activation(Vtab[s], tmpV, AF.Exp), "Sc")
        ctx.wait("T", ev)
        ctx.wait("V", ev)
    ctx.fence()

    BtT = RA.v(24576, BF16, P, NCH_ALL * S)
    n.BtT = BtT
    evb = {}
    for batch in range(8):
        ph = (batch % 2) * 1024
        if batch >= 2:
            ctx.wait("T", evb[batch % 2])
        for cc in range(8):
            c = batch * 8 + cc
            ins = T.transpose(
                pTb[:, ph + cc * S:ph + (cc + 1) * S],
                n.Btbf[0:S, BT_PAD + c * 128:BT_PAD + (c + 1) * 128],
                n.vI128b[0:S, 0:S])
        ev = ctx.inc(ins, "T")
        ctx.wait("V", ev)
        evb[batch % 2] = ctx.inc(
            V.tensor_copy(BtT[:, batch * 128:(batch + 1) * 128],
                          pTb[:, ph:ph + 128]), "V")
    ctx.fence()

    Estore = RA.v(16384, F32, P, 128 * S)
    n.Estore = Estore
    vseg = RA.v(26624, F32, P, 64)
    a128rep = RA.v(26880, F32, P, 64)
    hscan = RA.v(27136, F32, P, 64)
    btt3 = BtT.rearrange("p (c u) -> p c u", u=S)
    evseg = {0: None, 1: None, 2: None}
    for dt in range(NDT):
        for s in range(S):
            seg = dt * S + s
            b = seg % 3
            pb = p512[b]
            ctx.wait("T", evseg[b])
            rhs = btt3[:, :, s:s + 1].squeeze(2)
            ev = ctx.inc(T.matmul(pb[:, 0:64],
                                  Vtab[s][:, dt * 128:(dt + 1) * 128],
                                  rhs, start=True, stop=True), "T")
            ctx.wait("V", ev)
            V.tensor_tensor_scan(
                hscan, n.a128[:, seg:seg + 1].to_broadcast((P, 64)),
                pb[:, 0:64], 0.0, ALU.mult, ALU.add)
            V.tensor_scalar_mul(Estore[:, seg * S:(seg + 1) * S],
                                hscan[:, 47:63], n.a1c[:, seg:seg + 1])
            evseg[b] = ctx.incd("V")
    ctx.fence()
    if "ddbg" in n.dbg:
        ctx.inc(ctx.eng["G"].dma_start(out=n.dbg["ddbg"][:, 0:NDT * P],
                                       in_=n.ddiag[:]), "dG", 16)
        ctx.inc(ctx.eng["G"].dma_start(
            out=n.dbg["ddbg"][:, NDT * P:NDT * P + NDT].bitcast(F32)
            if False else n.dbg["ddbg"][:, NDT * P:NDT * P + 2 * NDT],
            in_=n.dcols[:].bitcast(BF16)), "dG", 16)
        ctx.fence()
    if "hdbg" in n.dbg:
        for i in range(4):
            ctx.inc(ctx.eng["G"].dma_start(
                out=n.dbg["hdbg"][:, i * 128:(i + 1) * 128],
                in_=Vtab[DBG_VTS + i][:, DBG_VTD * 128:(DBG_VTD + 1) * 128]),
                "dG", 16)
        ctx.fence()
    if "est" in n.dbg:
        ctx.inc(G.dma_start(out=n.dbg["est"][:], in_=Estore), "dG", 16)
        ctx.fence()


def emit_wprep_phase(nc, ctx, n):
    T, V, Sc, G, Sy = (ctx.eng[k] for k in ("T", "V", "Sc", "G", "Sy"))
    pTb = n.pTb
    RD = n.RD
    wn = [RD.v(i * 8192, BF16, P, 4096) for i in range(2)]
    wto = [RD.v(16384 + i * 2048, BF16, P, 1024) for i in range(2)]
    wcr = n.wc.rearrange("(ot p) k -> p ot k", p=P)
    ev_tr4 = {0: None, 1: None}
    ev_st = {0: None, 1: None}
    ev_cp = {0: None, 1: None}
    for kb in range(32):
        b4 = kb % 2
        ctx.wait("G", ev_tr4[b4])
        ev_ld = ctx.inc(G.dma_start(
            out=wn[b4].rearrange("p (ot k) -> p ot k", ot=NDT),
            in_=wcr[:, :, kb * 512:(kb + 1) * 512]), "dG", 16)
        ctx.wait("T", ev_ld)
        for sub in range(4):
            kc = kb * 4 + sub
            b = kc % 2
            ph = b * 1024
            ctx.wait("T", ev_cp[b])
            for ot in range(NDT):
                ins = T.transpose(pTb[:, ph + ot * 128:ph + (ot + 1) * 128],
                                  wn[b4][:, ot * 512 + sub * 128:
                                         ot * 512 + (sub + 1) * 128],
                                  n.vI128b)
            ev_tr = ctx.inc(ins, "T")
            if sub == 3:
                ev_tr4[b4] = ev_tr
            ctx.wait("V", ev_tr, ev_st[b])
            ev = ctx.inc(V.tensor_copy(wto[b], pTb[:, ph:ph + 1024]), "V")
            ev_cp[b] = ev
            ctx.wait("Sy", ev)
            ev_st[b] = ctx.inc(Sy.dma_start(
                out=n.wt_hbm[kc * 128:(kc + 1) * 128, :], in_=wto[b]),
                "dS", 16)
    ctx.fence()


def emit_ssm_main(nc, ctx, n):
    T, V, Sc, G, Sy = (ctx.eng[k] for k in ("T", "V", "Sc", "G", "Sy"))
    p512, pzz, pTb = n.p512, n.pzz, n.pTb
    RA, RD = n.RA, n.RD

    Bwide = RA.v(27648, BF16, P, 2176)
    for blk5 in range(5):
        width = 512 if blk5 < 4 else 128
        for d8 in range(8):
            src = n.Btbf[0:S,
                         OWN0 - 128 + blk5 * 512 - d8:
                         OWN0 - 128 + blk5 * 512 - d8 + width]
            ins = T.matmul(pzz[:, 0:width], n.vSelW[:, d8 * 128:(d8 + 1) * 128],
                           src, start=(d8 == 0), stop=(d8 == 7))
        ev = ctx.inc(ins, "T")
        ctx.wait("V", ev)
        ev = ctx.inc(V.tensor_copy(Bwide[:, blk5 * 512:blk5 * 512 + width],
                                   pzz[:, 0:width]), "V")
        ctx.wait("T", ev)

    ablock = [RD.v(dt * 2048, F32, P, 512) for dt in range(NDT)]
    abf = [RD.v(24576 + dt * 1024, BF16, P, 512) for dt in range(NDT)]
    n1bf = [RD.v(16384 + dt * 1024, BF16, P, 512) for dt in range(NDT)]
    sqscr = RA.v(26624, BF16, P, 512)
    rb = n.rows.ap()

    if "est2" in n.dbg:
        ctx.inc(G.dma_start(out=n.dbg["est2"][:], in_=n.Estore), "dG", 16)
        ctx.fence()
    for blk in range(4):
        ctx.fence()
        T.matmul(p512[0][:, :], n.vOnes1b,
                 rb[0:1, NT + blk * 512:NT + (blk + 1) * 512],
                 start=True, stop=True)
        ev = ctx.inc(T.matmul(p512[1][:, :], n.vOnes1b,
                              rb[0:1, blk * 512:(blk + 1) * 512],
                              start=True, stop=True), "T")
        ctx.wait("V", ev)
        for dt in range(NDT):
            xsl = n.x0f[dt][:, blk * 512:(blk + 1) * 512]
            V.tensor_mul(ablock[dt], xsl, p512[0][:, :])
            V.tensor_sub(ablock[dt], ablock[dt], p512[1][:, :])
            ins = V.tensor_copy(n1bf[dt], ablock[dt])
        ev_n1 = ctx.inc(ins, "V")
        if "stat" in n.dbg and blk == 0:
            ctx.wait("G", ev_n1)
            ctx.inc(G.dma_start(out=n.dbg["est2"][:, 0:512],
                                in_=ablock[0]), "dG", 16)
            V.tensor_copy(ablock[7], p512[1][:, :])
            ev_t = ctx.incd("V")
            ctx.wait("G", ev_t)
            ctx.inc(G.dma_start(out=n.dbg["est2"][:, 512:1024],
                                in_=ablock[7]), "dG", 16)
            ctx.wait("T", ctx.ev("dG"))
            ctx.wait("V", ctx.ev("dG"))
        ctx.wait("T", ev_n1)
        if "hdbg" in n.dbg and blk == 0:
            ctx.wait("G", ev_n1)
            ctx.inc(G.dma_start(out=n.dbg["hdbg"][:], in_=n1bf[N1DT]),
                    "dG", 16)
            ctx.wait("T", ctx.ev("dG"))
            ctx.wait("V", ctx.ev("dG"))

        bwpc = [RA.v(g * 1024, BF16, P, 512) for g in range(16)]
        ctx.wait("V", ctx.ev("T"))
        for g in range(16 if not SKIP_INTRA else 0):
            V.tensor_copy(bwpc[g], Bwide[:, 128 + blk * 512 - 8 * g:
                                         128 + blk * 512 - 8 * g + 512])
            for cc in range(4):
                if g > 0:
                    V.memset(bwpc[g][:, cc * 128:cc * 128 + 8 * g], 0.0)
                mslice = bwpc[g][:, cc * 128 + 8 * g:cc * 128 + 8 * g + 8]
                V.tensor_mul(mslice, mslice, n.vM0)
        ev_bwc = ctx.incd("V")
        for ps in range(2):
            dts = list(range(ps * 4, ps * 4 + 4))
            banks = [p512[0][:, :], p512[1][:, :], p512[2][:, :],
                     pzz[:, 0:512]]
            started = [False] * 4
            ctx.wait("T", ev_bwc)
            for g in range(16 if not SKIP_INTRA else 0):
                for i, dt in enumerate(dts):
                    ins = T.matmul(banks[i], n.W2[g][:, dt * 128:(dt + 1) * 128],
                                   bwpc[g], start=(not started[i]), stop=False)
                    started[i] = True
                ctx.inc(ins, "T")
            ev_dskip = None
            if not SKIP_DSKIP:
                for i, dt in enumerate(dts):
                    ins = T.matmul(banks[i], n.ddiag[:, dt * P:(dt + 1) * P],
                                   n1bf[dt], start=(not started[i]), stop=False)
                    started[i] = True
                ev_dskip = ctx.inc(ins, "T")
            rings = [RD.v(16384, BF16, P, 512), RD.v(17408, BF16, P, 512)]
            if SKIP_INTER:
                ctx.wait("V", ev_dskip)
                V.memset(rings[0][:, 0:128], 0.0)
                ev = ctx.incd("V")
                ctx.wait("T", ev)
            ev_q = [ev_dskip, ev_dskip]
            it = 0
            for i, dt in enumerate(dts):
                if SKIP_INTER:
                    ins = T.matmul(banks[i][:, 0:128], n.vI128b,
                                   rings[0][:, 0:128],
                                   start=(not started[i]), stop=True)
                    ctx.inc(ins, "T")
                    started[i] = True
                    continue
                for cc in range(4):
                    cown = blk * 4 + cc
                    for sq in range(4):
                        rbu = it % 2
                        ring = rings[rbu]
                        eng = "V" if rbu == 0 else "Sc"
                        ctx.wait(eng, ev_q[rbu])
                        for u in range(4):
                            s = sq * 4 + u
                            seg = dt * S + s
                            ecol = n.Estore[:, seg * S + cown:
                                            seg * S + cown + 1]
                            if rbu == 0:
                                V.tensor_scalar_mul(
                                    ring[:, u * 128:(u + 1) * 128],
                                    n.Rtab[seg], ecol)
                            else:
                                Sc.activation(
                                    ring[:, u * 128:(u + 1) * 128],
                                    n.Rtab[seg], AF.Copy, scale=ecol)
                        ev = ctx.incd(eng)
                        ctx.wait("T", ev)
                        for u in range(4):
                            last = (cc == 3 and sq == 3 and u == 3)
                            ins = T.matmul(
                                banks[i][:, cc * 128:(cc + 1) * 128],
                                n.vI128b, ring[:, u * 128:(u + 1) * 128],
                                start=(not started[i]), stop=last)
                            started[i] = True
                        ev_q[rbu] = ctx.inc(ins, "T")
                        it += 1
            ev_mm = ctx.ev("T")
            ev_a = []
            for i, dt in enumerate(dts):
                ctx.wait("Sc", ev_mm)
                ev_a.append(ctx.inc(Sc.copy(ablock[dt], banks[i]), "Sc"))
            for i, dt in enumerate(dts):
                ctx.wait("V", ev_a[i])
                ev_abf = ctx.inc(V.tensor_copy(abf[dt], banks[i]), "V")
            ctx.wait("T", ev_a[-1], ev_abf)

        if "aout" in n.dbg and blk == 0:
            ctx.wait("G", ctx.ev("Sc"), ctx.ev("V"))
            for dt in range(NDT):
                ctx.inc(G.dma_start(
                    out=n.dbg["aout"][dt * P:(dt + 1) * P, 0:512],
                    in_=ablock[dt]), "dG", 16)
        sqpair = [sqscr, bwpc[0]]
        ev_rd = [None, None]
        ctx.wait("Sc", ev_abf)
        for dt in range(NDT):
            bb = dt % 2
            ctx.wait("Sc", ev_rd[bb])
            ev = ctx.inc(Sc.activation(sqpair[bb], abf[dt], AF.Square), "Sc")
            ctx.wait("T", ev)
            T.matmul(pzz[0:1, 0:512], n.vOnesColB, abf[dt],
                     start=(dt == 0), stop=(dt == NDT - 1))
            ev_rd[bb] = ctx.inc(
                T.matmul(pzz[0:1, 512:1024], n.vOnesColB, sqpair[bb],
                         start=(dt == 0), stop=(dt == NDT - 1)), "T")
        lnscr = RD.v(24576, F32, 1, 1024)
        lncols = RD.v(28672, F32, P, 16)
        ctx.wait("V", ev_rd[0], ev_rd[1])
        ev = emit_ln_rows(nc, ctx, n, pzz[0:1, 0:512], pzz[0:1, 512:1024],
                          lnscr, lncols, p512[0][:, :], p512[1][:, :])
        ctx.wait("V", ev)
        for dt in range(NDT):
            xsl = n.x0f[dt][:, blk * 512:(blk + 1) * 512]
            V.tensor_mul(ablock[dt], ablock[dt], p512[0][:, :])
            V.tensor_add(ablock[dt], ablock[dt], p512[1][:, :])
            ins = V.tensor_add(xsl, xsl, ablock[dt])
        ctx.inc(ins, "V")
        ctx.fence()
    if "x1" in n.dbg:
        for dt in range(NDT):
            ctx.inc(G.dma_start(out=n.dbg["x1"][dt * P:(dt + 1) * P, :],
                                in_=n.x0f[dt]), "dG", 16)
        ctx.fence()


def emit_ffn(nc, ctx, n):
    T, V, Sc, G, Sy = (ctx.eng[k] for k in ("T", "V", "Sc", "G", "Sy"))
    p512, pzz = n.p512, n.pzz
    RA, RB, RC, RD = n.RA, n.RB, n.RC, n.RD

    zl = RA.v(0, BF16, P, 4096)
    ymr = RA.v(8192, BF16, P, 32 * P)
    n2 = [RA.v(16384 + dt * 1024, BF16, P, 512) for dt in range(NDT)]
    yring = [RA.v(24576 + i * 1024, BF16, P, 512) for i in range(6)]
    facc = [RD.v(dt * 2048, F32, P, 512) for dt in range(NDT)]
    hsup = RD.v(16384, BF16, P, 16 * 512)

    ctx.inc(G.dma_start(out=zl, in_=n.zlIn[:]), "dG", 16)
    ctx.inc(G.dma_start(out=ymr.rearrange("p (t c) -> p t c", t=32),
                        in_=n.ymIn.rearrange("t p c -> p t c")), "dG", 16)
    ctx.fence()

    wtbuf = [[RB.v(i * 2048, BF16, P, 1024) for i in range(16)],
             [RC.v(i * 2048, BF16, P, 1024) for i in range(16)]]
    ev_wt = [None, None]
    ev_wt_free = [None, None]

    def issue_wt(j):
        if j >= 32:
            return
        s8n = j % 8
        bn = j % 2
        ctx.wait("Sy", ev_wt_free[bn])
        for kcr in range(16):
            kc = s8n * 16 + kcr
            ins = Sy.dma_start(out=wtbuf[bn][kcr],
                               in_=n.wt_hbm[kc * 128:(kc + 1) * 128, :])
            ev_wt[bn] = ctx.inc(ins, "dS", 16)

    xs1 = [RD.v(bb * 2048, BF16, P, 512) for bb in range(2)]
    sqb = [RD.v(bb * 2048 + 1024, BF16, P, 512) for bb in range(2)]
    ev_mm2 = None
    ev_fa2 = None
    for q4 in range(4):
        ctx.fence()
        if q4 == 0 and PREFETCH_WT:
            issue_wt(0)
        ev_rd = [None, None]
        for dt in range(NDT):
            bb = dt % 2
            xsl = n.x0f[dt][:, q4 * 512:(q4 + 1) * 512]
            ctx.wait("Sc", ev_rd[bb])
            Sc.copy(xs1[bb], xsl)
            ev = ctx.inc(Sc.activation(sqb[bb], xsl, AF.Square), "Sc")
            ctx.wait("T", ev)
            T.matmul(pzz[0:1, 0:512], n.vOnesColB, xs1[bb],
                     start=(dt == 0), stop=(dt == NDT - 1))
            ev_rd[bb] = ctx.inc(T.matmul(pzz[0:1, 512:1024], n.vOnesColB,
                                         sqb[bb],
                                         start=(dt == 0), stop=(dt == NDT - 1)),
                                "T")
        lnscr = RD.v(16384, F32, 1, 1024)
        lncols = RD.v(20480, F32, P, 16)
        ctx.wait("V", ev_rd[0], ev_rd[1])
        ev = emit_ln_rows(nc, ctx, n, pzz[0:1, 0:512], pzz[0:1, 512:1024],
                          lnscr, lncols, p512[0][:, :], p512[1][:, :])
        ctx.wait("V", ev)
        for dt in range(NDT):
            xsl = n.x0f[dt][:, q4 * 512:(q4 + 1) * 512]
            V.tensor_mul(facc[dt], xsl, p512[0][:, :])
            ins = V.tensor_add(n2[dt], facc[dt], p512[1][:, :])
        ev_n2 = ctx.inc(ins, "V")

        for s8 in range(8):
            b = s8 % 2
            if PREFETCH_WT:
                issue_wt(q4 * 8 + s8 + 1)
            else:
                issue_wt(q4 * 8 + s8)
            hh = s8 // 2
            if s8 == 0:
                ctx.wait("T", ev_n2)
            ev_y = []
            for gi in range(4):
                gg = (s8 % 2) * 4 + gi
                ph = (gi % 2) * 512
                if gi >= 2:
                    ctx.wait("T", ev_y[gi - 2])
                ev = ctx.inc(T.matmul(pzz[:, ph:ph + 512],
                                      ymr[:, (hh * 8 + gg) * P:
                                          (hh * 8 + gg + 1) * P],
                                      n2[gg], start=True, stop=True), "T")
                ctx.wait("V", ev)
                evy = ctx.inc(V.tensor_copy(yring[gi], pzz[:, ph:ph + 512]),
                              "V")
                ev_y.append(evy)
            ctx.wait("T", ev_wt[b], *ev_y)
            zslots = [pzz[:, 0:512], pzz[:, 512:1024], n.pW[:, :]]
            ev_gel = [None, None, None]

            def cproj3(kk):
                ctx.wait("T", ev_gel[kk % 3])
                for ot in range(3):
                    ins = T.matmul(p512[ot][:, :],
                                   wtbuf[b][kk][:, ot * 128:(ot + 1) * 128],
                                   hsup[:, kk * 512:(kk + 1) * 512],
                                   start=(kk == 0), stop=(kk == 15))
                ctx.inc(ins, "T")

            for kcr in range(16):
                sl = kcr % 3
                kc = s8 * 16 + kcr
                a4 = kc % 4
                tp = (96, 0) if a4 == 3 else None
                ctx.wait("T", ev_gel[sl])
                ins = T.matmul(zslots[sl],
                               zl[32 * a4:32 * a4 + 32,
                                  (kc // 4) * 128:(kc // 4 + 1) * 128],
                               yring[kcr // 4][32 * a4:32 * a4 + 32, :],
                               start=True, stop=True, tile_position=tp)
                ev_z = ctx.inc(ins, "T")
                ctx.wait("Sc", ev_z)
                ev_gel[sl] = ctx.inc(Sc.activation(
                    hsup[:, kcr * 512:(kcr + 1) * 512],
                    zslots[sl], AF.Gelu), "Sc")
                if kcr > 0:
                    cproj3(kcr - 1)
            cproj3(15)
            ev_mm = ctx.ev("T")
            for kcr in range(16):
                for oi, ot in enumerate((3, 4)):
                    ins = T.matmul(
                        pzz[:, oi * 512:(oi + 1) * 512],
                        wtbuf[b][kcr][:, ot * 128:(ot + 1) * 128],
                        hsup[:, kcr * 512:(kcr + 1) * 512],
                        start=(kcr == 0), stop=(kcr == 15))
            ev_mm1 = ctx.inc(ins, "T")
            for ot in range(3):
                ctx.wait("V", ev_mm)
                if s8 == 0:
                    ins = V.tensor_copy(facc[ot], p512[ot][:, :])
                else:
                    ins = V.tensor_add(facc[ot], facc[ot], p512[ot][:, :])
            ev_fa = ctx.inc(ins, "V")
            ctx.wait("T", ev_fa)
            for kcr in range(16):
                for oi, ot in enumerate((5, 6, 7)):
                    ins = T.matmul(
                        p512[oi][:, :],
                        wtbuf[b][kcr][:, ot * 128:(ot + 1) * 128],
                        hsup[:, kcr * 512:(kcr + 1) * 512],
                        start=(kcr == 0), stop=(kcr == 15))
            ev_mm2 = ctx.inc(ins, "T")
            for oi, ot in enumerate((3, 4)):
                ctx.wait("V", ev_mm1)
                if s8 == 0:
                    ins = V.tensor_copy(facc[ot], pzz[:, oi * 512:(oi + 1) * 512])
                else:
                    ins = V.tensor_add(facc[ot], facc[ot],
                                       pzz[:, oi * 512:(oi + 1) * 512])
            ev_fa1 = ctx.inc(ins, "V")
            for oi, ot in enumerate((5, 6, 7)):
                ctx.wait("V", ev_mm2)
                if s8 == 0:
                    ins = V.tensor_copy(facc[ot], p512[oi][:, :])
                else:
                    ins = V.tensor_add(facc[ot], facc[ot], p512[oi][:, :])
            ev_fa2 = ctx.inc(ins, "V")
            ctx.wait("T", ev_fa1, ev_fa2)
            ev_wt_free[b] = ctx.ev("T")

        if "aout" in n.dbg and q4 == 0:
            ctx.fence()
            ctx.wait("G", ctx.ev("V"))
            for dt in range(NDT):
                ctx.inc(G.dma_start(
                    out=n.dbg["aout"][dt * P:(dt + 1) * P, 0:512],
                    in_=facc[dt]), "dG", 16)
            ctx.wait("T", ctx.ev("dG"))
            ctx.wait("V", ctx.ev("dG"))
            ctx.wait("Sc", ctx.ev("dG"))
        fbf = [RD.v(16384 + dt * 1024, BF16, P, 512) for dt in range(NDT)]
        sqf = [RD.v(24576, BF16, P, 512), RD.v(30720, BF16, P, 512)]
        ev_rd = [None, None]
        ctx.wait("Sc", ev_mm2, ev_fa2)
        for dt in range(NDT):
            bb = dt % 2
            Sc.copy(fbf[dt], facc[dt])
            ctx.wait("Sc", ev_rd[bb])
            ev = ctx.inc(Sc.activation(sqf[bb], facc[dt], AF.Square), "Sc")
            ctx.wait("T", ev)
            T.matmul(pzz[0:1, 0:512], n.vOnesColB, fbf[dt],
                     start=(dt == 0), stop=(dt == NDT - 1))
            ev_rd[bb] = ctx.inc(
                T.matmul(pzz[0:1, 512:1024], n.vOnesColB, sqf[bb],
                         start=(dt == 0), stop=(dt == NDT - 1)), "T")
        lnscr2 = RD.v(25600, F32, 1, 1024)
        lncols2 = RD.v(29696, F32, P, 16)
        ctx.wait("V", ev_rd[0], ev_rd[1])
        ev = emit_ln_rows(nc, ctx, n, pzz[0:1, 0:512], pzz[0:1, 512:1024],
                          lnscr2, lncols2, p512[0][:, :], p512[1][:, :])
        ctx.wait("V", ev, ev_fa2)
        for dt in range(NDT):
            xsl = n.x0f[dt][:, q4 * 512:(q4 + 1) * 512]
            V.tensor_mul(facc[dt], facc[dt], p512[0][:, :])
            V.tensor_add(facc[dt], facc[dt], p512[1][:, :])
            ins = V.tensor_add(xsl, xsl, facc[dt])
            ev = ctx.inc(ins, "V")
            ctx.wait("G", ev)
            ctx.inc(G.dma_start(out=n.out[dt * P:(dt + 1) * P,
                                         q4 * 512:(q4 + 1) * 512],
                                in_=xsl), "dG", 16)


def _mk_consts():
    c = {}
    c["cI128f"] = np.eye(P, dtype=np.float32)
    c["cI128b"] = np.eye(P, dtype=np.float32).astype(ml_dtypes.bfloat16)
    iota = np.arange(P, dtype=np.float32)
    c["cIotaNeg"] = np.tile(-iota[None, :], (P, 1))
    m0 = np.zeros((P, 8), dtype=np.float32)
    for p in range(P):
        m0[p, p // 16:] = 1.0
    c["cM0"] = m0.astype(ml_dtypes.bfloat16)
    sel = np.zeros((S, P), dtype=np.float32)
    for s in range(S):
        for d8 in range(8):
            sel[s, d8 * 16 + s] = 1.0
    c["cSelRep"] = sel
    selw = np.zeros((S, 8 * P), dtype=np.float32)
    for d8 in range(8):
        for s in range(S):
            selw[s, d8 * P + d8 * 16 + s] = 1.0
    c["cSelW"] = selw.astype(ml_dtypes.bfloat16)
    c["cOnes1"] = np.ones((1, P), dtype=np.float32)
    c["cOnes1b"] = np.ones((1, P), dtype=np.float32).astype(ml_dtypes.bfloat16)
    c["cNegD8"] = (-(np.arange(P) // 16).astype(np.float32))[:, None]
    c["cRevJ"] = (-(127.0 - np.arange(P, dtype=np.float32)))[:, None]
    c["cOnesColF"] = np.ones((P, 1), dtype=np.float32)
    c["cOnesColB"] = np.ones((P, 1), dtype=np.float32).astype(
        ml_dtypes.bfloat16)
    c["cEps"] = np.full((P, 1), EPS, dtype=np.float32)
    return c


def _host_inputs(x, logA, Bw, Cw, Dv, rm, ru, wcp):
    x = np.asarray(x)
    consts = _mk_consts()
    bwe = np.zeros((NDT, P, 18), dtype=np.float32)
    for dt in range(NDT):
        bwe[dt, :, 0:S] = np.asarray(Bw)[:, dt * P:(dt + 1) * P].T
        bwe[dt, :, 16] = 1.0
    ym = np.zeros((32, P, P), dtype=np.float32)
    rm = np.asarray(rm)
    for h in range(4):
        for g in range(8):
            for rr in range(8):
                ym[h * 8 + g, rr * 16:(rr + 1) * 16,
                   rr * 16:(rr + 1) * 16] = rm[h, g * 8 + rr]
    zl = np.zeros((P, 32 * P), dtype=np.float32)
    ru = np.asarray(ru)
    for t in range(128):
        h, r2 = t // 32, t % 32
        blk, a = t // 4, t % 4
        for rr2 in range(2):
            r_ = r2 * 2 + rr2
            zl[a * 32 + rr2 * 16: a * 32 + (rr2 + 1) * 16,
               blk * P + rr2 * 64: blk * P + (rr2 + 1) * 64] = ru[h, r_].T
    dv = np.asarray(Dv).reshape(NDT, P)

    maps = []
    for core in range(8):
        b, q = core // 4, core % 4
        t0 = q * NT
        xo = np.ascontiguousarray(x[b, t0:t0 + NT])
        pr = np.zeros((NPR, D), dtype=np.float32)
        lo = max(0, t0 - NPR)
        if t0 > 0:
            pr[NPR - (t0 - lo):] = x[b, lo:t0]
        maps.append(dict(
            xo=xo, xp=pr, logA=np.asarray(logA), CwIn=np.asarray(Cw),
            CwIn2=np.asarray(Cw), BwEIn=bwe, DvIn=dv, ymIn=ym, zlIn=zl,
            wc=np.asarray(wcp), **consts))
    return maps


def kernel(L0_attn_x, L0_attn_log_A_diag, L0_attn_B_w, L0_attn_C_w,
           L0_attn_D_vec, L0_ffn_rail_maker, L0_ffn_rail_up_proj,
           L0_ffn_w_cproj):
    maps = _host_inputs(L0_attn_x, L0_attn_log_A_diag, L0_attn_B_w,
                        L0_attn_C_w, L0_attn_D_vec, L0_ffn_rail_maker,
                        L0_ffn_rail_up_proj, L0_ffn_w_cproj)
    nc, _ = build_program()
    res = run_bass_kernel_spmd(nc, maps, core_ids=list(range(8)))
    full = np.zeros((2, 8192, D), dtype=np.float32)
    for core in range(8):
        b, q = core // 4, core % 4
        full[b, q * NT:(q + 1) * NT] = np.asarray(res.results[core]["out"]).T
    return full
